# version 14
# baseline (speedup 1.0000x reference)
import numpy as np
import ml_dtypes
import concourse.bass as bass
import concourse.mybir as mybir
from concourse.bass_utils import run_bass_kernel_spmd
from contextlib import ExitStack

F32 = mybir.dt.float32
BF16 = mybir.dt.bfloat16
U16 = mybir.dt.uint16
AF = mybir.ActivationFunctionType
ALU = mybir.AluOpType
AX = mybir.AxisListType

NCORE = 8
T = 16384
D = 1024
H = 16
HD = 64
CONV = 3072
IF = 2816
TLOC = T // NCORE
CK = 128
NCH = TLOC // CK
DC = D // 128
CCH = CONV // 128
LNW = TLOC + 3
MAGIC = float(1.5 * 2.0 ** 23)
LN8 = float(np.log(8.0))
BF = ml_dtypes.bfloat16


class SyncEng:

    _PASS = {"wait_ge", "sem_inc", "then_inc"}

    def __init__(self, eng, sem, selfsync=True):
        self.eng = eng
        self.sem = sem
        self.selfsync = selfsync
        self.count = 0
        self.par = False
        self.pending = None

    def _flush_pending(self):
        if self.pending is not None:
            inst, amt = self.pending
            inst.then_inc(self.sem, amt)
            self.count += amt
            self.pending = None
            return True
        return False

    def _pre_op(self):
        if not self.selfsync:
            return
        if self._flush_pending():
            if not self.par:
                self.eng.wait_ge(self.sem, self.count)

    def parallel(self):
        import contextlib

        @contextlib.contextmanager
        def cm():
            self._pre_op()
            self.par = True
            try:
                yield
            finally:
                self.par = False
        return cm()

    def close_stage(self, stg_sem, inc):
        assert self.pending is not None, "stage had no ops?"
        inst, _ = self.pending
        inst.then_inc(stg_sem, inc)
        self.pending = None

    def dma_batch(self, pairs):
        self._flush_pending()
        for dst, srcp in pairs:
            self.eng.dma_start(out=dst, in_=srcp).then_inc(self.sem, 16)
            self.count += 16
        self.eng.wait_ge(self.sem, self.count)

    def collective(self, *a, **kw):
        self._flush_pending()
        self.eng.collective_compute(*a, **kw).then_inc(self.sem, 1)
        self.count += 1
        self.eng.wait_ge(self.sem, self.count)

    def collective_async(self, done_sem, *a, **kw):
        self._flush_pending()
        self.eng.collective_compute(*a, **kw).then_inc(done_sem, 1)

    def wait_ge(self, *a, **kw):
        return self.eng.wait_ge(*a, **kw)

    def sem_inc(self, *a, **kw):
        return self.eng.sem_inc(*a, **kw)

    def __getattr__(self, name):
        f = getattr(self.eng, name)
        if not callable(f) or name.startswith("_"):
            return f

        def wrapped(*a, **kw):
            self._pre_op()
            inst = f(*a, **kw)
            amt = 16 if name == "dma_start" else 1
            self.pending = (inst, amt)
            return inst
        return wrapped


class KB:

    def __init__(self):
        self.stages = []
        self.afters = []

    def stage(self, after_=None, **fns):
        self.stages.append(fns)
        self.afters.append(after_)
        return len(self.stages) - 1

    def body(self, stg_sems, esem, eng_name):
        engs = list(stg_sems.keys())
        ncum = {e: [] for e in engs}
        cnt = {e: 0 for e in engs}
        for s in self.stages:
            for e in engs:
                if e in s:
                    cnt[e] += 1
                ncum[e].append(cnt[e])

        def run(eng):
            w = SyncEng(eng, esem,
                        selfsync=(eng_name not in ("tensor", "sync")))
            waited = {e: 0 for e in engs}
            for si, s in enumerate(self.stages):
                if eng_name not in s:
                    continue
                a = self.afters[si]
                if a is None:
                    a = si - 1
                if a >= 0:
                    for e in engs:
                        need = ncum[e][a]
                        if need > waited[e]:
                            eng.wait_ge(stg_sems[e], need)
                            waited[e] = need
                res = s[eng_name](w)
                if res == "self":
                    eng.sem_inc(stg_sems[eng_name], 1)
                else:
                    w.close_stage(stg_sems[eng_name], 1)
        return run


def build_program(max_stage=None, dbg=False, nocc=False):
    nc = bass.Bass("TRN2", target_bir_lowering=False,
                   dynamic_dma_scratch_size=4096)

    def din(name, shape, dtype=F32):
        return nc.dram_tensor(name, shape, dtype, kind="ExternalInput").ap()

    x_d = din("x", [TLOC, D])
    xprev_d = din("xprev", [3, D])
    selv_d = din("selv", [128, 8])
    sellt_d = din("sellt", [8, 1])
    invsel_d = din("invsel", [8, 16])
    wqkvt_d = din("wqkvt", [CCH * DC * 128, 128])
    wzt_d = din("wzt", [D, D])
    wbat_d = din("wbat", [D + 1, 32])
    alog_d = din("alog", [1, H])
    convw_d = din("convw", [CONV, 4])
    woutt_d = din("woutt", [D, D])
    wg_d = din("wg", [IF, D])
    wu_d = din("wu", [IF, D])
    wd_d = din("wd", [D, IF])
    tri_d = din("tri", [128, 128])
    ones_d = din("ones", [128, 128])
    identf_d = din("identf", [128, 128])
    identb_d = din("identb", [128, 128], BF16)
    i16_d = din("i16", [16, 16])
    maskb_d = din("maskb", [128, 128], U16)
    bones_d = din("bones", [128, 8 * 16], BF16)
    bcastp_d = din("bcastp", [16, 8 * 128], BF16)
    rep16_d = din("rep16", [16, D])
    su8_d = din("su8", [8, 8])

    y_d = nc.dram_tensor("y", [TLOC, D], F32, kind="ExternalOutput").ap()

    spk = "ExternalOutput" if dbg else "Internal"
    sp_oi = nc.dram_tensor("sp_oi", [TLOC, D], BF16, kind=spk).ap()
    sp_qn = nc.dram_tensor("sp_qn", [TLOC, D], BF16, kind=spk).ap()
    sp_z = nc.dram_tensor("sp_z", [TLOC, D], BF16, kind=spk).ap()
    sp_x2 = nc.dram_tensor("sp_x2", [TLOC, D], F32, kind=spk).ap()
    sp_xqf = nc.dram_tensor("sp_xqf", [TLOC, D], BF16, kind=spk).ap()
    if dbg:
        dbg_s = nc.dram_tensor("dbg_s", [8 * 65, D], F32,
                               kind="ExternalOutput").ap()
        dbg_a = nc.dram_tensor("dbg_a", [128, 8192], BF16,
                               kind="ExternalOutput").ap()
        dbg_b = nc.dram_tensor("dbg_b", [128, 64], F32,
                               kind="ExternalOutput").ap()
        dbg_c = nc.dram_tensor("dbg_c", [16, 128], F32,
                               kind="ExternalOutput").ap()
        dbg_d = nc.dram_tensor("dbg_d", [16, 256], BF16,
                               kind="ExternalOutput").ap()
        dbg_e = nc.dram_tensor("dbg_e", [128, 160], F32,
                               kind="ExternalOutput").ap()
    cc_send = nc.dram_tensor("cc_send", [8 * 65, D], F32, kind="Internal").ap()
    cc_recv = nc.dram_tensor("cc_recv", [8 * 65, D], F32, kind="Internal").ap()

    kb = KB()
    marks = {}
    st = ExitStack()

    def sb(name, shape, dtype=F32):
        return st.enter_context(nc.sbuf_tensor(name, shape, dtype))

    R1 = sb("R1", [128, CCH * LNW], BF16)
    R2 = sb("R2", [128, DC * LNW], BF16)
    R3 = sb("R3", [128, DC * D], BF16)
    R4 = sb("R4", [128, 8192])
    R5 = sb("R5", [128, 10240], BF16)

    def r4(c0, cn, p0=0, pn=128):
        return R4.ap()[p0:p0 + pn, c0:c0 + cn]

    def r5(c0, cn, p0=0, pn=128):
        return R5.ap()[p0:p0 + pn, c0:c0 + cn]

    MIXED = R1.ap()
    LN1 = R2.ap()
    WZT = R3.ap()
    WOUTT = R3.ap()
    wgT_q = R1.ap()[:, 0:8 * IF]
    wuT_q = R1.ap()[:, 8 * IF:16 * IF]

    def wdT_chunk(k):
        if k < 4:
            return R1.ap()[:, 16 * IF + k * D:16 * IF + (k + 1) * D]
        if k < 20:
            return R2.ap()[:, (k - 4) * D:(k - 3) * D]
        return R3.ap()[:, (k - 20) * D:(k - 19) * D]

    TRI = sb("TRI", [128, 128])
    ONES = sb("ONES", [128, 128])
    IDF = sb("IDF", [128, 128])
    IDB = sb("IDB", [128, 128], BF16)
    I16 = sb("I16", [16, 16])
    MASKB = sb("MASKB", [128, 128], U16)
    BONES = sb("BONES", [128, 8 * 16], BF16)
    BCASTP = sb("BCASTP", [16, 8 * 128], BF16)
    CONVW = sb("CONVW", [128, CCH * 4])
    WBAT = sb("WBAT", [128, 9 * 32], BF16)
    ALOG = sb("ALOG", [1, H])
    SELV = sb("SELV", [128, 8])
    SELLT = sb("SELLT", [8, 1])
    INVSEL = sb("INVSEL", [8, 16])
    ELGD = sb("ELGD", [128, NCH * H])
    EXPL = [sb("EXPL0", [128, H]), sb("EXPL1", [128, H])]
    LGDR = [sb("LGDR0", [1, H]), sb("LGDR1", [1, H])]
    NEGEA = sb("NEGEA", [128, H])
    SLOC = [sb("SLOC0", [128, H * HD], BF16), sb("SLOC1", [128, H * HD], BF16)]
    SINIT = sb("SINIT", [128, H * HD], BF16)
    BETA = sb("BETA", [128, H])
    TMP1 = sb("TMP1", [128, H])
    TMP2 = sb("TMP2", [128, H])
    TMP3 = sb("TMP3", [128, H])
    TMP4 = sb("TMP4", [128, H])
    GSB = sb("GSB", [128, H])
    NEGG = sb("NEGG", [128, H])
    GT = sb("GT", [16, 128])
    GTS = sb("GTS", [16, 128])
    E2A = sb("E2A", [128, H])
    E2 = sb("E2", [128, H])
    EXPG = sb("EXPG", [128, H])
    EB128 = sb("EB128", [128, H])
    B128S = sb("B128S", [128, H])
    LQ = sb("LQ", [16, 128])
    LK = sb("LK", [16, 128])
    M2Q = sb("M2Q", [16, 128], BF16)
    M2K = sb("M2K", [16, 128], BF16)
    MSQ = sb("MSQ", [128, H])
    RSTD = sb("RSTD", [128, H])
    STA = sb("STA", [128, 8])
    CCOL = sb("CCOL", [128, 4])
    ONESB = sb("ONESB", [1, 128], BF16)

    SQ = r5(0, 2048)
    SQF = R5.ap().bitcast(F32)[:, 0:1024]
    KH = r5(2048, 1024)
    UTM = r5(3072, 1024)
    QNF = r5(4096, 1024)
    KNF = r5(5120, 1024)
    EM = r5(6144, 2048)
    OIB = r5(8192, 1024)
    ZB = r5(9216, 1024)
    LNT = r5(0, 1024)
    WQSB = r5(1024, 1024)
    CS0 = r5(2048, 2048)
    CS1 = r5(4096, 2048)
    XT = r4(0, 1024)
    WQS = r4(1024, 1024)

    PB = [st.enter_context(nc.psum_tensor(f"PB{i}", [128, 512])) for i in range(8)]

    def pbank(i, p0, pn, c0, cn, dtype=F32):
        v = PB[i].ap()
        if dtype != F32:
            m = 2 if dtype == BF16 else 1
            return v.bitcast(dtype)[p0:p0 + pn, c0 * m:(c0 + cn) * m]
        return v[p0:p0 + pn, c0:c0 + cn]

    stg = st.enter_context(nc.semaphore("stg"))
    ccsem = st.enter_context(nc.semaphore("ccsem"))
    esems = {n: st.enter_context(nc.semaphore("es_" + n))
             for n in ("sync", "scalar", "vector", "tensor", "gpsimd")}

    def dseq(eng, pairs):
        eng.dma_batch(pairs)
        return "self"

    def p_zero(eng):
        return eng.memset(WQS[:, 256:288], 0.0)
    kb.stage(vector=p_zero)

    def p_loads(eng):
        pairs = [
            (TRI.ap(), tri_d), (ONES.ap(), ones_d), (IDF.ap(), identf_d),
            (IDB.ap(), identb_d), (I16.ap(), i16_d), (MASKB.ap(), maskb_d),
            (BONES.ap(), bones_d), (BCASTP.ap(), bcastp_d),
            (ALOG.ap(), alog_d), (SELV.ap(), selv_d), (SELLT.ap(), sellt_d),
            (INVSEL.ap(), invsel_d),
        ]
        for cc in range(CCH):
            pairs.append((CONVW.ap()[:, cc * 4:(cc + 1) * 4],
                          convw_d[cc * 128:(cc + 1) * 128, :]))
        for k in range(8):
            pairs.append((WQS[:, k * 32:(k + 1) * 32],
                          wbat_d[k * 128:(k + 1) * 128, :]))
        pairs.append((WQS[0:1, 256:288], wbat_d[1024:1025, :]))
        return dseq(eng, pairs)
    kb.stage(sync=p_loads)

    def p_init(eng):
        eng.memset(ONESB.ap(), 1.0)
        eng.memset(CCOL.ap()[:, 0:1], 1e-5)
        eng.memset(CCOL.ap()[:, 1:2], 1e-6)
        eng.memset(CCOL.ap()[:, 2:3], -LN8)
        eng.memset(CCOL.ap()[:, 3:4], 1.0)
        eng.memset(EXPL[0].ap(), 1.0)
        eng.memset(LGDR[0].ap(), 0.0)
        eng.memset(SLOC[0].ap(), 0.0)
        return eng.memset(SLOC[1].ap(), 0.0)
    kb.stage(vector=p_init)

    def p_ea1(eng):
        return eng.activation(out=TMP1.ap()[0:1, :], in_=ALOG.ap(), func=AF.Exp)
    kb.stage(scalar=p_ea1)

    def p_ea2(eng):
        return eng.tensor_scalar(out=TMP2.ap()[0:1, :], in0=TMP1.ap()[0:1, :],
                                 scalar1=-1.0, scalar2=None, op0=ALU.mult)
    kb.stage(vector=p_ea2)

    def p_ea3(eng):
        return eng.matmul(pbank(0, 0, 128, 0, 16), ONES.ap()[0:1, :],
                          TMP2.ap()[0:1, :], start=True, stop=True)
    kb.stage(tensor=p_ea3)

    def p_ea4(eng):
        return eng.tensor_copy(out=NEGEA.ap(), in_=pbank(0, 0, 128, 0, 16))
    kb.stage(vector=p_ea4)

    def p_wba(eng):
        return eng.copy(out=WBAT.ap(), in_=WQS[:, 0:288])
    kb.stage(scalar=p_wba)

    for dcz in range(DC):
        def p_wzl(eng, dcz=dcz):
            return dseq(eng, [(WQS, wzt_d[dcz * 128:(dcz + 1) * 128, :])])
        def p_wzc(eng, dcz=dcz):
            return eng.copy(out=WZT[:, dcz * D:(dcz + 1) * D], in_=WQS)
        kb.stage(sync=p_wzl)
        kb.stage(scalar=p_wzc)

    marks['prologue'] = len(kb.stages)
    def ln_tile(rows, col0):
        def s1(eng):
            with eng.parallel():
                eng.activation(out=r4(2048, 1024, 0, rows), in_=XT[0:rows, :],
                               func=AF.Copy, accum_out=STA.ap()[0:rows, 0:1])
                last = eng.activation(out=r4(3072, 1024, 0, rows),
                                      in_=XT[0:rows, :], func=AF.Square,
                                      accum_out=STA.ap()[0:rows, 1:2])
            return last
        kb.stage(scalar=s1)

        def s2(eng):
            eng.tensor_scalar(out=STA.ap()[0:rows, 2:3],
                              in0=STA.ap()[0:rows, 0:1], scalar1=1.0 / D,
                              scalar2=None, op0=ALU.mult)
            eng.tensor_scalar(out=STA.ap()[0:rows, 3:4],
                              in0=STA.ap()[0:rows, 1:2], scalar1=1.0 / D,
                              scalar2=None, op0=ALU.mult)
            eng.tensor_tensor(out=STA.ap()[0:rows, 4:5],
                              in0=STA.ap()[0:rows, 2:3],
                              in1=STA.ap()[0:rows, 2:3], op=ALU.mult)
            return eng.tensor_tensor(out=STA.ap()[0:rows, 5:6],
                                     in0=STA.ap()[0:rows, 3:4],
                                     in1=STA.ap()[0:rows, 4:5],
                                     op=ALU.subtract)
        kb.stage(vector=s2)

        def s3(eng):
            eng.activation(out=STA.ap()[0:rows, 6:7],
                           in_=STA.ap()[0:rows, 5:6], func=AF.Ln,
                           bias=CCOL.ap()[0:rows, 0:1])
            return eng.activation(out=STA.ap()[0:rows, 6:7],
                                  in_=STA.ap()[0:rows, 6:7], func=AF.Exp,
                                  scale=-0.5)
        kb.stage(scalar=s3)

        def s4(eng):
            return eng.scalar_tensor_tensor(
                out=STA.ap()[0:rows, 7:8], in0=STA.ap()[0:rows, 2:3],
                scalar=-1.0, in1=STA.ap()[0:rows, 6:7],
                op0=ALU.mult, op1=ALU.mult)
        kb.stage(vector=s4)

        def s5(eng):
            return eng.tensor_scalar(out=LNT[0:rows, :], in0=XT[0:rows, :],
                                     scalar1=STA.ap()[0:rows, 6:7],
                                     scalar2=STA.ap()[0:rows, 7:8],
                                     op0=ALU.mult, op1=ALU.add)
        kb.stage(vector=s5)

        def s6(eng):
            last = None
            for dc in range(DC):
                last = eng.transpose(pbank(dc, 0, 128, 0, 64, BF16)[:, 0:rows],
                                     LNT[0:rows, dc * 128:(dc + 1) * 128],
                                     IDB.ap()[0:rows, 0:rows])
            return last
        kb.stage(tensor=s6)

        def s7(eng):
            last = None
            with eng.parallel():
                for dc in range(DC):
                    last = eng.tensor_copy(
                        out=LN1[:, dc * LNW + col0:dc * LNW + col0 + rows],
                        in_=pbank(dc, 0, 128, 0, 64, BF16)[:, 0:rows])
            return last
        kb.stage(vector=s7)

    def l_xp(eng):
        return dseq(eng, [(XT[0:3, :], xprev_d)])
    kb.stage(sync=l_xp)
    ln_tile(3, 0)
    for tt in range(NCH):
        def l_x(eng, tt=tt):
            return dseq(eng, [(XT, x_d[tt * 128:(tt + 1) * 128, :])])
        kb.stage(sync=l_x)
        ln_tile(128, 3 + tt * 128)

    marks['ln'] = len(kb.stages)
    npieces = [(0, 512), (512, 512), (1024, 512), (1536, 512), (2048, 3)]
    for cc in range(CCH):
        def q_l(eng, cc=cc):
            return dseq(eng, [(WQS.rearrange("p (dc c) -> p dc c", c=128),
                               wqkvt_d[cc * DC * 128:(cc + 1) * DC * 128, :]
                               .rearrange("(dc p) c -> p dc c", p=128))])
        kb.stage(sync=q_l)

        def q_c(eng):
            return eng.copy(out=WQSB, in_=WQS)
        kb.stage(scalar=q_c)

        def q_mm(eng, cc=cc):
            last = None
            for dc in range(DC):
                for pi, (c0, cn) in enumerate(npieces):
                    bank = pi if pi < 4 else 4
                    last = eng.matmul(
                        pbank(bank, 0, 128, 0, cn),
                        WQSB[:, dc * 128:(dc + 1) * 128],
                        LN1[:, dc * LNW + c0:dc * LNW + c0 + cn],
                        start=(dc == 0), stop=(dc == DC - 1))
            return last
        kb.stage(tensor=q_mm)

        def q_ev(eng, cc=cc):
            last = None
            with eng.parallel():
                for pi, (c0, cn) in enumerate(npieces):
                    bank = pi if pi < 4 else 4
                    last = eng.tensor_copy(
                        out=MIXED[:, cc * LNW + c0:cc * LNW + c0 + cn],
                        in_=pbank(bank, 0, 128, 0, cn))
            return last
        kb.stage(vector=q_ev)

        def q_cv2(eng, cc=cc):
            base = cc * LNW
            eng.tensor_scalar(out=CS0, in0=MIXED[:, base:base + 2048],
                              scalar1=CONVW.ap()[:, cc * 4:cc * 4 + 1],
                              scalar2=None, op0=ALU.mult)
            eng.scalar_tensor_tensor(
                out=CS1, in0=MIXED[:, base + 1:base + 2049],
                scalar=CONVW.ap()[:, cc * 4 + 1:cc * 4 + 2],
                in1=CS0, op0=ALU.mult, op1=ALU.add)
            eng.scalar_tensor_tensor(
                out=CS0, in0=MIXED[:, base + 2:base + 2050],
                scalar=CONVW.ap()[:, cc * 4 + 2:cc * 4 + 3],
                in1=CS1, op0=ALU.mult, op1=ALU.add)
            return eng.scalar_tensor_tensor(
                out=CS1, in0=MIXED[:, base + 3:base + 2051],
                scalar=CONVW.ap()[:, cc * 4 + 3:cc * 4 + 4],
                in1=CS0, op0=ALU.mult, op1=ALU.add)
        kb.stage(vector=q_cv2)

        def q_cv3(eng, cc=cc):
            return eng.activation(out=MIXED[:, cc * LNW:cc * LNW + 2048],
                                  in_=CS1, func=AF.Silu)
        kb.stage(scalar=q_cv3)

    marks['qkv'] = len(kb.stages)
    BETA2 = [BETA, sb("BETAb", [128, H])]
    GSB2 = [GSB, sb("GSBb", [128, H])]
    NEGG2 = [NEGG, sb("NEGGb", [128, H])]
    GT2 = [GT, sb("GTb", [16, 128])]
    E22 = [E2, sb("E2b", [128, H])]
    EXPG2 = [EXPG, sb("EXPGb", [128, H])]
    EB1282 = [EB128, sb("EB128b", [128, H])]
    M2Q2 = [M2Q, sb("M2Qb", [16, 128], BF16)]
    M2K2 = [M2K, sb("M2Kb", [16, 128], BF16)]
    T1F = r4(0, 1024)

    prev_head_last = None
    prev_tail17 = [None, None]
    for c in range(NCH):
        pc = c % 2
        ct = c * 128
        lt = 3 + c * 128
        BETAc, GSBc, NEGGc = BETA2[pc].ap(), GSB2[pc].ap(), NEGG2[pc].ap()
        GTc, E2c, EXPGc = GT2[pc].ap(), E22[pc].ap(), EXPG2[pc].ap()
        EB128c = EB1282[pc].ap()
        M2Qc, M2Kc = M2Q2[pc].ap(), M2K2[pc].ap()

        def a_s1(eng, lt=lt):
            last = None
            for dc in range(DC):
                last = eng.matmul(
                    pbank(6, 0, 128, 0, 32),
                    LN1[:, dc * LNW + lt:dc * LNW + lt + 128],
                    WBAT.ap()[:, dc * 32:(dc + 1) * 32],
                    start=(dc == 0), stop=False)
            return eng.matmul(pbank(6, 0, 128, 0, 32), ONESB.ap(),
                              WBAT.ap()[0:1, 256:288], start=False, stop=True)
        h_first = kb.stage(tensor=a_s1, after_=prev_head_last)

        def a_s2(eng, BETAc=BETAc):
            eng.activation(out=BETAc, in_=pbank(6, 0, 128, 0, 16),
                           func=AF.Sigmoid)
            eng.activation(out=TMP1.ap(), in_=pbank(6, 0, 128, 16, 16),
                           func=AF.Abs)
            eng.activation(out=TMP2.ap(), in_=TMP1.ap(), func=AF.Exp,
                           scale=-1.0)
            eng.activation(out=TMP1.ap(), in_=TMP2.ap(), func=AF.Ln,
                           bias=CCOL.ap()[:, 3:4])
            return eng.activation(out=TMP3.ap(), in_=pbank(6, 0, 128, 16, 16),
                                  func=AF.Relu)
        kb.stage(scalar=a_s2)

        def a_s3(eng):
            eng.tensor_tensor(out=TMP4.ap(), in0=TMP3.ap(), in1=TMP1.ap(),
                              op=ALU.add)
            return eng.tensor_tensor(out=TMP2.ap(), in0=TMP4.ap(),
                                     in1=NEGEA.ap(), op=ALU.mult)
        kb.stage(vector=a_s3)

        def a_s4(eng):
            eng.transpose(pbank(7, 0, 16, 0, 128), TMP2.ap(), IDF.ap())
            eng.matmul(pbank(6, 0, 128, 32, 16), TRI.ap(), TMP2.ap(),
                       start=True, stop=True)
            return eng.matmul(pbank(6, 0, 128, 48, 16), ONES.ap(), TMP2.ap(),
                              start=True, stop=True)
        kb.stage(tensor=a_s4)

        def a_s5v(eng, pc=pc, GSBc=GSBc, NEGGc=NEGGc):
            eng.tensor_copy(out=GSBc, in_=pbank(6, 0, 128, 32, 16))
            with eng.parallel():
                eng.tensor_copy(out=GTS.ap(), in_=pbank(7, 0, 16, 0, 128))
                eng.tensor_copy(out=B128S.ap(), in_=pbank(6, 0, 128, 48, 16))
                eng.tensor_scalar(out=NEGGc, in0=pbank(6, 0, 128, 32, 16),
                                  scalar1=-1.0, scalar2=None, op0=ALU.mult)
                eng.tensor_tensor(out=E2A.ap(), in0=pbank(6, 0, 128, 48, 16),
                                  in1=GSBc, op=ALU.subtract)
                last = eng.tensor_tensor(out=LGDR[1 - pc].ap(),
                                         in0=LGDR[pc].ap(),
                                         in1=pbank(6, 0, 1, 48, 16),
                                         op=ALU.add)
            return last

        def a_s5s(eng, ct=ct):
            last = None
            with eng.parallel():
                for qc in range(16):
                    last = eng.activation(
                        out=SQ[:, qc * 128:(qc + 1) * 128],
                        in_=MIXED[:, qc * LNW + ct:qc * LNW + ct + 128],
                        func=AF.Square)
            return last
        kb.stage(vector=a_s5v, scalar=a_s5s)

        def a_s6v(eng, GTc=GTc):
            return eng.tensor_tensor_scan(out=GTc, data0=GTS.ap(),
                                          data1=GTS.ap(), initial=0.0,
                                          op0=ALU.add, op1=ALU.bypass)

        def a_s6s(eng, GSBc=GSBc, EXPGc=EXPGc, EB128c=EB128c, E2c=E2c):
            with eng.parallel():
                eng.activation(out=EXPGc, in_=GSBc, func=AF.Exp)
                eng.activation(out=EB128c, in_=B128S.ap(), func=AF.Exp)
                last = eng.activation(out=E2c, in_=E2A.ap(), func=AF.Exp)
            return last

        def a_s6t(eng):
            last = None
            for qc in range(8):
                last = eng.matmul(pbank(7, 0, 16, 128, 128),
                                  BONES.ap()[:, qc * 16:(qc + 1) * 16],
                                  SQ[:, qc * 128:(qc + 1) * 128],
                                  start=(qc == 0), stop=(qc == 7))
            for qc in range(8):
                last = eng.matmul(pbank(7, 0, 16, 256, 128),
                                  BONES.ap()[:, qc * 16:(qc + 1) * 16],
                                  SQ[:, (8 + qc) * 128:(9 + qc) * 128],
                                  start=(qc == 0), stop=(qc == 7))
            return last
        kb.stage(vector=a_s6v, scalar=a_s6s, tensor=a_s6t)

        def a_s7s(eng, M2Qc=M2Qc, M2Kc=M2Kc):
            with eng.parallel():
                eng.activation(out=LQ.ap(), in_=pbank(7, 0, 16, 128, 128),
                               func=AF.Ln, bias=CCOL.ap()[0:16, 1:2])
                eng.activation(out=LK.ap(), in_=pbank(7, 0, 16, 256, 128),
                               func=AF.Ln, bias=CCOL.ap()[0:16, 1:2])
            with eng.parallel():
                eng.activation(out=M2Qc, in_=LQ.ap(), func=AF.Exp,
                               scale=-0.5, bias=CCOL.ap()[0:16, 2:3])
                last = eng.activation(out=M2Kc, in_=LK.ap(), func=AF.Exp,
                                      scale=-0.5)
            return last

        def a_s7v(eng, c=c, pc=pc, EXPGc=EXPGc, EB128c=EB128c):
            with eng.parallel():
                eng.tensor_tensor(out=ELGD.ap()[:, c * H:(c + 1) * H],
                                  in0=EXPGc, in1=EXPL[pc].ap(), op=ALU.mult)
                last = eng.tensor_tensor(out=EXPL[1 - pc].ap(),
                                         in0=EXPL[pc].ap(),
                                         in1=EB128c, op=ALU.mult)
            return last
        h_last = kb.stage(scalar=a_s7s, vector=a_s7v)
        prev_head_last = h_last

        def a_s8t(eng, M2Qc=M2Qc, M2Kc=M2Kc):
            last = None
            for qc in range(8):
                last = eng.matmul(pbank(qc % 4, 0, 128, 128 * (qc // 4), 128),
                                  BCASTP.ap()[:, qc * 128:(qc + 1) * 128],
                                  M2Qc, start=True, stop=True)
            for qc in range(8):
                last = eng.matmul(
                    pbank(qc % 4, 0, 128, 256 + 128 * (qc // 4), 128),
                    BCASTP.ap()[:, qc * 128:(qc + 1) * 128],
                    M2Kc, start=True, stop=True)
            return last
        kb.stage(tensor=a_s8t)

        def a_s9v(eng, ct=ct):
            last = None
            with eng.parallel():
             for qc in range(8):
                last = eng.tensor_tensor(
                    out=QNF[:, qc * 128:(qc + 1) * 128],
                    in0=MIXED[:, qc * LNW + ct:qc * LNW + ct + 128],
                    in1=pbank(qc % 4, 0, 128, 128 * (qc // 4), 128),
                    op=ALU.mult)
             for qc in range(8):
                last = eng.tensor_tensor(
                    out=KNF[:, qc * 128:(qc + 1) * 128],
                    in0=MIXED[:, (8 + qc) * LNW + ct:(8 + qc) * LNW + ct + 128],
                    in1=pbank(qc % 4, 0, 128, 256 + 128 * (qc // 4), 128),
                    op=ALU.mult)
            return last
        kb.stage(vector=a_s9v)

        def a_s10t(eng, ct=ct, GTc=GTc):
            last = None
            for h in range(H):
                last = eng.matmul(
                    pbank(h % 4, 0, 128, 128 * (h // 4), 128),
                    I16.ap()[:, h:h + 1].broadcast_to([16, 128]),
                    GTc, start=True, stop=True)
            for qc in range(8):
                last = eng.transpose(pbank(4, 0, 128, qc * 64, 64, BF16),
                                     KNF[:, qc * 128:(qc + 1) * 128],
                                     IDB.ap())
            for qc in range(8):
                last = eng.transpose(
                    pbank(5, 0, 128, qc * 64, 64, BF16),
                    MIXED[:, (16 + qc) * LNW + ct:(16 + qc) * LNW + ct + 128],
                    IDB.ap())
            return last
        kb.stage(tensor=a_s10t)

        def a_s10v(eng, NEGGc=NEGGc):
            last = None
            with eng.parallel():
             for h in range(H):
                last = eng.tensor_scalar(
                    out=r4(2048, 2048)[:, h * 128:(h + 1) * 128],
                    in0=pbank(h % 4, 0, 128, 128 * (h // 4), 128),
                    scalar1=NEGGc[:, h:h + 1], scalar2=0.0,
                    op0=ALU.add, op1=ALU.min)
            return last
        kb.stage(vector=a_s10v)

        def a_s11s(eng, E2c=E2c, BETAc=BETAc):
            last = None
            with eng.parallel():
             for h in range(H):
                last = eng.activation(
                    out=EM[:, h * 128:(h + 1) * 128],
                    in_=r4(2048, 2048)[:, h * 128:(h + 1) * 128],
                    func=AF.Exp)
             for qc in range(8):
                for hh in range(2):
                    h = 2 * qc + hh
                    last = eng.activation(
                        out=KH[:, h * 64:(h + 1) * 64],
                        in_=pbank(4, 0, 128, qc * 64, 64, BF16)
                        [:, hh * 64:(hh + 1) * 64],
                        func=AF.Copy, scale=E2c[:, h:h + 1])
                    last = eng.activation(
                        out=UTM[:, h * 64:(h + 1) * 64],
                        in_=pbank(5, 0, 128, qc * 64, 64, BF16)
                        [:, hh * 64:(hh + 1) * 64],
                        func=AF.Copy, scale=BETAc[:, h:h + 1])
            return last
        kb.stage(scalar=a_s11s)

        def a_s12v(eng):
            last = None
            with eng.parallel():
             for h in range(H):
                v = EM[:, h * 128:(h + 1) * 128].bitcast(U16)
                last = eng.tensor_tensor(out=v, in0=v, in1=MASKB.ap(),
                                         op=ALU.bitwise_and)
            return last
        kb.stage(vector=a_s12v)

        def a_s13t(eng, lt=lt):
            last = None
            for h in range(H):
                qc, hh = h // 2, h % 2
                last = eng.matmul(
                    pbank(h % 4, 0, 128, 128 * (h // 4), 128),
                    KNF[hh * 64:(hh + 1) * 64, qc * 128:(qc + 1) * 128],
                    QNF[hh * 64:(hh + 1) * 64, qc * 128:(qc + 1) * 128],
                    start=True, stop=True)
            for dc in range(DC):
                for pi in range(2):
                    last = eng.matmul(
                        pbank(4 + pi, 0, 128, 0, 512),
                        LN1[:, dc * LNW + lt:dc * LNW + lt + 128],
                        WZT[:, dc * D + pi * 512:dc * D + (pi + 1) * 512],
                        start=(dc == 0), stop=(dc == DC - 1))
            return last
        kb.stage(tensor=a_s13t)

        def a_s14v(eng):
            last = None
            with eng.parallel():
             for h in range(H):
                last = eng.tensor_tensor(
                    out=EM[:, h * 128:(h + 1) * 128],
                    in0=pbank(h % 4, 0, 128, 128 * (h // 4), 128),
                    in1=EM[:, h * 128:(h + 1) * 128], op=ALU.mult)
             for pi in range(2):
                last = eng.tensor_copy(out=ZB[:, pi * 512:(pi + 1) * 512],
                                       in_=pbank(4 + pi, 0, 128, 0, 512))
            return last
        kb.stage(vector=a_s14v)

        def a_s15t(eng, pc=pc):
            last = None
            for h in range(H):
                last = eng.matmul(pbank(h % 2, 0, 128, (h // 2) * 64, 64),
                                  EM[:, h * 128:(h + 1) * 128],
                                  UTM[:, h * 64:(h + 1) * 64],
                                  start=True, stop=True)
            for h in range(H):
                qc, hh = h // 2, h % 2
                last = eng.matmul(
                    pbank(2 + h % 2, 0, 128, (h // 2) * 64, 64),
                    QNF[hh * 64:(hh + 1) * 64, qc * 128:(qc + 1) * 128],
                    SLOC[pc].ap()[hh * 64:(hh + 1) * 64,
                                  h * 64:(h + 1) * 64],
                    start=True, stop=True)
            for h in range(H):
                last = eng.matmul(pbank(4 + h % 2, 0, 64, (h // 2) * 64, 64),
                                  KH[:, h * 64:(h + 1) * 64],
                                  UTM[:, h * 64:(h + 1) * 64],
                                  start=True, stop=True)
            for h in range(H):
                last = eng.matmul(pbank(4 + h % 2, 64, 64, (h // 2) * 64, 64),
                                  KH[:, h * 64:(h + 1) * 64],
                                  UTM[:, h * 64:(h + 1) * 64],
                                  start=True, stop=True,
                                  tile_position=(0, 64))
            return last
        kb.stage(tensor=a_s15t)

        def a_s16s(eng, EXPGc=EXPGc):
            last = None
            with eng.parallel():
             for h in range(H):
                last = eng.activation(
                    out=T1F[:, h * 64:(h + 1) * 64],
                    in_=pbank(2 + h % 2, 0, 128, (h // 2) * 64, 64),
                    func=AF.Copy, scale=EXPGc[:, h:h + 1])
            return last
        kb.stage(scalar=a_s16s)

        def a_s17v(eng, pc=pc, EB128c=EB128c):
            last = None
            with eng.parallel():
             for h in range(H):
                last = eng.tensor_tensor(
                    out=OIB[:, h * 64:(h + 1) * 64],
                    in0=T1F[:, h * 64:(h + 1) * 64],
                    in1=pbank(h % 2, 0, 128, (h // 2) * 64, 64), op=ALU.add)
             for h in range(H):
                for b0 in (0, 64):
                    last = eng.scalar_tensor_tensor(
                        out=SLOC[1 - pc].ap()[b0:b0 + 64,
                                              h * 64:(h + 1) * 64],
                        in0=SLOC[pc].ap()[b0:b0 + 64, h * 64:(h + 1) * 64],
                        scalar=EB128c[b0:b0 + 64, h:h + 1],
                        in1=pbank(4 + h % 2, b0, 64, (h // 2) * 64, 64),
                        op0=ALU.mult, op1=ALU.add)
            return last
        t17 = kb.stage(vector=a_s17v)
        prev_tail17[pc] = t17

        def a_s18(eng, c=c):
            return dseq(eng, [
                (sp_oi[c * 128:(c + 1) * 128, :], OIB),
                (sp_qn[c * 128:(c + 1) * 128, :], QNF),
                (sp_z[c * 128:(c + 1) * 128, :], ZB),
            ])
        kb.stage(sync=a_s18)

    marks['attn'] = len(kb.stages)
    def quant_chain(eng, src, F, out_bf, sa, sb_, sc, bm, bm2):
        nb = F // 16
        eng.tensor_reduce(out=bm, in_=src.rearrange("p (b s) -> p b s", s=16),
                          axis=AX.X, op=ALU.max, apply_absolute_value=True)
        eng.tensor_scalar(out=bm2, in0=bm, scalar1=1.0 / 6.0, scalar2=1e-12,
                          op0=ALU.mult, op1=ALU.max)
        eng.reciprocal(out=bm, in_=bm2)
        eng.tensor_tensor(out=sa.rearrange("p (b s) -> p b s", s=16),
                          in0=src.rearrange("p (b s) -> p b s", s=16),
                          in1=bm.unsqueeze(2).broadcast_to([128, nb, 16]),
                          op=ALU.mult)
        with eng.parallel():
            eng.tensor_scalar(out=sb_, in0=sa, scalar1=2.0, scalar2=-2.0,
                              op0=ALU.min, op1=ALU.max)
            eng.tensor_scalar(out=sc, in0=sa, scalar1=4.0, scalar2=-4.0,
                              op0=ALU.min, op1=ALU.max)
        eng.scalar_tensor_tensor(out=sb_, in0=sa, scalar=0.5, in1=sb_,
                                 op0=ALU.mult, op1=ALU.add)
        eng.scalar_tensor_tensor(out=sc, in0=sc, scalar=0.5, in1=sb_,
                                 op0=ALU.mult, op1=ALU.add)
        eng.tensor_scalar(out=sa, in0=sc, scalar1=MAGIC, scalar2=MAGIC,
                          op0=ALU.add, op1=ALU.subtract)
        with eng.parallel():
            eng.tensor_scalar(out=sb_, in0=sa, scalar1=4.0, scalar2=-4.0,
                              op0=ALU.min, op1=ALU.max)
            eng.tensor_scalar(out=sc, in0=sa, scalar1=6.0, scalar2=-6.0,
                              op0=ALU.min, op1=ALU.max)
        eng.scalar_tensor_tensor(out=sa, in0=sa, scalar=2.0, in1=sc,
                                 op0=ALU.mult, op1=ALU.subtract)
        eng.scalar_tensor_tensor(out=sb_, in0=sb_, scalar=-0.5, in1=sa,
                                 op0=ALU.mult, op1=ALU.add)
        return eng.tensor_tensor(
            out=out_bf.rearrange("p (b s) -> p b s", s=16),
            in0=sb_.rearrange("p (b s) -> p b s", s=16),
            in1=bm2.unsqueeze(2).broadcast_to([128, nb, 16]), op=ALU.mult)

    def emit_fq(part):
        FSRC = r4(0, 1408)
        FSA = r4(1408, 1408)
        FSB = r4(2816, 1408)
        FSC = r4(4224, 1408)
        FBM = r4(5632, 88)
        FBM2 = r4(5720, 88)
        FOUT = r5(0, 1408)

        def fq_seg(src_dram, rows0, cols0, F, dst_fn, sub0):
            def f_l(eng):
                return dseq(eng, [(FSRC[:, 0:F],
                                   src_dram[rows0:rows0 + 128, cols0:cols0 + F])])
            kb.stage(sync=f_l)

            def f_q(eng):
                return quant_chain(eng, FSRC[:, 0:F], F, FOUT[:, 0:F],
                                   FSA[:, 0:F], FSB[:, 0:F], FSC[:, 0:F],
                                   FBM[:, 0:F // 16], FBM2[:, 0:F // 16])
            kb.stage(vector=f_q)

            nsub = F // 128
            for s0 in range(0, nsub, 8):
                def f_tr(eng, s0=s0, nsub=nsub):
                    last = None
                    for k in range(s0, min(s0 + 8, nsub)):
                        last = eng.transpose(pbank(k % 8, 0, 128, 0, 64, BF16),
                                             FOUT[:, k * 128:(k + 1) * 128],
                                             IDB.ap())
                    return last
                kb.stage(tensor=f_tr)

                def f_ev(eng, s0=s0, nsub=nsub, sub0=sub0):
                    last = None
                    with eng.parallel():
                     for k in range(s0, min(s0 + 8, nsub)):
                        last = eng.tensor_copy(
                            out=dst_fn(sub0 + k),
                            in_=pbank(k % 8, 0, 128, 0, 64, BF16))
                    return last
                kb.stage(vector=f_ev)

        if part == "main":
            for wsrc, wdst in ((wg_d, wgT_q), (wu_d, wuT_q)):
                for nt in range(IF // 128):
                    def mk(wdst=wdst, nt=nt):
                        return lambda k: wdst[:, k * IF + nt * 128:
                                              k * IF + nt * 128 + 128]
                    fq_seg(wsrc, nt * 128, 0, D, mk(), 0)
            for dt in range(DC):
                def mkd(dt=dt):
                    return lambda k: wdT_chunk(k)[:, dt * 128:dt * 128 + 128]
                fq_seg(wd_d, dt * 128, 0, 1408, mkd(), 0)
        else:
            for dt in range(DC):
                def mkd(dt=dt):
                    return lambda k: wdT_chunk(k)[:, dt * 128:dt * 128 + 128]
                fq_seg(wd_d, dt * 128, 1408, 1408, mkd(), 11)


    SSTG = r4(0, 1024, 0, 65)
    MSKT = r4(1024, 1024, 0, 65)
    RFIN = r4(2048, 1024, 0, 64)
    REP = r4(3072, 1024, 0, 16)
    ACC = r4(4096, 1024, 0, 64)
    WSTG = r4(5120, 1024)
    DTOTS = r4(6144, 16, 0, 8)
    DPT = r4(6160, 16, 0, 8)
    LNDP = r4(6176, 16, 0, 8)
    COEF = r4(6192, 16, 0, 8)
    COEFM = r4(6208, 16, 0, 8)
    COEFT = r4(6224, 8, 0, 16)
    SU8S = r4(6232, 8, 0, 8)

    def x_s1(eng):
        eng.memset(SSTG[64:65, :], 0.0)
        return eng.tensor_copy(out=SSTG[0:64, :], in_=SLOC[0].ap()[0:64, :])
    kb.stage(vector=x_s1)

    def x_s2(eng):
        return eng.activation(out=TMP1.ap()[0:1, 0:16], in_=LGDR[0].ap(),
                              func=AF.Exp)
    kb.stage(scalar=x_s2)

    def x_s2b(eng):
        eng.dma_batch([(SSTG[64:65, 0:16], TMP1.ap()[0:1, 0:16])])
        return "self"
    kb.stage(gpsimd=x_s2b)

    for j in range(8):
        def x_mask(eng, j=j):
            return eng.tensor_scalar(out=MSKT, in0=SSTG,
                                     scalar1=SELV.ap()[0:65, j:j + 1],
                                     scalar2=None, op0=ALU.mult)
        kb.stage(vector=x_mask)

        def x_send(eng, j=j):
            return dseq(eng, [(cc_send[j * 65:(j + 1) * 65, :], MSKT)])
        kb.stage(sync=x_send)

    def x_cc(eng):
        if nocc:
            eng.dma_batch([(cc_recv[j * 65:(j + 1) * 65, :], MSKT)
                           for j in range(8)])
            eng.sem_inc(ccsem, 1)
        else:
            eng.collective_async(ccsem, "AllReduce", ALU.add,
                                 replica_groups=[[0, 1, 2, 3, 4, 5, 6, 7]],
                                 ins=[cc_send.opt()], outs=[cc_recv.opt()])
        return "self"
    kb.stage(gpsimd=x_cc)

    emit_fq("main")

    def x_rl(eng):
        eng.wait_ge(ccsem, 1)
        return dseq(eng, [
            (DTOTS, cc_recv.rearrange("(j r) c -> j r c", r=65)[:, 64, 0:16]),
            (REP, rep16_d),
            (SU8S, su8_d),
        ])
    kb.stage(sync=x_rl)

    def x_accz(eng):
        return eng.memset(ACC, 0.0)
    kb.stage(vector=x_accz)

    def x_dp(eng):
        return eng.scalar_tensor_tensor(out=DPT, in0=DTOTS,
                                        scalar=SELLT.ap(), in1=INVSEL.ap(),
                                        op0=ALU.mult, op1=ALU.add)
    kb.stage(vector=x_dp)

    def x_lnc(eng):
        return eng.tensor_scalar(out=DPT, in0=DPT, scalar1=1e-38,
                                 scalar2=None, op0=ALU.max)
    kb.stage(vector=x_lnc)

    def x_ln(eng):
        return eng.activation(out=LNDP, in_=DPT, func=AF.Ln)
    kb.stage(scalar=x_ln)

    def x_mm(eng):
        return eng.matmul(pbank(0, 0, 8, 0, 16), SU8S, LNDP,
                          start=True, stop=True)
    kb.stage(tensor=x_mm)

    def x_coef(eng):
        return eng.activation(out=COEF, in_=pbank(0, 0, 8, 0, 16),
                              func=AF.Exp)
    kb.stage(scalar=x_coef)

    def x_coefm(eng):
        return eng.tensor_scalar(out=COEFM, in0=COEF, scalar1=SELLT.ap(),
                                 scalar2=None, op0=ALU.mult)
    kb.stage(vector=x_coefm)

    def x_ct(eng):
        return eng.transpose(pbank(1, 0, 16, 0, 8), COEFM, IDF.ap()[0:8, 0:8])
    kb.stage(tensor=x_ct)

    def x_cte(eng):
        return eng.tensor_copy(out=COEFT, in_=pbank(1, 0, 16, 0, 8))
    kb.stage(vector=x_cte)

    for j in range(8):
        def x_rf(eng, j=j):
            return dseq(eng, [(RFIN, cc_recv[j * 65:j * 65 + 64, :])])
        kb.stage(sync=x_rf)

        def x_bc(eng, j=j):
            eng.matmul(pbank(2, 0, 64, 0, 512),
                       COEFT[:, j:j + 1].broadcast_to([16, 64]),
                       REP[:, 0:512], start=True, stop=True)
            return eng.matmul(pbank(3, 0, 64, 0, 512),
                              COEFT[:, j:j + 1].broadcast_to([16, 64]),
                              REP[:, 512:1024], start=True, stop=True)
        kb.stage(tensor=x_bc)

        def x_acc(eng, j=j):
            eng.tensor_tensor(out=MSKT[0:64, 0:512],
                              in0=pbank(2, 0, 64, 0, 512),
                              in1=RFIN[:, 0:512], op=ALU.mult)
            eng.tensor_tensor(out=MSKT[0:64, 512:1024],
                              in0=pbank(3, 0, 64, 0, 512),
                              in1=RFIN[:, 512:1024], op=ALU.mult)
            return eng.tensor_tensor(out=ACC, in0=ACC, in1=MSKT[0:64, :],
                                     op=ALU.add)
        kb.stage(vector=x_acc)

    def x_cast(eng):
        return eng.tensor_copy(out=SINIT.ap()[0:64, :], in_=ACC)
    kb.stage(vector=x_cast)

    if dbg:
        def x_dbg(eng):
            pairs = []
            for jj in range(8):
                pairs.append((dbg_s[jj * 65:(jj + 1) * 65, :],
                              cc_recv[jj * 65:(jj + 1) * 65, :]))
            return dseq(eng, pairs)
        kb.stage(sync=x_dbg)

    def x_cast2(eng):
        eng.dma_batch([(SINIT.ap()[64:128, :], SINIT.ap()[0:64, :])])
        return "self"
    kb.stage(gpsimd=x_cast2)

    for dcz in range(DC):
        def w_l(eng, dcz=dcz):
            return dseq(eng, [(WSTG, woutt_d[dcz * 128:(dcz + 1) * 128, :])])
        def w_c(eng, dcz=dcz):
            return eng.copy(out=WOUTT[:, dcz * D:(dcz + 1) * D], in_=WSTG)
        kb.stage(sync=w_l)
        kb.stage(scalar=w_c)

    marks['exch'] = len(kb.stages)
    XR = r4(0, 1024)
    OFULL = r4(1024, 1024)
    T2 = r4(2048, 1024)
    X2T = r4(3072, 1024)
    LN2T = r4(4096, 1024)
    QSA = r4(5120, 1024)
    QSB = r4(6144, 1024)
    QSC = r4(7168, 1024)
    CBM = r4(2048, 64)
    CBM2 = r4(2112, 64)
    OIR = r5(0, 1024)
    QNR = r5(1024, 1024)
    ZR = r5(2048, 1024)
    SLZ = r5(3072, 1024)
    OGD = r5(4096, 1024)
    OGF = r5(5120, 1024)
    XQB = r5(6144, 1024)

    for c in range(NCH):
        def c_s1(eng, c=c):
            return dseq(eng, [
                (OIR, sp_oi[c * 128:(c + 1) * 128, :]),
                (QNR, sp_qn[c * 128:(c + 1) * 128, :]),
                (ZR, sp_z[c * 128:(c + 1) * 128, :]),
                (XR, x_d[c * 128:(c + 1) * 128, :]),
            ])
        kb.stage(sync=c_s1)

        def c_s2(eng):
            last = None
            for h in range(H):
                qc, hh = h // 2, h % 2
                last = eng.matmul(
                    pbank(h % 2, 0, 128, (h // 2) * 64, 64),
                    QNR[hh * 64:(hh + 1) * 64, qc * 128:(qc + 1) * 128],
                    SINIT.ap()[hh * 64:(hh + 1) * 64, h * 64:(h + 1) * 64],
                    start=True, stop=True)
            return last
        kb.stage(tensor=c_s2)

        def c_s3(eng, c=c):
            last = eng.activation(out=SLZ, in_=ZR, func=AF.Silu)
            for h in range(H):
                last = eng.activation(
                    out=T2[:, h * 64:(h + 1) * 64],
                    in_=pbank(h % 2, 0, 128, (h // 2) * 64, 64),
                    func=AF.Copy, scale=ELGD.ap()[:, c * H + h:c * H + h + 1])
            return last
        kb.stage(scalar=c_s3)

        def c_s4(eng):
            return eng.tensor_tensor(out=OFULL, in0=T2, in1=OIR, op=ALU.add)
        kb.stage(vector=c_s4)

        def c_s5(eng):
            last = None
            for h in range(H):
                last = eng.activation(out=T2[:, 0:64],
                                      in_=OFULL[:, h * 64:(h + 1) * 64],
                                      func=AF.Square,
                                      accum_out=MSQ.ap()[:, h:h + 1])
            return last
        kb.stage(scalar=c_s5)

        def c_s6(eng):
            return eng.tensor_scalar(out=RSTD.ap(), in0=MSQ.ap(),
                                     scalar1=1.0 / HD, scalar2=1e-6,
                                     op0=ALU.mult, op1=ALU.add)
        kb.stage(vector=c_s6)

        def c_s7(eng):
            eng.activation(out=MSQ.ap(), in_=RSTD.ap(), func=AF.Ln)
            return eng.activation(out=RSTD.ap(), in_=MSQ.ap(), func=AF.Exp,
                                  scale=-0.5)
        kb.stage(scalar=c_s7)

        def c_s8(eng):
            last = None
            for h in range(H):
                last = eng.scalar_tensor_tensor(
                    out=OGD[:, h * 64:(h + 1) * 64],
                    in0=OFULL[:, h * 64:(h + 1) * 64],
                    scalar=RSTD.ap()[:, h:h + 1],
                    in1=SLZ[:, h * 64:(h + 1) * 64],
                    op0=ALU.mult, op1=ALU.mult)
            return last
        kb.stage(vector=c_s8)

        def c_s9(eng):
            last = None
            for dc in range(DC):
                last = eng.transpose(pbank(dc, 0, 128, 0, 64, BF16),
                                     OGD[:, dc * 128:(dc + 1) * 128],
                                     IDB.ap())
            return last
        kb.stage(tensor=c_s9)

        def c_s10(eng):
            last = None
            with eng.parallel():
             for dc in range(DC):
                last = eng.tensor_copy(out=OGF[:, dc * 128:(dc + 1) * 128],
                                       in_=pbank(dc, 0, 128, 0, 64, BF16))
            return last
        kb.stage(vector=c_s10)

        def c_s11(eng):
            last = None
            for dc in range(DC):
                for pi in range(2):
                    last = eng.matmul(
                        pbank(4 + pi, 0, 128, 0, 512),
                        OGF[:, dc * 128:(dc + 1) * 128],
                        WOUTT[:, dc * D + pi * 512:dc * D + (pi + 1) * 512],
                        start=(dc == 0), stop=(dc == DC - 1))
            return last
        kb.stage(tensor=c_s11)

        def c_s12(eng):
            with eng.parallel():
                eng.tensor_tensor(out=X2T[:, 0:512],
                                  in0=pbank(4, 0, 128, 0, 512),
                                  in1=XR[:, 0:512], op=ALU.add)
                last = eng.tensor_tensor(out=X2T[:, 512:1024],
                                         in0=pbank(5, 0, 128, 0, 512),
                                         in1=XR[:, 512:1024], op=ALU.add)
            return last
        kb.stage(vector=c_s12)

        def c_s13(eng):
            with eng.parallel():
                eng.activation(out=T2[:, 0:512], in_=X2T[:, 0:512],
                               func=AF.Copy, accum_out=STA.ap()[:, 2:3])
                eng.activation(out=T2[:, 512:1024], in_=X2T[:, 512:1024],
                               func=AF.Copy, accum_out=STA.ap()[:, 3:4])
                eng.activation(out=LN2T[:, 0:512], in_=X2T[:, 0:512],
                               func=AF.Square, accum_out=STA.ap()[:, 4:5])
                last = eng.activation(out=LN2T[:, 512:1024],
                                      in_=X2T[:, 512:1024],
                                      func=AF.Square,
                                      accum_out=STA.ap()[:, 5:6])
            return last
        kb.stage(scalar=c_s13)

        def c_s14(eng):
            with eng.parallel():
                eng.tensor_tensor(out=STA.ap()[:, 0:1], in0=STA.ap()[:, 2:3],
                                  in1=STA.ap()[:, 3:4], op=ALU.add)
                eng.tensor_tensor(out=STA.ap()[:, 1:2], in0=STA.ap()[:, 4:5],
                                  in1=STA.ap()[:, 5:6], op=ALU.add)
            eng.tensor_scalar(out=STA.ap()[:, 2:3], in0=STA.ap()[:, 0:1],
                              scalar1=1.0 / D, scalar2=None, op0=ALU.mult)
            eng.tensor_scalar(out=STA.ap()[:, 3:4], in0=STA.ap()[:, 1:2],
                              scalar1=1.0 / D, scalar2=None, op0=ALU.mult)
            eng.tensor_tensor(out=STA.ap()[:, 4:5], in0=STA.ap()[:, 2:3],
                              in1=STA.ap()[:, 2:3], op=ALU.mult)
            return eng.tensor_tensor(out=STA.ap()[:, 5:6],
                                     in0=STA.ap()[:, 3:4],
                                     in1=STA.ap()[:, 4:5], op=ALU.subtract)
        kb.stage(vector=c_s14)

        def c_s15(eng):
            eng.activation(out=STA.ap()[:, 6:7], in_=STA.ap()[:, 5:6],
                           func=AF.Ln, bias=CCOL.ap()[:, 0:1])
            return eng.activation(out=STA.ap()[:, 6:7], in_=STA.ap()[:, 6:7],
                                  func=AF.Exp, scale=-0.5)
        kb.stage(scalar=c_s15)

        def c_s16(eng):
            return eng.scalar_tensor_tensor(
                out=STA.ap()[:, 7:8], in0=STA.ap()[:, 2:3], scalar=-1.0,
                in1=STA.ap()[:, 6:7], op0=ALU.mult, op1=ALU.mult)
        kb.stage(vector=c_s16)

        def c_s17(eng):
            return eng.tensor_scalar(out=LN2T, in0=X2T,
                                     scalar1=STA.ap()[:, 6:7],
                                     scalar2=STA.ap()[:, 7:8],
                                     op0=ALU.mult, op1=ALU.add)
        kb.stage(vector=c_s17)

        def c_s18(eng):
            return quant_chain(eng, LN2T, D, XQB, QSA, QSB, QSC, CBM, CBM2)
        kb.stage(vector=c_s18)

        def c_s19(eng):
            last = None
            for dc in range(DC):
                last = eng.transpose(pbank(dc, 0, 128, 0, 64, BF16),
                                     XQB[:, dc * 128:(dc + 1) * 128],
                                     IDB.ap())
            return last
        kb.stage(tensor=c_s19)

        def c_s20(eng):
            last = None
            with eng.parallel():
             for dc in range(DC):
                last = eng.tensor_copy(out=OGF[:, dc * 128:(dc + 1) * 128],
                                       in_=pbank(dc, 0, 128, 0, 64, BF16))
            return last
        kb.stage(vector=c_s20)

        def c_s21(eng, c=c):
            return dseq(eng, [
                (sp_xqf[c * 128:(c + 1) * 128, :], OGF),
                (sp_x2[c * 128:(c + 1) * 128, :], X2T),
            ])
        kb.stage(sync=c_s21)

    marks['c'] = len(kb.stages)
    emit_fq("tail")

    marks['fq'] = len(kb.stages)
    DX2R = r4(0, 1024)
    DHB = r4(1024, 2816)
    DSA = r4(3840, 1408)
    DSB = r4(5248, 1408)
    DSC = r4(6656, 1408)
    DBM = r4(8064, 88)
    DBM2 = R5.ap().bitcast(F32)[:, 4280:4368]
    XQFR = r5(0, 1024)
    HQ = r5(1024, 2816)
    HQFD = r5(3840, 2816)
    OUTT = R5.ap().bitcast(F32)[:, 3328:4352]
    DBM2 = R5.ap().bitcast(F32)[:, 4352:4440]

    prev_out = None
    prev_lastmm = None
    for c in range(NCH):
        def d_s1(eng, c=c):
            return dseq(eng, [(XQFR, sp_xqf[c * 128:(c + 1) * 128, :])])
        i_s1 = kb.stage(sync=d_s1, after_=prev_lastmm)
        DSL = r4(0, 352)

        def mk_mm(np_):
            b = (np_ % 2) * 2
            def d_mm(eng, np_=np_, b=b):
                last = None
                for dc in range(DC):
                    last = eng.matmul(
                        pbank(b, 0, 128, 0, 352),
                        XQFR[:, dc * 128:(dc + 1) * 128],
                        wgT_q[:, dc * IF + np_ * 352:
                              dc * IF + (np_ + 1) * 352],
                        start=(dc == 0), stop=(dc == DC - 1))
                    last = eng.matmul(
                        pbank(b + 1, 0, 128, 0, 352),
                        XQFR[:, dc * 128:(dc + 1) * 128],
                        wuT_q[:, dc * IF + np_ * 352:
                              dc * IF + (np_ + 1) * 352],
                        start=(dc == 0), stop=(dc == DC - 1))
                return last
            return d_mm

        def mk_si(np_):
            b = (np_ % 2) * 2
            def d_si(eng, np_=np_, b=b):
                return eng.activation(out=DSL, in_=pbank(b, 0, 128, 0, 352),
                                      func=AF.Silu)
            return d_si

        def mk_h(np_):
            b = (np_ % 2) * 2
            def d_h(eng, np_=np_, b=b):
                return eng.tensor_tensor(
                    out=DHB[:, np_ * 352:(np_ + 1) * 352],
                    in0=DSL, in1=pbank(b + 1, 0, 128, 0, 352), op=ALU.mult)
            return d_h

        i_mm = [None] * 8
        i_si = [None] * 8
        i_h = [None] * 8
        for np_ in range(8):
            amm = i_s1 if np_ == 0 else (
                i_mm[0] if np_ == 1 else i_h[np_ - 2])
            i_mm[np_] = kb.stage(tensor=mk_mm(np_), after_=amm)
            asi = i_mm[np_] if np_ == 0 else i_h[np_ - 1]
            asi = max(asi, i_mm[np_]) if np_ > 0 else i_mm[0]
            i_si[np_] = kb.stage(scalar=mk_si(np_), after_=asi)
            i_h[np_] = kb.stage(vector=mk_h(np_), after_=i_si[np_])

        def d_s1b(eng, c=c):
            return dseq(eng, [(DX2R, sp_x2[c * 128:(c + 1) * 128, :])])
        kb.stage(sync=d_s1b, after_=i_h[7])

        def d_q0(eng):
            return quant_chain(eng, DHB[:, 0:1408], 1408, HQ[:, 0:1408],
                               DSA, DSB, DSC, DBM, DBM2)
        i_q0 = kb.stage(vector=d_q0, after_=i_h[3])

        def d_q1(eng):
            return quant_chain(eng, DHB[:, 1408:2816], 1408,
                               HQ[:, 1408:2816], DSA, DSB, DSC, DBM, DBM2)
        i_q1 = kb.stage(vector=d_q1, after_=i_h[7])

        last_te = None
        tr_groups = [(0, 4), (4, 8), (8, 11), (11, 15), (15, 19), (19, 22)]
        for gi, (k0, k1) in enumerate(tr_groups):
            def d_tr(eng, k0=k0, k1=k1):
                last = None
                for k in range(k0, k1):
                    last = eng.transpose(
                        pbank(4 + (k - k0) % 4, 0, 128, 0, 64, BF16),
                        HQ[:, k * 128:(k + 1) * 128], IDB.ap())
                return last
            i_tr = kb.stage(tensor=d_tr,
                            after_=(i_q0 if gi == 0 else None))

            def d_te(eng, k0=k0, k1=k1):
                last = None
                with eng.parallel():
                 for k in range(k0, k1):
                    last = eng.tensor_copy(
                        out=HQFD[:, k * 128:(k + 1) * 128],
                        in_=pbank(4 + (k - k0) % 4, 0, 128, 0, 64, BF16))
                return last
            last_te = kb.stage(vector=d_te)

        def d_dn(eng):
            last = None
            for k in range(22):
                for pi in range(2):
                    last = eng.matmul(pbank(4 + pi, 0, 128, 0, 512),
                                      HQFD[:, k * 128:(k + 1) * 128],
                                      wdT_chunk(k)[:, pi * 512:(pi + 1) * 512],
                                      start=(k == 0), stop=(k == 21))
            return last
        i_dn = kb.stage(tensor=d_dn, after_=last_te)
        prev_lastmm = i_dn

        def d_out(eng):
            with eng.parallel():
                eng.tensor_tensor(out=OUTT[:, 0:512],
                                  in0=pbank(4, 0, 128, 0, 512),
                                  in1=DX2R[:, 0:512], op=ALU.add)
                last = eng.tensor_tensor(out=OUTT[:, 512:1024],
                                         in0=pbank(5, 0, 128, 0, 512),
                                         in1=DX2R[:, 512:1024], op=ALU.add)
            return last
        i_out = kb.stage(vector=d_out, after_=i_dn)
        prev_out = i_out

        def d_st(eng, c=c):
            return dseq(eng, [(y_d[c * 128:(c + 1) * 128, :], OUTT)])
        kb.stage(sync=d_st, after_=i_out)

    marks['all'] = len(kb.stages)
    if max_stage is not None:
        if isinstance(max_stage, str):
            max_stage = marks[max_stage]
        kb.stages = kb.stages[:max_stage]
    nc._phase_marks = marks
    stgs = {n: st.enter_context(nc.semaphore("st_" + n))
            for n in ("sync", "scalar", "vector", "tensor", "gpsimd")}
    with nc.Block() as block:
        block.sync(kb.body(stgs, esems["sync"], "sync"))
        block.scalar(kb.body(stgs, esems["scalar"], "scalar"))
        block.vector(kb.body(stgs, esems["vector"], "vector"))
        block.tensor(kb.body(stgs, esems["tensor"], "tensor"))
        block.gpsimd(kb.body(stgs, esems["gpsimd"], "gpsimd"))

    st.close()
    return nc



def kernel(L0_attn_x, L0_attn_W_qkv, L0_attn_conv_w, L0_attn_W_z, L0_attn_W_b,
           L0_attn_W_a, L0_attn_dt_bias, L0_attn_A_log, L0_attn_W_out,
           L0_ffn_w_gate, L0_ffn_w_up, L0_ffn_w_down):
    x = np.ascontiguousarray(np.asarray(L0_attn_x, np.float32).reshape(T, D))
    wqkv = np.asarray(L0_attn_W_qkv, np.float32)
    convw = np.ascontiguousarray(
        np.asarray(L0_attn_conv_w, np.float32).reshape(CONV, 4))
    wz = np.asarray(L0_attn_W_z, np.float32)
    wb = np.asarray(L0_attn_W_b, np.float32)
    wa = np.asarray(L0_attn_W_a, np.float32)
    dtb = np.asarray(L0_attn_dt_bias, np.float32)
    alog = np.asarray(L0_attn_A_log, np.float32)
    wout = np.asarray(L0_attn_W_out, np.float32)
    wg = np.ascontiguousarray(np.asarray(L0_ffn_w_gate, np.float32))
    wu = np.ascontiguousarray(np.asarray(L0_ffn_w_up, np.float32))
    wd = np.ascontiguousarray(np.asarray(L0_ffn_w_down, np.float32))

    wqkvt = np.ascontiguousarray(wqkv.T)
    wqkvt_tiled = np.ascontiguousarray(
        wqkvt.reshape(DC, 128, CCH, 128).transpose(2, 0, 1, 3)
    ).reshape(CCH * DC * 128, 128)
    wzt = np.ascontiguousarray(wz.T)
    woutt = np.ascontiguousarray(wout.T)
    wbat = np.zeros((D + 1, 32), np.float32)
    wbat[:D, 0:16] = wb.T
    wbat[:D, 16:32] = wa.T
    wbat[D, 16:32] = dtb

    s_ar = np.arange(128)
    tri = (s_ar[:, None] <= s_ar[None, :]).astype(np.float32)
    ones = np.ones((128, 128), np.float32)
    identf = np.eye(128, dtype=np.float32)
    identb = np.eye(128).astype(BF)
    i16 = np.eye(16, dtype=np.float32)
    maskb = np.where(s_ar[:, None] <= s_ar[None, :], 0xFFFF, 0).astype(np.uint16)
    bones = np.zeros((128, 8 * 16), np.float32)
    for qc in range(8):
        for p in range(128):
            bones[p, qc * 16 + 2 * qc + (1 if p >= 64 else 0)] = 1
    bcastp = np.zeros((16, 8 * 128), np.float32)
    for qc in range(8):
        for p in range(128):
            bcastp[2 * qc + (1 if p >= 64 else 0), qc * 128 + p] = 1
    rep16 = np.zeros((16, D), np.float32)
    for h in range(16):
        rep16[h, h * 64:(h + 1) * 64] = 1
    su8 = (np.arange(8)[:, None] > np.arange(8)[None, :]).astype(np.float32)

    shared = dict(
        wqkvt=wqkvt_tiled, wzt=wzt, wbat=wbat, alog=alog.reshape(1, H),
        convw=convw, woutt=woutt, wg=wg, wu=wu, wd=wd,
        tri=tri, ones=ones, identf=identf, identb=identb, i16=i16,
        maskb=maskb, bones=bones.astype(BF), bcastp=bcastp.astype(BF),
        rep16=rep16, su8=su8,
    )

    in_maps = []
    for r in range(NCORE):
        selv = np.zeros((128, 8), np.float32)
        selv[:, r] = 1.0
        sellt = (np.arange(8) < r).astype(np.float32).reshape(8, 1)
        invsel = np.tile(1.0 - sellt, (1, 16)).astype(np.float32)
        xprev = np.zeros((3, D), np.float32)
        if r > 0:
            xprev = x[r * TLOC - 3:r * TLOC]
        m = dict(shared)
        m.update(x=np.ascontiguousarray(x[r * TLOC:(r + 1) * TLOC]),
                 xprev=np.ascontiguousarray(xprev),
                 selv=selv, sellt=sellt, invsel=invsel)
        in_maps.append(m)

    nc = build_program()
    res = run_bass_kernel_spmd(nc, in_maps, core_ids=list(range(NCORE)))
    out = np.concatenate(
        [np.asarray(res.results[r]["y"]) for r in range(NCORE)], axis=0)
    return out.reshape(1, T, D).astype(np.float32)


if __name__ == "__main__":
    import sys
    if "--build" in sys.argv:
        nc = build_program()
        n = sum(1 for _ in nc.all_instructions())
        print("built ok, instructions:", n)


# version 15
# speedup vs baseline: 1.0282x; 1.0282x over previous
import numpy as np
import ml_dtypes
import concourse.bass as bass
import concourse.mybir as mybir
from concourse.bass_utils import run_bass_kernel_spmd
from contextlib import ExitStack

F32 = mybir.dt.float32
BF16 = mybir.dt.bfloat16
U16 = mybir.dt.uint16
AF = mybir.ActivationFunctionType
ALU = mybir.AluOpType
AX = mybir.AxisListType

NCORE = 8
T = 16384
D = 1024
H = 16
HD = 64
CONV = 3072
IF = 2816
TLOC = T // NCORE
CK = 128
NCH = TLOC // CK
DC = D // 128
CCH = CONV // 128
LNW = TLOC + 3
MAGIC = float(1.5 * 2.0 ** 23)
LN8 = float(np.log(8.0))
BF = ml_dtypes.bfloat16


class SyncEng:

    _PASS = {"wait_ge", "sem_inc", "then_inc"}

    def __init__(self, eng, sem, selfsync=True):
        self.eng = eng
        self.sem = sem
        self.selfsync = selfsync
        self.count = 0
        self.par = False
        self.pending = None

    def _flush_pending(self):
        if self.pending is not None:
            inst, amt = self.pending
            inst.then_inc(self.sem, amt)
            self.count += amt
            self.pending = None
            return True
        return False

    def _pre_op(self):
        if not self.selfsync:
            return
        if self._flush_pending():
            if not self.par:
                self.eng.wait_ge(self.sem, self.count)

    def parallel(self):
        import contextlib

        @contextlib.contextmanager
        def cm():
            self._pre_op()
            self.par = True
            try:
                yield
            finally:
                self.par = False
        return cm()

    def close_stage(self, stg_sem, inc):
        assert self.pending is not None, "stage had no ops?"
        inst, _ = self.pending
        inst.then_inc(stg_sem, inc)
        self.pending = None

    def dma_batch(self, pairs):
        self._flush_pending()
        for dst, srcp in pairs:
            self.eng.dma_start(out=dst, in_=srcp).then_inc(self.sem, 16)
            self.count += 16
        self.eng.wait_ge(self.sem, self.count)

    def collective(self, *a, **kw):
        self._flush_pending()
        self.eng.collective_compute(*a, **kw).then_inc(self.sem, 1)
        self.count += 1
        self.eng.wait_ge(self.sem, self.count)

    def collective_async(self, done_sem, *a, **kw):
        self._flush_pending()
        self.eng.collective_compute(*a, **kw).then_inc(done_sem, 1)

    def wait_ge(self, *a, **kw):
        return self.eng.wait_ge(*a, **kw)

    def sem_inc(self, *a, **kw):
        return self.eng.sem_inc(*a, **kw)

    def __getattr__(self, name):
        f = getattr(self.eng, name)
        if not callable(f) or name.startswith("_"):
            return f

        def wrapped(*a, **kw):
            self._pre_op()
            inst = f(*a, **kw)
            amt = 16 if name == "dma_start" else 1
            self.pending = (inst, amt)
            return inst
        return wrapped


class KB:

    def __init__(self):
        self.stages = []
        self.afters = []

    def stage(self, after_=None, **fns):
        self.stages.append(fns)
        self.afters.append(after_)
        return len(self.stages) - 1

    def body(self, stg_sems, esem, eng_name):
        engs = list(stg_sems.keys())
        ncum = {e: [] for e in engs}
        cnt = {e: 0 for e in engs}
        for s in self.stages:
            for e in engs:
                if e in s:
                    cnt[e] += 1
                ncum[e].append(cnt[e])

        def run(eng):
            w = SyncEng(eng, esem,
                        selfsync=(eng_name not in ("tensor", "sync")))
            waited = {e: 0 for e in engs}

            def waitup(e, idx):
                need = ncum[e][idx]
                if need > waited[e]:
                    eng.wait_ge(stg_sems[e], need)
                    waited[e] = need

            for si, s in enumerate(self.stages):
                if eng_name not in s:
                    continue
                a = self.afters[si]
                if isinstance(a, dict):
                    for e, idx in a.items():
                        if idx is not None and idx >= 0:
                            waitup(e, idx)
                else:
                    if a is None:
                        a = si - 1
                    if a >= 0:
                        for e in engs:
                            waitup(e, a)
                res = s[eng_name](w)
                if res == "self":
                    eng.sem_inc(stg_sems[eng_name], 1)
                else:
                    w.close_stage(stg_sems[eng_name], 1)
        return run


def build_program(max_stage=None, dbg=False, nocc=False):
    nc = bass.Bass("TRN2", target_bir_lowering=False,
                   dynamic_dma_scratch_size=4096)

    def din(name, shape, dtype=F32):
        return nc.dram_tensor(name, shape, dtype, kind="ExternalInput").ap()

    x_d = din("x", [TLOC, D])
    xprev_d = din("xprev", [3, D])
    selv_d = din("selv", [128, 8])
    sellt_d = din("sellt", [8, 1])
    invsel_d = din("invsel", [8, 16])
    wqkvt_d = din("wqkvt", [CCH * DC * 128, 128])
    wzt_d = din("wzt", [D, D])
    wbat_d = din("wbat", [D + 1, 32])
    alog_d = din("alog", [1, H])
    convw_d = din("convw", [CONV, 4])
    woutt_d = din("woutt", [D, D])
    wg_d = din("wg", [IF, D])
    wu_d = din("wu", [IF, D])
    wd_d = din("wd", [D, IF])
    tri_d = din("tri", [128, 128])
    ones_d = din("ones", [128, 128])
    identf_d = din("identf", [128, 128])
    identb_d = din("identb", [128, 128], BF16)
    i16_d = din("i16", [16, 16])
    maskb_d = din("maskb", [128, 128], U16)
    bones_d = din("bones", [128, 8 * 16], BF16)
    bcastp_d = din("bcastp", [16, 8 * 128], BF16)
    rep16_d = din("rep16", [16, D])
    su8_d = din("su8", [8, 8])

    y_d = nc.dram_tensor("y", [TLOC, D], F32, kind="ExternalOutput").ap()

    spk = "ExternalOutput" if dbg else "Internal"
    sp_oi = nc.dram_tensor("sp_oi", [TLOC, D], BF16, kind=spk).ap()
    sp_qn = nc.dram_tensor("sp_qn", [TLOC, D], BF16, kind=spk).ap()
    sp_z = nc.dram_tensor("sp_z", [TLOC, D], BF16, kind=spk).ap()
    sp_x2 = nc.dram_tensor("sp_x2", [TLOC, D], F32, kind=spk).ap()
    sp_xqf = nc.dram_tensor("sp_xqf", [TLOC, D], BF16, kind=spk).ap()
    if dbg:
        dbg_s = nc.dram_tensor("dbg_s", [8 * 65, D], F32,
                               kind="ExternalOutput").ap()
        dbg_a = nc.dram_tensor("dbg_a", [128, 8192], BF16,
                               kind="ExternalOutput").ap()
        dbg_b = nc.dram_tensor("dbg_b", [128, 64], F32,
                               kind="ExternalOutput").ap()
        dbg_c = nc.dram_tensor("dbg_c", [16, 128], F32,
                               kind="ExternalOutput").ap()
        dbg_d = nc.dram_tensor("dbg_d", [16, 256], BF16,
                               kind="ExternalOutput").ap()
        dbg_e = nc.dram_tensor("dbg_e", [128, 160], F32,
                               kind="ExternalOutput").ap()
    cc_send = nc.dram_tensor("cc_send", [8 * 65, D], F32, kind="Internal").ap()
    cc_recv = nc.dram_tensor("cc_recv", [8 * 65, D], F32, kind="Internal").ap()

    kb = KB()
    marks = {}
    st = ExitStack()

    def sb(name, shape, dtype=F32):
        return st.enter_context(nc.sbuf_tensor(name, shape, dtype))

    R1 = sb("R1", [128, CCH * LNW], BF16)
    R2 = sb("R2", [128, DC * LNW], BF16)
    R3 = sb("R3", [128, DC * D], BF16)
    R4 = sb("R4", [128, 8192])
    R5 = sb("R5", [128, 10240], BF16)

    def r4(c0, cn, p0=0, pn=128):
        return R4.ap()[p0:p0 + pn, c0:c0 + cn]

    def r5(c0, cn, p0=0, pn=128):
        return R5.ap()[p0:p0 + pn, c0:c0 + cn]

    MIXED = R1.ap()
    LN1 = R2.ap()
    WZT = R3.ap()
    WOUTT = R3.ap()
    wgT_q = R1.ap()[:, 0:8 * IF]
    wuT_q = R1.ap()[:, 8 * IF:16 * IF]

    def wdT_chunk(k):
        if k < 4:
            return R1.ap()[:, 16 * IF + k * D:16 * IF + (k + 1) * D]
        if k < 20:
            return R2.ap()[:, (k - 4) * D:(k - 3) * D]
        return R3.ap()[:, (k - 20) * D:(k - 19) * D]

    TRI = sb("TRI", [128, 128])
    ONES = sb("ONES", [128, 128])
    IDF = sb("IDF", [128, 128])
    IDB = sb("IDB", [128, 128], BF16)
    I16 = sb("I16", [16, 16])
    MASKB = sb("MASKB", [128, 128], U16)
    BONES = sb("BONES", [128, 8 * 16], BF16)
    BCASTP = sb("BCASTP", [16, 8 * 128], BF16)
    CONVW = sb("CONVW", [128, CCH * 4])
    WBAT = sb("WBAT", [128, 9 * 32], BF16)
    ALOG = sb("ALOG", [1, H])
    SELV = sb("SELV", [128, 8])
    SELLT = sb("SELLT", [8, 1])
    INVSEL = sb("INVSEL", [8, 16])
    ELGD = sb("ELGD", [128, NCH * H])
    EXPL = [sb("EXPL0", [128, H]), sb("EXPL1", [128, H])]
    LGDR = [sb("LGDR0", [1, H]), sb("LGDR1", [1, H])]
    NEGEA = sb("NEGEA", [128, H])
    SLOC = [sb("SLOC0", [128, H * HD], BF16), sb("SLOC1", [128, H * HD], BF16)]
    SINIT = sb("SINIT", [128, H * HD], BF16)
    BETA = sb("BETA", [128, H])
    TMP1 = sb("TMP1", [128, H])
    TMP2 = sb("TMP2", [128, H])
    TMP3 = sb("TMP3", [128, H])
    TMP4 = sb("TMP4", [128, H])
    GSB = sb("GSB", [128, H])
    NEGG = sb("NEGG", [128, H])
    GT = sb("GT", [16, 128])
    GTS = sb("GTS", [16, 128])
    E2A = sb("E2A", [128, H])
    E2 = sb("E2", [128, H])
    EXPG = sb("EXPG", [128, H])
    EB128 = sb("EB128", [128, H])
    B128S = sb("B128S", [128, H])
    LQ = sb("LQ", [16, 128])
    LK = sb("LK", [16, 128])
    M2Q = sb("M2Q", [16, 128], BF16)
    M2K = sb("M2K", [16, 128], BF16)
    MSQ = sb("MSQ", [128, H])
    RSTD = sb("RSTD", [128, H])
    STA = sb("STA", [128, 8])
    CCOL = sb("CCOL", [128, 4])
    ONESB = sb("ONESB", [1, 128], BF16)

    SQ = r5(0, 2048)
    SQF = R5.ap().bitcast(F32)[:, 0:1024]
    KH = r5(2048, 1024)
    UTM = r5(3072, 1024)
    QNF = r5(4096, 1024)
    KNF = r5(5120, 1024)
    EM = r5(6144, 2048)
    OIB = r5(8192, 1024)
    ZB = r5(9216, 1024)
    LNT = r5(0, 1024)
    WQSB = r5(1024, 1024)
    CS0 = r5(2048, 2048)
    CS1 = r5(4096, 2048)
    XT = r4(0, 1024)
    WQS = r4(1024, 1024)

    PB = [st.enter_context(nc.psum_tensor(f"PB{i}", [128, 512])) for i in range(8)]

    def pbank(i, p0, pn, c0, cn, dtype=F32):
        v = PB[i].ap()
        if dtype != F32:
            m = 2 if dtype == BF16 else 1
            return v.bitcast(dtype)[p0:p0 + pn, c0 * m:(c0 + cn) * m]
        return v[p0:p0 + pn, c0:c0 + cn]

    stg = st.enter_context(nc.semaphore("stg"))
    ccsem = st.enter_context(nc.semaphore("ccsem"))
    esems = {n: st.enter_context(nc.semaphore("es_" + n))
             for n in ("sync", "scalar", "vector", "tensor", "gpsimd")}

    def dseq(eng, pairs):
        eng.dma_batch(pairs)
        return "self"

    def p_zero(eng):
        return eng.memset(WQS[:, 256:288], 0.0)
    kb.stage(vector=p_zero)

    def p_loads(eng):
        pairs = [
            (TRI.ap(), tri_d), (ONES.ap(), ones_d), (IDF.ap(), identf_d),
            (IDB.ap(), identb_d), (I16.ap(), i16_d), (MASKB.ap(), maskb_d),
            (BONES.ap(), bones_d), (BCASTP.ap(), bcastp_d),
            (ALOG.ap(), alog_d), (SELV.ap(), selv_d), (SELLT.ap(), sellt_d),
            (INVSEL.ap(), invsel_d),
        ]
        for cc in range(CCH):
            pairs.append((CONVW.ap()[:, cc * 4:(cc + 1) * 4],
                          convw_d[cc * 128:(cc + 1) * 128, :]))
        for k in range(8):
            pairs.append((WQS[:, k * 32:(k + 1) * 32],
                          wbat_d[k * 128:(k + 1) * 128, :]))
        pairs.append((WQS[0:1, 256:288], wbat_d[1024:1025, :]))
        return dseq(eng, pairs)
    kb.stage(sync=p_loads)

    def p_init(eng):
        eng.memset(ONESB.ap(), 1.0)
        eng.memset(CCOL.ap()[:, 0:1], 1e-5)
        eng.memset(CCOL.ap()[:, 1:2], 1e-6)
        eng.memset(CCOL.ap()[:, 2:3], -LN8)
        eng.memset(CCOL.ap()[:, 3:4], 1.0)
        eng.memset(EXPL[0].ap(), 1.0)
        eng.memset(LGDR[0].ap(), 0.0)
        eng.memset(SLOC[0].ap(), 0.0)
        return eng.memset(SLOC[1].ap(), 0.0)
    kb.stage(vector=p_init)

    def p_ea1(eng):
        return eng.activation(out=TMP1.ap()[0:1, :], in_=ALOG.ap(), func=AF.Exp)
    kb.stage(scalar=p_ea1)

    def p_ea2(eng):
        return eng.tensor_scalar(out=TMP2.ap()[0:1, :], in0=TMP1.ap()[0:1, :],
                                 scalar1=-1.0, scalar2=None, op0=ALU.mult)
    kb.stage(vector=p_ea2)

    def p_ea3(eng):
        return eng.matmul(pbank(0, 0, 128, 0, 16), ONES.ap()[0:1, :],
                          TMP2.ap()[0:1, :], start=True, stop=True)
    kb.stage(tensor=p_ea3)

    def p_ea4(eng):
        return eng.tensor_copy(out=NEGEA.ap(), in_=pbank(0, 0, 128, 0, 16))
    kb.stage(vector=p_ea4)

    def p_wba(eng):
        return eng.copy(out=WBAT.ap(), in_=WQS[:, 0:288])
    kb.stage(scalar=p_wba)

    for dcz in range(DC):
        def p_wzl(eng, dcz=dcz):
            return dseq(eng, [(WQS, wzt_d[dcz * 128:(dcz + 1) * 128, :])])
        def p_wzc(eng, dcz=dcz):
            return eng.copy(out=WZT[:, dcz * D:(dcz + 1) * D], in_=WQS)
        kb.stage(sync=p_wzl)
        kb.stage(scalar=p_wzc)

    marks['prologue'] = len(kb.stages)
    def ln_tile(rows, col0):
        def s1(eng):
            with eng.parallel():
                eng.activation(out=r4(2048, 1024, 0, rows), in_=XT[0:rows, :],
                               func=AF.Copy, accum_out=STA.ap()[0:rows, 0:1])
                last = eng.activation(out=r4(3072, 1024, 0, rows),
                                      in_=XT[0:rows, :], func=AF.Square,
                                      accum_out=STA.ap()[0:rows, 1:2])
            return last
        kb.stage(scalar=s1)

        def s2(eng):
            eng.tensor_scalar(out=STA.ap()[0:rows, 2:3],
                              in0=STA.ap()[0:rows, 0:1], scalar1=1.0 / D,
                              scalar2=None, op0=ALU.mult)
            eng.tensor_scalar(out=STA.ap()[0:rows, 3:4],
                              in0=STA.ap()[0:rows, 1:2], scalar1=1.0 / D,
                              scalar2=None, op0=ALU.mult)
            eng.tensor_tensor(out=STA.ap()[0:rows, 4:5],
                              in0=STA.ap()[0:rows, 2:3],
                              in1=STA.ap()[0:rows, 2:3], op=ALU.mult)
            return eng.tensor_tensor(out=STA.ap()[0:rows, 5:6],
                                     in0=STA.ap()[0:rows, 3:4],
                                     in1=STA.ap()[0:rows, 4:5],
                                     op=ALU.subtract)
        kb.stage(vector=s2)

        def s3(eng):
            eng.activation(out=STA.ap()[0:rows, 6:7],
                           in_=STA.ap()[0:rows, 5:6], func=AF.Ln,
                           bias=CCOL.ap()[0:rows, 0:1])
            return eng.activation(out=STA.ap()[0:rows, 6:7],
                                  in_=STA.ap()[0:rows, 6:7], func=AF.Exp,
                                  scale=-0.5)
        kb.stage(scalar=s3)

        def s4(eng):
            return eng.scalar_tensor_tensor(
                out=STA.ap()[0:rows, 7:8], in0=STA.ap()[0:rows, 2:3],
                scalar=-1.0, in1=STA.ap()[0:rows, 6:7],
                op0=ALU.mult, op1=ALU.mult)
        kb.stage(vector=s4)

        def s5(eng):
            return eng.tensor_scalar(out=LNT[0:rows, :], in0=XT[0:rows, :],
                                     scalar1=STA.ap()[0:rows, 6:7],
                                     scalar2=STA.ap()[0:rows, 7:8],
                                     op0=ALU.mult, op1=ALU.add)
        kb.stage(vector=s5)

        def s6(eng):
            last = None
            for dc in range(DC):
                last = eng.transpose(pbank(dc, 0, 128, 0, 64, BF16)[:, 0:rows],
                                     LNT[0:rows, dc * 128:(dc + 1) * 128],
                                     IDB.ap()[0:rows, 0:rows])
            return last
        kb.stage(tensor=s6)

        def s7(eng):
            last = None
            with eng.parallel():
                for dc in range(DC):
                    last = eng.tensor_copy(
                        out=LN1[:, dc * LNW + col0:dc * LNW + col0 + rows],
                        in_=pbank(dc, 0, 128, 0, 64, BF16)[:, 0:rows])
            return last
        kb.stage(vector=s7)

    def l_xp(eng):
        return dseq(eng, [(XT[0:3, :], xprev_d)])
    kb.stage(sync=l_xp)
    ln_tile(3, 0)
    for tt in range(NCH):
        def l_x(eng, tt=tt):
            return dseq(eng, [(XT, x_d[tt * 128:(tt + 1) * 128, :])])
        kb.stage(sync=l_x)
        ln_tile(128, 3 + tt * 128)

    marks['ln'] = len(kb.stages)
    npieces = [(0, 512), (512, 512), (1024, 512), (1536, 512), (2048, 3)]
    for cc in range(CCH):
        def q_l(eng, cc=cc):
            return dseq(eng, [(WQS.rearrange("p (dc c) -> p dc c", c=128),
                               wqkvt_d[cc * DC * 128:(cc + 1) * DC * 128, :]
                               .rearrange("(dc p) c -> p dc c", p=128))])
        kb.stage(sync=q_l)

        def q_c(eng):
            return eng.copy(out=WQSB, in_=WQS)
        kb.stage(scalar=q_c)

        def q_mm(eng, cc=cc):
            last = None
            for dc in range(DC):
                for pi, (c0, cn) in enumerate(npieces):
                    bank = pi if pi < 4 else 4
                    last = eng.matmul(
                        pbank(bank, 0, 128, 0, cn),
                        WQSB[:, dc * 128:(dc + 1) * 128],
                        LN1[:, dc * LNW + c0:dc * LNW + c0 + cn],
                        start=(dc == 0), stop=(dc == DC - 1))
            return last
        kb.stage(tensor=q_mm)

        def q_ev(eng, cc=cc):
            last = None
            with eng.parallel():
                for pi, (c0, cn) in enumerate(npieces):
                    bank = pi if pi < 4 else 4
                    last = eng.tensor_copy(
                        out=MIXED[:, cc * LNW + c0:cc * LNW + c0 + cn],
                        in_=pbank(bank, 0, 128, 0, cn))
            return last
        kb.stage(vector=q_ev)

        def q_cv2(eng, cc=cc):
            base = cc * LNW
            eng.tensor_scalar(out=CS0, in0=MIXED[:, base:base + 2048],
                              scalar1=CONVW.ap()[:, cc * 4:cc * 4 + 1],
                              scalar2=None, op0=ALU.mult)
            eng.scalar_tensor_tensor(
                out=CS1, in0=MIXED[:, base + 1:base + 2049],
                scalar=CONVW.ap()[:, cc * 4 + 1:cc * 4 + 2],
                in1=CS0, op0=ALU.mult, op1=ALU.add)
            eng.scalar_tensor_tensor(
                out=CS0, in0=MIXED[:, base + 2:base + 2050],
                scalar=CONVW.ap()[:, cc * 4 + 2:cc * 4 + 3],
                in1=CS1, op0=ALU.mult, op1=ALU.add)
            return eng.scalar_tensor_tensor(
                out=CS1, in0=MIXED[:, base + 3:base + 2051],
                scalar=CONVW.ap()[:, cc * 4 + 3:cc * 4 + 4],
                in1=CS0, op0=ALU.mult, op1=ALU.add)
        kb.stage(vector=q_cv2)

        def q_cv3(eng, cc=cc):
            return eng.activation(out=MIXED[:, cc * LNW:cc * LNW + 2048],
                                  in_=CS1, func=AF.Silu)
        kb.stage(scalar=q_cv3)

    marks['qkv'] = len(kb.stages)
    BETA2 = [BETA, sb("BETAb", [128, H])]
    GSB2 = [GSB, sb("GSBb", [128, H])]
    NEGG2 = [NEGG, sb("NEGGb", [128, H])]
    GT2 = [GT, sb("GTb", [16, 128])]
    E22 = [E2, sb("E2b", [128, H])]
    EXPG2 = [EXPG, sb("EXPGb", [128, H])]
    EB1282 = [EB128, sb("EB128b", [128, H])]
    M2Q2 = [M2Q, sb("M2Qb", [16, 128], BF16)]
    M2K2 = [M2K, sb("M2Kb", [16, 128], BF16)]
    T1F = r4(0, 1024)

    prev_head_last = None
    prev_tail17 = [None, None]
    for c in range(NCH):
        pc = c % 2
        ct = c * 128
        lt = 3 + c * 128
        BETAc, GSBc, NEGGc = BETA2[pc].ap(), GSB2[pc].ap(), NEGG2[pc].ap()
        GTc, E2c, EXPGc = GT2[pc].ap(), E22[pc].ap(), EXPG2[pc].ap()
        EB128c = EB1282[pc].ap()
        M2Qc, M2Kc = M2Q2[pc].ap(), M2K2[pc].ap()

        def a_s1(eng, lt=lt):
            last = None
            for dc in range(DC):
                last = eng.matmul(
                    pbank(6, 0, 128, 0, 32),
                    LN1[:, dc * LNW + lt:dc * LNW + lt + 128],
                    WBAT.ap()[:, dc * 32:(dc + 1) * 32],
                    start=(dc == 0), stop=False)
            return eng.matmul(pbank(6, 0, 128, 0, 32), ONESB.ap(),
                              WBAT.ap()[0:1, 256:288], start=False, stop=True)
        h_first = kb.stage(tensor=a_s1, after_=prev_head_last)

        def a_s2(eng, BETAc=BETAc):
            eng.activation(out=BETAc, in_=pbank(6, 0, 128, 0, 16),
                           func=AF.Sigmoid)
            eng.activation(out=TMP1.ap(), in_=pbank(6, 0, 128, 16, 16),
                           func=AF.Abs)
            eng.activation(out=TMP2.ap(), in_=TMP1.ap(), func=AF.Exp,
                           scale=-1.0)
            eng.activation(out=TMP1.ap(), in_=TMP2.ap(), func=AF.Ln,
                           bias=CCOL.ap()[:, 3:4])
            return eng.activation(out=TMP3.ap(), in_=pbank(6, 0, 128, 16, 16),
                                  func=AF.Relu)
        kb.stage(scalar=a_s2)

        def a_s3(eng):
            eng.tensor_tensor(out=TMP4.ap(), in0=TMP3.ap(), in1=TMP1.ap(),
                              op=ALU.add)
            return eng.tensor_tensor(out=TMP2.ap(), in0=TMP4.ap(),
                                     in1=NEGEA.ap(), op=ALU.mult)
        kb.stage(vector=a_s3)

        def a_s4(eng):
            eng.transpose(pbank(7, 0, 16, 0, 128), TMP2.ap(), IDF.ap())
            eng.matmul(pbank(6, 0, 128, 32, 16), TRI.ap(), TMP2.ap(),
                       start=True, stop=True)
            return eng.matmul(pbank(6, 0, 128, 48, 16), ONES.ap(), TMP2.ap(),
                              start=True, stop=True)
        kb.stage(tensor=a_s4)

        def a_s5v(eng, pc=pc, GSBc=GSBc, NEGGc=NEGGc):
            eng.tensor_copy(out=GSBc, in_=pbank(6, 0, 128, 32, 16))
            with eng.parallel():
                eng.tensor_copy(out=GTS.ap(), in_=pbank(7, 0, 16, 0, 128))
                eng.tensor_copy(out=B128S.ap(), in_=pbank(6, 0, 128, 48, 16))
                eng.tensor_scalar(out=NEGGc, in0=pbank(6, 0, 128, 32, 16),
                                  scalar1=-1.0, scalar2=None, op0=ALU.mult)
                eng.tensor_tensor(out=E2A.ap(), in0=pbank(6, 0, 128, 48, 16),
                                  in1=GSBc, op=ALU.subtract)
                last = eng.tensor_tensor(out=LGDR[1 - pc].ap(),
                                         in0=LGDR[pc].ap(),
                                         in1=pbank(6, 0, 1, 48, 16),
                                         op=ALU.add)
            return last

        def a_s5s(eng, ct=ct):
            last = None
            with eng.parallel():
                for qc in range(16):
                    last = eng.activation(
                        out=SQ[:, qc * 128:(qc + 1) * 128],
                        in_=MIXED[:, qc * LNW + ct:qc * LNW + ct + 128],
                        func=AF.Square)
            return last
        kb.stage(vector=a_s5v, scalar=a_s5s)

        def a_s6v(eng, GTc=GTc):
            return eng.tensor_tensor_scan(out=GTc, data0=GTS.ap(),
                                          data1=GTS.ap(), initial=0.0,
                                          op0=ALU.add, op1=ALU.bypass)

        def a_s6s(eng, GSBc=GSBc, EXPGc=EXPGc, EB128c=EB128c, E2c=E2c):
            with eng.parallel():
                eng.activation(out=EXPGc, in_=GSBc, func=AF.Exp)
                eng.activation(out=EB128c, in_=B128S.ap(), func=AF.Exp)
                last = eng.activation(out=E2c, in_=E2A.ap(), func=AF.Exp)
            return last

        def a_s6t(eng):
            last = None
            for qc in range(8):
                last = eng.matmul(pbank(7, 0, 16, 128, 128),
                                  BONES.ap()[:, qc * 16:(qc + 1) * 16],
                                  SQ[:, qc * 128:(qc + 1) * 128],
                                  start=(qc == 0), stop=(qc == 7))
            for qc in range(8):
                last = eng.matmul(pbank(7, 0, 16, 256, 128),
                                  BONES.ap()[:, qc * 16:(qc + 1) * 16],
                                  SQ[:, (8 + qc) * 128:(9 + qc) * 128],
                                  start=(qc == 0), stop=(qc == 7))
            return last
        kb.stage(vector=a_s6v, scalar=a_s6s, tensor=a_s6t)

        def a_s7s(eng, M2Qc=M2Qc, M2Kc=M2Kc):
            with eng.parallel():
                eng.activation(out=LQ.ap(), in_=pbank(7, 0, 16, 128, 128),
                               func=AF.Ln, bias=CCOL.ap()[0:16, 1:2])
                eng.activation(out=LK.ap(), in_=pbank(7, 0, 16, 256, 128),
                               func=AF.Ln, bias=CCOL.ap()[0:16, 1:2])
            with eng.parallel():
                eng.activation(out=M2Qc, in_=LQ.ap(), func=AF.Exp,
                               scale=-0.5, bias=CCOL.ap()[0:16, 2:3])
                last = eng.activation(out=M2Kc, in_=LK.ap(), func=AF.Exp,
                                      scale=-0.5)
            return last

        def a_s7v(eng, c=c, pc=pc, EXPGc=EXPGc, EB128c=EB128c):
            with eng.parallel():
                eng.tensor_tensor(out=ELGD.ap()[:, c * H:(c + 1) * H],
                                  in0=EXPGc, in1=EXPL[pc].ap(), op=ALU.mult)
                last = eng.tensor_tensor(out=EXPL[1 - pc].ap(),
                                         in0=EXPL[pc].ap(),
                                         in1=EB128c, op=ALU.mult)
            return last
        h_last = kb.stage(scalar=a_s7s, vector=a_s7v)
        prev_head_last = h_last

        def a_s8t(eng, M2Qc=M2Qc, M2Kc=M2Kc):
            last = None
            for qc in range(8):
                last = eng.matmul(pbank(qc % 4, 0, 128, 128 * (qc // 4), 128),
                                  BCASTP.ap()[:, qc * 128:(qc + 1) * 128],
                                  M2Qc, start=True, stop=True)
            for qc in range(8):
                last = eng.matmul(
                    pbank(qc % 4, 0, 128, 256 + 128 * (qc // 4), 128),
                    BCASTP.ap()[:, qc * 128:(qc + 1) * 128],
                    M2Kc, start=True, stop=True)
            return last
        kb.stage(tensor=a_s8t)

        def a_s9v(eng, ct=ct):
            last = None
            with eng.parallel():
             for qc in range(8):
                last = eng.tensor_tensor(
                    out=QNF[:, qc * 128:(qc + 1) * 128],
                    in0=MIXED[:, qc * LNW + ct:qc * LNW + ct + 128],
                    in1=pbank(qc % 4, 0, 128, 128 * (qc // 4), 128),
                    op=ALU.mult)
             for qc in range(8):
                last = eng.tensor_tensor(
                    out=KNF[:, qc * 128:(qc + 1) * 128],
                    in0=MIXED[:, (8 + qc) * LNW + ct:(8 + qc) * LNW + ct + 128],
                    in1=pbank(qc % 4, 0, 128, 256 + 128 * (qc // 4), 128),
                    op=ALU.mult)
            return last
        kb.stage(vector=a_s9v)

        def a_s10t(eng, ct=ct, GTc=GTc):
            last = None
            for h in range(H):
                last = eng.matmul(
                    pbank(h % 4, 0, 128, 128 * (h // 4), 128),
                    I16.ap()[:, h:h + 1].broadcast_to([16, 128]),
                    GTc, start=True, stop=True)
            for qc in range(8):
                last = eng.transpose(pbank(4, 0, 128, qc * 64, 64, BF16),
                                     KNF[:, qc * 128:(qc + 1) * 128],
                                     IDB.ap())
            for qc in range(8):
                last = eng.transpose(
                    pbank(5, 0, 128, qc * 64, 64, BF16),
                    MIXED[:, (16 + qc) * LNW + ct:(16 + qc) * LNW + ct + 128],
                    IDB.ap())
            return last
        kb.stage(tensor=a_s10t)

        def a_s10v(eng, NEGGc=NEGGc):
            last = None
            with eng.parallel():
             for h in range(H):
                last = eng.tensor_scalar(
                    out=r4(2048, 2048)[:, h * 128:(h + 1) * 128],
                    in0=pbank(h % 4, 0, 128, 128 * (h // 4), 128),
                    scalar1=NEGGc[:, h:h + 1], scalar2=0.0,
                    op0=ALU.add, op1=ALU.min)
            return last
        kb.stage(vector=a_s10v)

        def a_s11s(eng, E2c=E2c, BETAc=BETAc):
            last = None
            with eng.parallel():
             for h in range(H):
                last = eng.activation(
                    out=EM[:, h * 128:(h + 1) * 128],
                    in_=r4(2048, 2048)[:, h * 128:(h + 1) * 128],
                    func=AF.Exp)
             for qc in range(8):
                for hh in range(2):
                    h = 2 * qc + hh
                    last = eng.activation(
                        out=KH[:, h * 64:(h + 1) * 64],
                        in_=pbank(4, 0, 128, qc * 64, 64, BF16)
                        [:, hh * 64:(hh + 1) * 64],
                        func=AF.Copy, scale=E2c[:, h:h + 1])
                    last = eng.activation(
                        out=UTM[:, h * 64:(h + 1) * 64],
                        in_=pbank(5, 0, 128, qc * 64, 64, BF16)
                        [:, hh * 64:(hh + 1) * 64],
                        func=AF.Copy, scale=BETAc[:, h:h + 1])
            return last
        kb.stage(scalar=a_s11s)

        def a_s12v(eng):
            last = None
            with eng.parallel():
             for h in range(H):
                v = EM[:, h * 128:(h + 1) * 128].bitcast(U16)
                last = eng.tensor_tensor(out=v, in0=v, in1=MASKB.ap(),
                                         op=ALU.bitwise_and)
            return last
        kb.stage(vector=a_s12v)

        def a_s13t(eng, lt=lt):
            last = None
            for h in range(H):
                qc, hh = h // 2, h % 2
                last = eng.matmul(
                    pbank(h % 4, 0, 128, 128 * (h // 4), 128),
                    KNF[hh * 64:(hh + 1) * 64, qc * 128:(qc + 1) * 128],
                    QNF[hh * 64:(hh + 1) * 64, qc * 128:(qc + 1) * 128],
                    start=True, stop=True)
            for dc in range(DC):
                for pi in range(2):
                    last = eng.matmul(
                        pbank(4 + pi, 0, 128, 0, 512),
                        LN1[:, dc * LNW + lt:dc * LNW + lt + 128],
                        WZT[:, dc * D + pi * 512:dc * D + (pi + 1) * 512],
                        start=(dc == 0), stop=(dc == DC - 1))
            return last
        kb.stage(tensor=a_s13t)

        def a_s14v(eng):
            last = None
            with eng.parallel():
             for h in range(H):
                last = eng.tensor_tensor(
                    out=EM[:, h * 128:(h + 1) * 128],
                    in0=pbank(h % 4, 0, 128, 128 * (h // 4), 128),
                    in1=EM[:, h * 128:(h + 1) * 128], op=ALU.mult)
             for pi in range(2):
                last = eng.tensor_copy(out=ZB[:, pi * 512:(pi + 1) * 512],
                                       in_=pbank(4 + pi, 0, 128, 0, 512))
            return last
        kb.stage(vector=a_s14v)

        def a_s15t(eng, pc=pc):
            last = None
            for h in range(H):
                last = eng.matmul(pbank(h % 2, 0, 128, (h // 2) * 64, 64),
                                  EM[:, h * 128:(h + 1) * 128],
                                  UTM[:, h * 64:(h + 1) * 64],
                                  start=True, stop=True)
            for h in range(H):
                qc, hh = h // 2, h % 2
                last = eng.matmul(
                    pbank(2 + h % 2, 0, 128, (h // 2) * 64, 64),
                    QNF[hh * 64:(hh + 1) * 64, qc * 128:(qc + 1) * 128],
                    SLOC[pc].ap()[hh * 64:(hh + 1) * 64,
                                  h * 64:(h + 1) * 64],
                    start=True, stop=True)
            for h in range(H):
                last = eng.matmul(pbank(4 + h % 2, 0, 64, (h // 2) * 64, 64),
                                  KH[:, h * 64:(h + 1) * 64],
                                  UTM[:, h * 64:(h + 1) * 64],
                                  start=True, stop=True)
            for h in range(H):
                last = eng.matmul(pbank(4 + h % 2, 64, 64, (h // 2) * 64, 64),
                                  KH[:, h * 64:(h + 1) * 64],
                                  UTM[:, h * 64:(h + 1) * 64],
                                  start=True, stop=True,
                                  tile_position=(0, 64))
            return last
        kb.stage(tensor=a_s15t)

        def a_s16s(eng, EXPGc=EXPGc):
            last = None
            with eng.parallel():
             for h in range(H):
                last = eng.activation(
                    out=T1F[:, h * 64:(h + 1) * 64],
                    in_=pbank(2 + h % 2, 0, 128, (h // 2) * 64, 64),
                    func=AF.Copy, scale=EXPGc[:, h:h + 1])
            return last
        kb.stage(scalar=a_s16s)

        def a_s17v(eng, pc=pc, EB128c=EB128c):
            last = None
            with eng.parallel():
             for h in range(H):
                last = eng.tensor_tensor(
                    out=OIB[:, h * 64:(h + 1) * 64],
                    in0=T1F[:, h * 64:(h + 1) * 64],
                    in1=pbank(h % 2, 0, 128, (h // 2) * 64, 64), op=ALU.add)
             for h in range(H):
                for b0 in (0, 64):
                    last = eng.scalar_tensor_tensor(
                        out=SLOC[1 - pc].ap()[b0:b0 + 64,
                                              h * 64:(h + 1) * 64],
                        in0=SLOC[pc].ap()[b0:b0 + 64, h * 64:(h + 1) * 64],
                        scalar=EB128c[b0:b0 + 64, h:h + 1],
                        in1=pbank(4 + h % 2, b0, 64, (h // 2) * 64, 64),
                        op0=ALU.mult, op1=ALU.add)
            return last
        t17 = kb.stage(vector=a_s17v)
        prev_tail17[pc] = t17

        def a_s18(eng, c=c):
            return dseq(eng, [
                (sp_oi[c * 128:(c + 1) * 128, :], OIB),
                (sp_qn[c * 128:(c + 1) * 128, :], QNF),
                (sp_z[c * 128:(c + 1) * 128, :], ZB),
            ])
        kb.stage(sync=a_s18)

    marks['attn'] = len(kb.stages)
    def quant_chain(eng, src, F, out_bf, sa, sb_, sc, bm, bm2):
        nb = F // 16
        eng.tensor_reduce(out=bm, in_=src.rearrange("p (b s) -> p b s", s=16),
                          axis=AX.X, op=ALU.max, apply_absolute_value=True)
        eng.tensor_scalar(out=bm2, in0=bm, scalar1=1.0 / 6.0, scalar2=1e-12,
                          op0=ALU.mult, op1=ALU.max)
        eng.reciprocal(out=bm, in_=bm2)
        eng.tensor_tensor(out=sa.rearrange("p (b s) -> p b s", s=16),
                          in0=src.rearrange("p (b s) -> p b s", s=16),
                          in1=bm.unsqueeze(2).broadcast_to([128, nb, 16]),
                          op=ALU.mult)
        with eng.parallel():
            eng.tensor_scalar(out=sb_, in0=sa, scalar1=2.0, scalar2=-2.0,
                              op0=ALU.min, op1=ALU.max)
            eng.tensor_scalar(out=sc, in0=sa, scalar1=4.0, scalar2=-4.0,
                              op0=ALU.min, op1=ALU.max)
        eng.scalar_tensor_tensor(out=sb_, in0=sa, scalar=0.5, in1=sb_,
                                 op0=ALU.mult, op1=ALU.add)
        eng.scalar_tensor_tensor(out=sc, in0=sc, scalar=0.5, in1=sb_,
                                 op0=ALU.mult, op1=ALU.add)
        eng.tensor_scalar(out=sa, in0=sc, scalar1=MAGIC, scalar2=MAGIC,
                          op0=ALU.add, op1=ALU.subtract)
        with eng.parallel():
            eng.tensor_scalar(out=sb_, in0=sa, scalar1=4.0, scalar2=-4.0,
                              op0=ALU.min, op1=ALU.max)
            eng.tensor_scalar(out=sc, in0=sa, scalar1=6.0, scalar2=-6.0,
                              op0=ALU.min, op1=ALU.max)
        eng.scalar_tensor_tensor(out=sa, in0=sa, scalar=2.0, in1=sc,
                                 op0=ALU.mult, op1=ALU.subtract)
        eng.scalar_tensor_tensor(out=sb_, in0=sb_, scalar=-0.5, in1=sa,
                                 op0=ALU.mult, op1=ALU.add)
        return eng.tensor_tensor(
            out=out_bf.rearrange("p (b s) -> p b s", s=16),
            in0=sb_.rearrange("p (b s) -> p b s", s=16),
            in1=bm2.unsqueeze(2).broadcast_to([128, nb, 16]), op=ALU.mult)

    def emit_fq(part):
        FSRC = r4(0, 1408)
        FSA = r4(1408, 1408)
        FSB = r4(2816, 1408)
        FSC = r4(4224, 1408)
        FBM = r4(5632, 88)
        FBM2 = r4(5720, 88)
        FOUT = r5(0, 1408)

        def fq_seg(src_dram, rows0, cols0, F, dst_fn, sub0):
            def f_l(eng):
                return dseq(eng, [(FSRC[:, 0:F],
                                   src_dram[rows0:rows0 + 128, cols0:cols0 + F])])
            kb.stage(sync=f_l)

            def f_q(eng):
                return quant_chain(eng, FSRC[:, 0:F], F, FOUT[:, 0:F],
                                   FSA[:, 0:F], FSB[:, 0:F], FSC[:, 0:F],
                                   FBM[:, 0:F // 16], FBM2[:, 0:F // 16])
            kb.stage(vector=f_q)

            nsub = F // 128
            for s0 in range(0, nsub, 8):
                def f_tr(eng, s0=s0, nsub=nsub):
                    last = None
                    for k in range(s0, min(s0 + 8, nsub)):
                        last = eng.transpose(pbank(k % 8, 0, 128, 0, 64, BF16),
                                             FOUT[:, k * 128:(k + 1) * 128],
                                             IDB.ap())
                    return last
                kb.stage(tensor=f_tr)

                def f_ev(eng, s0=s0, nsub=nsub, sub0=sub0):
                    last = None
                    with eng.parallel():
                     for k in range(s0, min(s0 + 8, nsub)):
                        last = eng.tensor_copy(
                            out=dst_fn(sub0 + k),
                            in_=pbank(k % 8, 0, 128, 0, 64, BF16))
                    return last
                kb.stage(vector=f_ev)

        if part == "main":
            for wsrc, wdst in ((wg_d, wgT_q), (wu_d, wuT_q)):
                for nt in range(IF // 128):
                    def mk(wdst=wdst, nt=nt):
                        return lambda k: wdst[:, k * IF + nt * 128:
                                              k * IF + nt * 128 + 128]
                    fq_seg(wsrc, nt * 128, 0, D, mk(), 0)
            for dt in range(DC):
                def mkd(dt=dt):
                    return lambda k: wdT_chunk(k)[:, dt * 128:dt * 128 + 128]
                fq_seg(wd_d, dt * 128, 0, 1408, mkd(), 0)
        else:
            for dt in range(DC):
                def mkd(dt=dt):
                    return lambda k: wdT_chunk(k)[:, dt * 128:dt * 128 + 128]
                fq_seg(wd_d, dt * 128, 1408, 1408, mkd(), 11)


    SSTG = r4(0, 1024, 0, 65)
    MSKT = r4(1024, 1024, 0, 65)
    RFIN = r4(2048, 1024, 0, 64)
    REP = r4(3072, 1024, 0, 16)
    ACC = r4(4096, 1024, 0, 64)
    WSTG = r4(5120, 1024)
    DTOTS = r4(6144, 16, 0, 8)
    DPT = r4(6160, 16, 0, 8)
    LNDP = r4(6176, 16, 0, 8)
    COEF = r4(6192, 16, 0, 8)
    COEFM = r4(6208, 16, 0, 8)
    COEFT = r4(6224, 8, 0, 16)
    SU8S = r4(6232, 8, 0, 8)

    def x_s1(eng):
        eng.memset(SSTG[64:65, :], 0.0)
        return eng.tensor_copy(out=SSTG[0:64, :], in_=SLOC[0].ap()[0:64, :])
    kb.stage(vector=x_s1)

    def x_s2(eng):
        return eng.activation(out=TMP1.ap()[0:1, 0:16], in_=LGDR[0].ap(),
                              func=AF.Exp)
    kb.stage(scalar=x_s2)

    def x_s2b(eng):
        eng.dma_batch([(SSTG[64:65, 0:16], TMP1.ap()[0:1, 0:16])])
        return "self"
    kb.stage(gpsimd=x_s2b)

    for j in range(8):
        def x_mask(eng, j=j):
            return eng.tensor_scalar(out=MSKT, in0=SSTG,
                                     scalar1=SELV.ap()[0:65, j:j + 1],
                                     scalar2=None, op0=ALU.mult)
        kb.stage(vector=x_mask)

        def x_send(eng, j=j):
            return dseq(eng, [(cc_send[j * 65:(j + 1) * 65, :], MSKT)])
        kb.stage(sync=x_send)

    def x_cc(eng):
        if nocc:
            eng.dma_batch([(cc_recv[j * 65:(j + 1) * 65, :], MSKT)
                           for j in range(8)])
            eng.sem_inc(ccsem, 1)
        else:
            eng.collective_async(ccsem, "AllReduce", ALU.add,
                                 replica_groups=[[0, 1, 2, 3, 4, 5, 6, 7]],
                                 ins=[cc_send.opt()], outs=[cc_recv.opt()])
        return "self"
    kb.stage(gpsimd=x_cc)

    emit_fq("main")

    def x_rl(eng):
        eng.wait_ge(ccsem, 1)
        return dseq(eng, [
            (DTOTS, cc_recv.rearrange("(j r) c -> j r c", r=65)[:, 64, 0:16]),
            (REP, rep16_d),
            (SU8S, su8_d),
        ])
    kb.stage(sync=x_rl)

    def x_accz(eng):
        return eng.memset(ACC, 0.0)
    kb.stage(vector=x_accz)

    def x_dp(eng):
        return eng.scalar_tensor_tensor(out=DPT, in0=DTOTS,
                                        scalar=SELLT.ap(), in1=INVSEL.ap(),
                                        op0=ALU.mult, op1=ALU.add)
    kb.stage(vector=x_dp)

    def x_lnc(eng):
        return eng.tensor_scalar(out=DPT, in0=DPT, scalar1=1e-38,
                                 scalar2=None, op0=ALU.max)
    kb.stage(vector=x_lnc)

    def x_ln(eng):
        return eng.activation(out=LNDP, in_=DPT, func=AF.Ln)
    kb.stage(scalar=x_ln)

    def x_mm(eng):
        return eng.matmul(pbank(0, 0, 8, 0, 16), SU8S, LNDP,
                          start=True, stop=True)
    kb.stage(tensor=x_mm)

    def x_coef(eng):
        return eng.activation(out=COEF, in_=pbank(0, 0, 8, 0, 16),
                              func=AF.Exp)
    kb.stage(scalar=x_coef)

    def x_coefm(eng):
        return eng.tensor_scalar(out=COEFM, in0=COEF, scalar1=SELLT.ap(),
                                 scalar2=None, op0=ALU.mult)
    kb.stage(vector=x_coefm)

    def x_ct(eng):
        return eng.transpose(pbank(1, 0, 16, 0, 8), COEFM, IDF.ap()[0:8, 0:8])
    kb.stage(tensor=x_ct)

    def x_cte(eng):
        return eng.tensor_copy(out=COEFT, in_=pbank(1, 0, 16, 0, 8))
    kb.stage(vector=x_cte)

    for j in range(8):
        def x_rf(eng, j=j):
            return dseq(eng, [(RFIN, cc_recv[j * 65:j * 65 + 64, :])])
        kb.stage(sync=x_rf)

        def x_bc(eng, j=j):
            eng.matmul(pbank(2, 0, 64, 0, 512),
                       COEFT[:, j:j + 1].broadcast_to([16, 64]),
                       REP[:, 0:512], start=True, stop=True)
            return eng.matmul(pbank(3, 0, 64, 0, 512),
                              COEFT[:, j:j + 1].broadcast_to([16, 64]),
                              REP[:, 512:1024], start=True, stop=True)
        kb.stage(tensor=x_bc)

        def x_acc(eng, j=j):
            eng.tensor_tensor(out=MSKT[0:64, 0:512],
                              in0=pbank(2, 0, 64, 0, 512),
                              in1=RFIN[:, 0:512], op=ALU.mult)
            eng.tensor_tensor(out=MSKT[0:64, 512:1024],
                              in0=pbank(3, 0, 64, 0, 512),
                              in1=RFIN[:, 512:1024], op=ALU.mult)
            return eng.tensor_tensor(out=ACC, in0=ACC, in1=MSKT[0:64, :],
                                     op=ALU.add)
        kb.stage(vector=x_acc)

    def x_cast(eng):
        return eng.tensor_copy(out=SINIT.ap()[0:64, :], in_=ACC)
    kb.stage(vector=x_cast)

    if dbg:
        def x_dbg(eng):
            pairs = []
            for jj in range(8):
                pairs.append((dbg_s[jj * 65:(jj + 1) * 65, :],
                              cc_recv[jj * 65:(jj + 1) * 65, :]))
            return dseq(eng, pairs)
        kb.stage(sync=x_dbg)

    def x_cast2(eng):
        eng.dma_batch([(SINIT.ap()[64:128, :], SINIT.ap()[0:64, :])])
        return "self"
    kb.stage(gpsimd=x_cast2)

    for dcz in range(DC):
        def w_l(eng, dcz=dcz):
            return dseq(eng, [(WSTG, woutt_d[dcz * 128:(dcz + 1) * 128, :])])
        def w_c(eng, dcz=dcz):
            return eng.copy(out=WOUTT[:, dcz * D:(dcz + 1) * D], in_=WSTG)
        kb.stage(sync=w_l)
        kb.stage(scalar=w_c)

    marks['exch'] = len(kb.stages)
    XR = r4(0, 1024)
    OFULL = r4(1024, 1024)
    T2 = r4(2048, 1024)
    X2T = r4(3072, 1024)
    LN2T = r4(4096, 1024)
    QSA = r4(5120, 1024)
    QSB = r4(6144, 1024)
    QSC = r4(7168, 1024)
    CBM = r4(2048, 64)
    CBM2 = r4(2112, 64)
    OIR = r5(0, 1024)
    QNR = r5(1024, 1024)
    ZR = r5(2048, 1024)
    SLZ = r5(3072, 1024)
    OGD = r5(4096, 1024)
    OGF = r5(5120, 1024)
    XQB = r5(6144, 1024)

    for c in range(NCH):
        def c_s1(eng, c=c):
            return dseq(eng, [
                (OIR, sp_oi[c * 128:(c + 1) * 128, :]),
                (QNR, sp_qn[c * 128:(c + 1) * 128, :]),
                (ZR, sp_z[c * 128:(c + 1) * 128, :]),
                (XR, x_d[c * 128:(c + 1) * 128, :]),
            ])
        kb.stage(sync=c_s1)

        def c_s2(eng):
            last = None
            for h in range(H):
                qc, hh = h // 2, h % 2
                last = eng.matmul(
                    pbank(h % 2, 0, 128, (h // 2) * 64, 64),
                    QNR[hh * 64:(hh + 1) * 64, qc * 128:(qc + 1) * 128],
                    SINIT.ap()[hh * 64:(hh + 1) * 64, h * 64:(h + 1) * 64],
                    start=True, stop=True)
            return last
        kb.stage(tensor=c_s2)

        def c_s3(eng, c=c):
            last = eng.activation(out=SLZ, in_=ZR, func=AF.Silu)
            for h in range(H):
                last = eng.activation(
                    out=T2[:, h * 64:(h + 1) * 64],
                    in_=pbank(h % 2, 0, 128, (h // 2) * 64, 64),
                    func=AF.Copy, scale=ELGD.ap()[:, c * H + h:c * H + h + 1])
            return last
        kb.stage(scalar=c_s3)

        def c_s4(eng):
            return eng.tensor_tensor(out=OFULL, in0=T2, in1=OIR, op=ALU.add)
        kb.stage(vector=c_s4)

        def c_s5(eng):
            last = None
            for h in range(H):
                last = eng.activation(out=T2[:, 0:64],
                                      in_=OFULL[:, h * 64:(h + 1) * 64],
                                      func=AF.Square,
                                      accum_out=MSQ.ap()[:, h:h + 1])
            return last
        kb.stage(scalar=c_s5)

        def c_s6(eng):
            return eng.tensor_scalar(out=RSTD.ap(), in0=MSQ.ap(),
                                     scalar1=1.0 / HD, scalar2=1e-6,
                                     op0=ALU.mult, op1=ALU.add)
        kb.stage(vector=c_s6)

        def c_s7(eng):
            eng.activation(out=MSQ.ap(), in_=RSTD.ap(), func=AF.Ln)
            return eng.activation(out=RSTD.ap(), in_=MSQ.ap(), func=AF.Exp,
                                  scale=-0.5)
        kb.stage(scalar=c_s7)

        def c_s8(eng):
            last = None
            for h in range(H):
                last = eng.scalar_tensor_tensor(
                    out=OGD[:, h * 64:(h + 1) * 64],
                    in0=OFULL[:, h * 64:(h + 1) * 64],
                    scalar=RSTD.ap()[:, h:h + 1],
                    in1=SLZ[:, h * 64:(h + 1) * 64],
                    op0=ALU.mult, op1=ALU.mult)
            return last
        kb.stage(vector=c_s8)

        def c_s9(eng):
            last = None
            for dc in range(DC):
                last = eng.transpose(pbank(dc, 0, 128, 0, 64, BF16),
                                     OGD[:, dc * 128:(dc + 1) * 128],
                                     IDB.ap())
            return last
        kb.stage(tensor=c_s9)

        def c_s10(eng):
            last = None
            with eng.parallel():
             for dc in range(DC):
                last = eng.tensor_copy(out=OGF[:, dc * 128:(dc + 1) * 128],
                                       in_=pbank(dc, 0, 128, 0, 64, BF16))
            return last
        kb.stage(vector=c_s10)

        def c_s11(eng):
            last = None
            for dc in range(DC):
                for pi in range(2):
                    last = eng.matmul(
                        pbank(4 + pi, 0, 128, 0, 512),
                        OGF[:, dc * 128:(dc + 1) * 128],
                        WOUTT[:, dc * D + pi * 512:dc * D + (pi + 1) * 512],
                        start=(dc == 0), stop=(dc == DC - 1))
            return last
        kb.stage(tensor=c_s11)

        def c_s12(eng):
            with eng.parallel():
                eng.tensor_tensor(out=X2T[:, 0:512],
                                  in0=pbank(4, 0, 128, 0, 512),
                                  in1=XR[:, 0:512], op=ALU.add)
                last = eng.tensor_tensor(out=X2T[:, 512:1024],
                                         in0=pbank(5, 0, 128, 0, 512),
                                         in1=XR[:, 512:1024], op=ALU.add)
            return last
        kb.stage(vector=c_s12)

        def c_s13(eng):
            with eng.parallel():
                eng.activation(out=T2[:, 0:512], in_=X2T[:, 0:512],
                               func=AF.Copy, accum_out=STA.ap()[:, 2:3])
                eng.activation(out=T2[:, 512:1024], in_=X2T[:, 512:1024],
                               func=AF.Copy, accum_out=STA.ap()[:, 3:4])
                eng.activation(out=LN2T[:, 0:512], in_=X2T[:, 0:512],
                               func=AF.Square, accum_out=STA.ap()[:, 4:5])
                last = eng.activation(out=LN2T[:, 512:1024],
                                      in_=X2T[:, 512:1024],
                                      func=AF.Square,
                                      accum_out=STA.ap()[:, 5:6])
            return last
        kb.stage(scalar=c_s13)

        def c_s14(eng):
            with eng.parallel():
                eng.tensor_tensor(out=STA.ap()[:, 0:1], in0=STA.ap()[:, 2:3],
                                  in1=STA.ap()[:, 3:4], op=ALU.add)
                eng.tensor_tensor(out=STA.ap()[:, 1:2], in0=STA.ap()[:, 4:5],
                                  in1=STA.ap()[:, 5:6], op=ALU.add)
            eng.tensor_scalar(out=STA.ap()[:, 2:3], in0=STA.ap()[:, 0:1],
                              scalar1=1.0 / D, scalar2=None, op0=ALU.mult)
            eng.tensor_scalar(out=STA.ap()[:, 3:4], in0=STA.ap()[:, 1:2],
                              scalar1=1.0 / D, scalar2=None, op0=ALU.mult)
            eng.tensor_tensor(out=STA.ap()[:, 4:5], in0=STA.ap()[:, 2:3],
                              in1=STA.ap()[:, 2:3], op=ALU.mult)
            return eng.tensor_tensor(out=STA.ap()[:, 5:6],
                                     in0=STA.ap()[:, 3:4],
                                     in1=STA.ap()[:, 4:5], op=ALU.subtract)
        kb.stage(vector=c_s14)

        def c_s15(eng):
            eng.activation(out=STA.ap()[:, 6:7], in_=STA.ap()[:, 5:6],
                           func=AF.Ln, bias=CCOL.ap()[:, 0:1])
            return eng.activation(out=STA.ap()[:, 6:7], in_=STA.ap()[:, 6:7],
                                  func=AF.Exp, scale=-0.5)
        kb.stage(scalar=c_s15)

        def c_s16(eng):
            return eng.scalar_tensor_tensor(
                out=STA.ap()[:, 7:8], in0=STA.ap()[:, 2:3], scalar=-1.0,
                in1=STA.ap()[:, 6:7], op0=ALU.mult, op1=ALU.mult)
        kb.stage(vector=c_s16)

        def c_s17(eng):
            return eng.tensor_scalar(out=LN2T, in0=X2T,
                                     scalar1=STA.ap()[:, 6:7],
                                     scalar2=STA.ap()[:, 7:8],
                                     op0=ALU.mult, op1=ALU.add)
        kb.stage(vector=c_s17)

        def c_s18(eng):
            return quant_chain(eng, LN2T, D, XQB, QSA, QSB, QSC, CBM, CBM2)
        kb.stage(vector=c_s18)

        def c_s19(eng):
            last = None
            for dc in range(DC):
                last = eng.transpose(pbank(dc, 0, 128, 0, 64, BF16),
                                     XQB[:, dc * 128:(dc + 1) * 128],
                                     IDB.ap())
            return last
        kb.stage(tensor=c_s19)

        def c_s20(eng):
            last = None
            with eng.parallel():
             for dc in range(DC):
                last = eng.tensor_copy(out=OGF[:, dc * 128:(dc + 1) * 128],
                                       in_=pbank(dc, 0, 128, 0, 64, BF16))
            return last
        kb.stage(vector=c_s20)

        def c_s21(eng, c=c):
            return dseq(eng, [
                (sp_xqf[c * 128:(c + 1) * 128, :], OGF),
                (sp_x2[c * 128:(c + 1) * 128, :], X2T),
            ])
        kb.stage(sync=c_s21)

    marks['c'] = len(kb.stages)
    emit_fq("tail")

    marks['fq'] = len(kb.stages)
    DX2R = r4(0, 1024)
    DHB = r4(1024, 2816)
    DSA = r4(3840, 1408)
    DSB = r4(5248, 1408)
    DSC = r4(6656, 1408)
    DBM = r4(8064, 88)
    DBM2 = R5.ap().bitcast(F32)[:, 4280:4368]
    XQFR = r5(0, 1024)
    HQ = r5(1024, 2816)
    HQFD = r5(3840, 2816)
    OUTT = R5.ap().bitcast(F32)[:, 3328:4352]
    DBM2 = R5.ap().bitcast(F32)[:, 4352:4440]

    DSL = r4(7712, 352)
    tr_groups = [(0, 4), (4, 8), (8, 11), (11, 15), (15, 19), (19, 22)]

    def mk_mm(np_):
        b = (np_ % 2) * 2
        def d_mm(eng, np_=np_, b=b):
            last = None
            for dc in range(DC):
                last = eng.matmul(
                    pbank(b, 0, 128, 0, 352),
                    XQFR[:, dc * 128:(dc + 1) * 128],
                    wgT_q[:, dc * IF + np_ * 352:
                          dc * IF + (np_ + 1) * 352],
                    start=(dc == 0), stop=(dc == DC - 1))
                last = eng.matmul(
                    pbank(b + 1, 0, 128, 0, 352),
                    XQFR[:, dc * 128:(dc + 1) * 128],
                    wuT_q[:, dc * IF + np_ * 352:
                          dc * IF + (np_ + 1) * 352],
                    start=(dc == 0), stop=(dc == DC - 1))
            return last
        return d_mm

    def mk_si(np_):
        b = (np_ % 2) * 2
        def d_si(eng, np_=np_, b=b):
            return eng.activation(out=DSL, in_=pbank(b, 0, 128, 0, 352),
                                  func=AF.Silu)
        return d_si

    def mk_h(np_):
        b = (np_ % 2) * 2
        def d_h(eng, np_=np_, b=b):
            return eng.tensor_tensor(
                out=DHB[:, np_ * 352:(np_ + 1) * 352],
                in0=DSL, in1=pbank(b + 1, 0, 128, 0, 352), op=ALU.mult)
        return d_h

    def mk_tr(k0, k1):
        def d_tr(eng, k0=k0, k1=k1):
            last = None
            for k in range(k0, k1):
                last = eng.transpose(
                    pbank(4 + (k - k0) % 4, 0, 128, 0, 64, BF16),
                    HQ[:, k * 128:(k + 1) * 128], IDB.ap())
            return last
        return d_tr

    def mk_te(k0, k1):
        def d_te(eng, k0=k0, k1=k1):
            last = None
            with eng.parallel():
             for k in range(k0, k1):
                last = eng.tensor_copy(
                    out=HQFD[:, k * 128:(k + 1) * 128],
                    in_=pbank(4 + (k - k0) % 4, 0, 128, 0, 64, BF16))
            return last
        return d_te

    def d_dn(eng):
        last = None
        for k in range(22):
            for pi in range(2):
                last = eng.matmul(pbank(4 + pi, 0, 128, 0, 512),
                                  HQFD[:, k * 128:(k + 1) * 128],
                                  wdT_chunk(k)[:, pi * 512:(pi + 1) * 512],
                                  start=(k == 0), stop=(k == 21))
        return last

    def d_out(eng):
        with eng.parallel():
            eng.tensor_tensor(out=OUTT[:, 0:512],
                              in0=pbank(4, 0, 128, 0, 512),
                              in1=DX2R[:, 0:512], op=ALU.add)
            last = eng.tensor_tensor(out=OUTT[:, 512:1024],
                                     in0=pbank(5, 0, 128, 0, 512),
                                     in1=DX2R[:, 512:1024], op=ALU.add)
        return last

    def emit_tail_tr(pv, c):
        first = pv.get("out") is None
        last_te = last_tr = None
        for gi, (k0, k1) in enumerate(tr_groups):
            if gi == 0:
                a_tr = pv["q1"] if first else {"vector": pv["out"]}
            else:
                a_tr = last_te if first else {"vector": last_te}
            last_tr = kb.stage(tensor=mk_tr(k0, k1), after_=a_tr)
            last_te = kb.stage(vector=mk_te(k0, k1),
                               after_=(last_tr if first
                                       else {"tensor": last_tr}))
        return last_tr, last_te

    def emit_tail_dn(pv, c, last_te):
        first = pv.get("st") is None
        i_dn = kb.stage(tensor=d_dn,
                        after_=(last_te if first
                                else {"vector": last_te}))
        if first:
            a_out = i_dn
        else:
            a_out = {"tensor": i_dn,
                     "sync": max(pv["s1b"], pv["st"])}
        i_out = kb.stage(vector=d_out, after_=a_out)

        def d_st(eng, c=c):
            return dseq(eng, [(y_d[c * 128:(c + 1) * 128, :], OUTT)])
        i_st = kb.stage(sync=d_st,
                        after_=(i_out if first else {"vector": i_out}))
        return i_out, i_st

    pv = None
    for c in range(NCH):
        def d_s1(eng, c=c):
            return dseq(eng, [(XQFR, sp_xqf[c * 128:(c + 1) * 128, :])])
        i_s1 = kb.stage(sync=d_s1,
                        after_=(None if pv is None
                                else {"tensor": pv["mm7"]}))

        i_mm = [None] * 8
        i_si = [None] * 8
        i_h = [None] * 8
        for np_ in range(8):
            if pv is None:
                amm = i_s1 if np_ == 0 else (
                    i_mm[0] if np_ == 1 else i_h[np_ - 2])
            elif np_ == 0:
                amm = {"sync": i_s1, "scalar": pv["si6"],
                       "vector": pv["h6"]}
            elif np_ == 1:
                amm = {"sync": i_s1, "scalar": pv["si7"],
                       "vector": pv["h7"]}
            else:
                amm = {"sync": i_s1, "scalar": i_si[np_ - 2],
                       "vector": i_h[np_ - 2]}
            i_mm[np_] = kb.stage(tensor=mk_mm(np_), after_=amm)
            if pv is None:
                asi = (max(i_h[np_ - 1], i_mm[np_]) if np_ > 0
                       else i_mm[0])
            elif np_ == 0:
                asi = {"tensor": i_mm[0], "vector": pv["q1"]}
            else:
                asi = {"tensor": i_mm[np_], "vector": i_h[np_ - 1]}
            i_si[np_] = kb.stage(scalar=mk_si(np_), after_=asi)
            ah = (i_si[np_] if pv is None else
                  {"scalar": i_si[np_], "tensor": i_mm[np_]})
            i_h[np_] = kb.stage(vector=mk_h(np_), after_=ah)

        if pv is not None:
            last_tr, last_te = emit_tail_tr(pv, c - 1)
        else:
            last_tr = last_te = None

        def d_q0(eng):
            return quant_chain(eng, DHB[:, 0:1408], 1408, HQ[:, 0:1408],
                               DSA, DSB, DSC, DBM, DBM2)
        i_q0 = kb.stage(vector=d_q0,
                        after_=(i_h[3] if pv is None
                                else {"tensor": last_tr}))

        def d_q1(eng):
            return quant_chain(eng, DHB[:, 1408:2816], 1408,
                               HQ[:, 1408:2816], DSA, DSB, DSC, DBM, DBM2)
        i_q1 = kb.stage(vector=d_q1,
                        after_=(i_h[7] if pv is None else {}))

        if pv is not None:
            i_out, i_st = emit_tail_dn(pv, c - 1, last_te)
        else:
            i_out = i_st = None

        def d_s1b(eng, c=c):
            return dseq(eng, [(DX2R, sp_x2[c * 128:(c + 1) * 128, :])])
        i_s1b = kb.stage(sync=d_s1b,
                         after_=(i_h[7] if pv is None
                                 else {"vector": i_out}))

        pv = {"mm7": i_mm[7], "si6": i_si[6], "si7": i_si[7],
              "h6": i_h[6], "h7": i_h[7], "q1": i_q1,
              "out": i_out, "st": i_st, "s1b": i_s1b}

    last_tr, last_te = emit_tail_tr(pv, NCH - 1)
    emit_tail_dn(pv, NCH - 1, last_te)

    marks['all'] = len(kb.stages)
    if max_stage is not None:
        if isinstance(max_stage, str):
            max_stage = marks[max_stage]
        kb.stages = kb.stages[:max_stage]
    nc._phase_marks = marks
    stgs = {n: st.enter_context(nc.semaphore("st_" + n))
            for n in ("sync", "scalar", "vector", "tensor", "gpsimd")}
    with nc.Block() as block:
        block.sync(kb.body(stgs, esems["sync"], "sync"))
        block.scalar(kb.body(stgs, esems["scalar"], "scalar"))
        block.vector(kb.body(stgs, esems["vector"], "vector"))
        block.tensor(kb.body(stgs, esems["tensor"], "tensor"))
        block.gpsimd(kb.body(stgs, esems["gpsimd"], "gpsimd"))

    st.close()
    return nc



def kernel(L0_attn_x, L0_attn_W_qkv, L0_attn_conv_w, L0_attn_W_z, L0_attn_W_b,
           L0_attn_W_a, L0_attn_dt_bias, L0_attn_A_log, L0_attn_W_out,
           L0_ffn_w_gate, L0_ffn_w_up, L0_ffn_w_down):
    x = np.ascontiguousarray(np.asarray(L0_attn_x, np.float32).reshape(T, D))
    wqkv = np.asarray(L0_attn_W_qkv, np.float32)
    convw = np.ascontiguousarray(
        np.asarray(L0_attn_conv_w, np.float32).reshape(CONV, 4))
    wz = np.asarray(L0_attn_W_z, np.float32)
    wb = np.asarray(L0_attn_W_b, np.float32)
    wa = np.asarray(L0_attn_W_a, np.float32)
    dtb = np.asarray(L0_attn_dt_bias, np.float32)
    alog = np.asarray(L0_attn_A_log, np.float32)
    wout = np.asarray(L0_attn_W_out, np.float32)
    wg = np.ascontiguousarray(np.asarray(L0_ffn_w_gate, np.float32))
    wu = np.ascontiguousarray(np.asarray(L0_ffn_w_up, np.float32))
    wd = np.ascontiguousarray(np.asarray(L0_ffn_w_down, np.float32))

    wqkvt = np.ascontiguousarray(wqkv.T)
    wqkvt_tiled = np.ascontiguousarray(
        wqkvt.reshape(DC, 128, CCH, 128).transpose(2, 0, 1, 3)
    ).reshape(CCH * DC * 128, 128)
    wzt = np.ascontiguousarray(wz.T)
    woutt = np.ascontiguousarray(wout.T)
    wbat = np.zeros((D + 1, 32), np.float32)
    wbat[:D, 0:16] = wb.T
    wbat[:D, 16:32] = wa.T
    wbat[D, 16:32] = dtb

    s_ar = np.arange(128)
    tri = (s_ar[:, None] <= s_ar[None, :]).astype(np.float32)
    ones = np.ones((128, 128), np.float32)
    identf = np.eye(128, dtype=np.float32)
    identb = np.eye(128).astype(BF)
    i16 = np.eye(16, dtype=np.float32)
    maskb = np.where(s_ar[:, None] <= s_ar[None, :], 0xFFFF, 0).astype(np.uint16)
    bones = np.zeros((128, 8 * 16), np.float32)
    for qc in range(8):
        for p in range(128):
            bones[p, qc * 16 + 2 * qc + (1 if p >= 64 else 0)] = 1
    bcastp = np.zeros((16, 8 * 128), np.float32)
    for qc in range(8):
        for p in range(128):
            bcastp[2 * qc + (1 if p >= 64 else 0), qc * 128 + p] = 1
    rep16 = np.zeros((16, D), np.float32)
    for h in range(16):
        rep16[h, h * 64:(h + 1) * 64] = 1
    su8 = (np.arange(8)[:, None] > np.arange(8)[None, :]).astype(np.float32)

    shared = dict(
        wqkvt=wqkvt_tiled, wzt=wzt, wbat=wbat, alog=alog.reshape(1, H),
        convw=convw, woutt=woutt, wg=wg, wu=wu, wd=wd,
        tri=tri, ones=ones, identf=identf, identb=identb, i16=i16,
        maskb=maskb, bones=bones.astype(BF), bcastp=bcastp.astype(BF),
        rep16=rep16, su8=su8,
    )

    in_maps = []
    for r in range(NCORE):
        selv = np.zeros((128, 8), np.float32)
        selv[:, r] = 1.0
        sellt = (np.arange(8) < r).astype(np.float32).reshape(8, 1)
        invsel = np.tile(1.0 - sellt, (1, 16)).astype(np.float32)
        xprev = np.zeros((3, D), np.float32)
        if r > 0:
            xprev = x[r * TLOC - 3:r * TLOC]
        m = dict(shared)
        m.update(x=np.ascontiguousarray(x[r * TLOC:(r + 1) * TLOC]),
                 xprev=np.ascontiguousarray(xprev),
                 selv=selv, sellt=sellt, invsel=invsel)
        in_maps.append(m)

    nc = build_program()
    res = run_bass_kernel_spmd(nc, in_maps, core_ids=list(range(NCORE)))
    out = np.concatenate(
        [np.asarray(res.results[r]["y"]) for r in range(NCORE)], axis=0)
    return out.reshape(1, T, D).astype(np.float32)


if __name__ == "__main__":
    import sys
    if "--build" in sys.argv:
        nc = build_program()
        n = sum(1 for _ in nc.all_instructions())
        print("built ok, instructions:", n)


# version 16
# speedup vs baseline: 1.0561x; 1.0271x over previous
import numpy as np
import ml_dtypes
import concourse.bass as bass
import concourse.mybir as mybir
from concourse.bass_utils import run_bass_kernel_spmd
from contextlib import ExitStack

F32 = mybir.dt.float32
BF16 = mybir.dt.bfloat16
U16 = mybir.dt.uint16
AF = mybir.ActivationFunctionType
ALU = mybir.AluOpType
AX = mybir.AxisListType

NCORE = 8
T = 16384
D = 1024
H = 16
HD = 64
CONV = 3072
IF = 2816
TLOC = T // NCORE
CK = 128
NCH = TLOC // CK
DC = D // 128
CCH = CONV // 128
LNW = TLOC + 3
MAGIC = float(1.5 * 2.0 ** 23)
LN8 = float(np.log(8.0))
BF = ml_dtypes.bfloat16


class SyncEng:

    _PASS = {"wait_ge", "sem_inc", "then_inc"}

    def __init__(self, eng, sem, selfsync=True):
        self.eng = eng
        self.sem = sem
        self.selfsync = selfsync
        self.count = 0
        self.par = False
        self.pending = None

    def _flush_pending(self):
        if self.pending is not None:
            inst, amt = self.pending
            inst.then_inc(self.sem, amt)
            self.count += amt
            self.pending = None
            return True
        return False

    def _pre_op(self):
        if not self.selfsync:
            return
        if self._flush_pending():
            if not self.par:
                self.eng.wait_ge(self.sem, self.count)

    def parallel(self):
        import contextlib

        @contextlib.contextmanager
        def cm():
            self._pre_op()
            self.par = True
            try:
                yield
            finally:
                self.par = False
        return cm()

    def close_stage(self, stg_sem, inc):
        assert self.pending is not None, "stage had no ops?"
        inst, _ = self.pending
        inst.then_inc(stg_sem, inc)
        self.pending = None

    def dma_batch(self, pairs):
        self._flush_pending()
        for dst, srcp in pairs:
            self.eng.dma_start(out=dst, in_=srcp).then_inc(self.sem, 16)
            self.count += 16
        self.eng.wait_ge(self.sem, self.count)

    def collective(self, *a, **kw):
        self._flush_pending()
        self.eng.collective_compute(*a, **kw).then_inc(self.sem, 1)
        self.count += 1
        self.eng.wait_ge(self.sem, self.count)

    def collective_async(self, done_sem, *a, **kw):
        self._flush_pending()
        self.eng.collective_compute(*a, **kw).then_inc(done_sem, 1)

    def wait_ge(self, *a, **kw):
        return self.eng.wait_ge(*a, **kw)

    def sem_inc(self, *a, **kw):
        return self.eng.sem_inc(*a, **kw)

    def __getattr__(self, name):
        f = getattr(self.eng, name)
        if not callable(f) or name.startswith("_"):
            return f

        def wrapped(*a, **kw):
            self._pre_op()
            inst = f(*a, **kw)
            amt = 16 if name == "dma_start" else 1
            self.pending = (inst, amt)
            return inst
        return wrapped


class KB:

    def __init__(self):
        self.stages = []
        self.afters = []

    def stage(self, after_=None, **fns):
        self.stages.append(fns)
        self.afters.append(after_)
        return len(self.stages) - 1

    def body(self, stg_sems, esem, eng_name):
        engs = list(stg_sems.keys())
        ncum = {e: [] for e in engs}
        cnt = {e: 0 for e in engs}
        for s in self.stages:
            for e in engs:
                if e in s:
                    cnt[e] += 1
                ncum[e].append(cnt[e])

        def run(eng):
            w = SyncEng(eng, esem,
                        selfsync=(eng_name not in ("tensor", "sync")))
            waited = {e: 0 for e in engs}

            def waitup(e, idx):
                need = ncum[e][idx]
                if need > waited[e]:
                    eng.wait_ge(stg_sems[e], need)
                    waited[e] = need

            for si, s in enumerate(self.stages):
                if eng_name not in s:
                    continue
                a = self.afters[si]
                if isinstance(a, dict):
                    if w.selfsync and si > 0:
                        waitup(eng_name, si - 1)
                    for e, idx in a.items():
                        if idx is not None and idx >= 0:
                            waitup(e, idx)
                else:
                    if a is None:
                        a = si - 1
                    if a >= 0:
                        for e in engs:
                            waitup(e, a)
                res = s[eng_name](w)
                if res == "self":
                    eng.sem_inc(stg_sems[eng_name], 1)
                else:
                    w.close_stage(stg_sems[eng_name], 1)
        return run


def build_program(max_stage=None, dbg=False, nocc=False):
    nc = bass.Bass("TRN2", target_bir_lowering=False,
                   dynamic_dma_scratch_size=4096)

    def din(name, shape, dtype=F32):
        return nc.dram_tensor(name, shape, dtype, kind="ExternalInput").ap()

    x_d = din("x", [TLOC, D])
    xprev_d = din("xprev", [3, D])
    selv_d = din("selv", [128, 8])
    sellt_d = din("sellt", [8, 1])
    invsel_d = din("invsel", [8, 16])
    wqkvt_d = din("wqkvt", [CCH * DC * 128, 128])
    wzt_d = din("wzt", [D, D])
    wbat_d = din("wbat", [D + 1, 32])
    alog_d = din("alog", [1, H])
    convw_d = din("convw", [CONV, 4])
    woutt_d = din("woutt", [D, D])
    wg_d = din("wg", [IF, D])
    wu_d = din("wu", [IF, D])
    wd_d = din("wd", [D, IF])
    tri_d = din("tri", [128, 128])
    ones_d = din("ones", [128, 128])
    identf_d = din("identf", [128, 128])
    identb_d = din("identb", [128, 128], BF16)
    i16_d = din("i16", [16, 16])
    maskb_d = din("maskb", [128, 128], U16)
    bones_d = din("bones", [128, 8 * 16], BF16)
    bcastp_d = din("bcastp", [16, 8 * 128], BF16)
    rep16_d = din("rep16", [16, D])
    su8_d = din("su8", [8, 8])

    y_d = nc.dram_tensor("y", [TLOC, D], F32, kind="ExternalOutput").ap()

    spk = "ExternalOutput" if dbg else "Internal"
    sp_oi = nc.dram_tensor("sp_oi", [TLOC, D], BF16, kind=spk).ap()
    sp_qn = nc.dram_tensor("sp_qn", [TLOC, D], BF16, kind=spk).ap()
    sp_z = nc.dram_tensor("sp_z", [TLOC, D], BF16, kind=spk).ap()
    sp_x2 = nc.dram_tensor("sp_x2", [TLOC, D], F32, kind=spk).ap()
    sp_xqf = nc.dram_tensor("sp_xqf", [TLOC, D], BF16, kind=spk).ap()
    if dbg:
        dbg_s = nc.dram_tensor("dbg_s", [8 * 65, D], F32,
                               kind="ExternalOutput").ap()
        dbg_a = nc.dram_tensor("dbg_a", [128, 8192], BF16,
                               kind="ExternalOutput").ap()
        dbg_b = nc.dram_tensor("dbg_b", [128, 64], F32,
                               kind="ExternalOutput").ap()
        dbg_c = nc.dram_tensor("dbg_c", [16, 128], F32,
                               kind="ExternalOutput").ap()
        dbg_d = nc.dram_tensor("dbg_d", [16, 256], BF16,
                               kind="ExternalOutput").ap()
        dbg_e = nc.dram_tensor("dbg_e", [128, 160], F32,
                               kind="ExternalOutput").ap()
    cc_send = nc.dram_tensor("cc_send", [8 * 65, D], F32, kind="Internal").ap()
    cc_recv = nc.dram_tensor("cc_recv", [8 * 65, D], F32, kind="Internal").ap()

    kb = KB()
    marks = {}
    st = ExitStack()

    def sb(name, shape, dtype=F32):
        return st.enter_context(nc.sbuf_tensor(name, shape, dtype))

    R1 = sb("R1", [128, CCH * LNW], BF16)
    R2 = sb("R2", [128, DC * LNW], BF16)
    R3 = sb("R3", [128, DC * D], BF16)
    R4 = sb("R4", [128, 8192])
    R5 = sb("R5", [128, 10240], BF16)

    def r4(c0, cn, p0=0, pn=128):
        return R4.ap()[p0:p0 + pn, c0:c0 + cn]

    def r5(c0, cn, p0=0, pn=128):
        return R5.ap()[p0:p0 + pn, c0:c0 + cn]

    MIXED = R1.ap()
    LN1 = R2.ap()
    WZT = R3.ap()
    WOUTT = R3.ap()
    wgT_q = R1.ap()[:, 0:8 * IF]
    wuT_q = R1.ap()[:, 8 * IF:16 * IF]

    def wdT_chunk(k):
        if k < 4:
            return R1.ap()[:, 16 * IF + k * D:16 * IF + (k + 1) * D]
        if k < 20:
            return R2.ap()[:, (k - 4) * D:(k - 3) * D]
        return R3.ap()[:, (k - 20) * D:(k - 19) * D]

    TRI = sb("TRI", [128, 128])
    ONES = sb("ONES", [128, 128])
    IDF = sb("IDF", [128, 128])
    IDB = sb("IDB", [128, 128], BF16)
    I16 = sb("I16", [16, 16])
    MASKB = sb("MASKB", [128, 128], U16)
    BONES = sb("BONES", [128, 8 * 16], BF16)
    BCASTP = sb("BCASTP", [16, 8 * 128], BF16)
    CONVW = sb("CONVW", [128, CCH * 4])
    WBAT = sb("WBAT", [128, 9 * 32], BF16)
    ALOG = sb("ALOG", [1, H])
    SELV = sb("SELV", [128, 8])
    SELLT = sb("SELLT", [8, 1])
    INVSEL = sb("INVSEL", [8, 16])
    ELGD = sb("ELGD", [128, NCH * H])
    EXPL = [sb("EXPL0", [128, H]), sb("EXPL1", [128, H])]
    LGDR = [sb("LGDR0", [1, H]), sb("LGDR1", [1, H])]
    NEGEA = sb("NEGEA", [128, H])
    SLOC = [sb("SLOC0", [128, H * HD], BF16), sb("SLOC1", [128, H * HD], BF16)]
    SINIT = sb("SINIT", [128, H * HD], BF16)
    BETA = sb("BETA", [128, H])
    TMP1 = sb("TMP1", [128, H])
    TMP2 = sb("TMP2", [128, H])
    TMP3 = sb("TMP3", [128, H])
    TMP4 = sb("TMP4", [128, H])
    GSB = sb("GSB", [128, H])
    NEGG = sb("NEGG", [128, H])
    GT = sb("GT", [16, 128])
    GTS = sb("GTS", [16, 128])
    E2A = sb("E2A", [128, H])
    E2 = sb("E2", [128, H])
    EXPG = sb("EXPG", [128, H])
    EB128 = sb("EB128", [128, H])
    B128S = sb("B128S", [128, H])
    LQ = sb("LQ", [16, 128])
    LK = sb("LK", [16, 128])
    M2Q = sb("M2Q", [16, 128], BF16)
    M2K = sb("M2K", [16, 128], BF16)
    MSQ = sb("MSQ", [128, H])
    RSTD = sb("RSTD", [128, H])
    STA = sb("STA", [128, 8])
    CCOL = sb("CCOL", [128, 4])
    ONESB = sb("ONESB", [1, 128], BF16)

    SQ = r5(0, 2048)
    SQF = R5.ap().bitcast(F32)[:, 0:1024]
    KH = r5(2048, 1024)
    UTM = r5(3072, 1024)
    QNF = r5(4096, 1024)
    KNF = r5(5120, 1024)
    EM = r5(6144, 2048)
    OIB = r5(8192, 1024)
    ZB = r5(9216, 1024)
    LNT = r5(0, 1024)
    WQSB = r5(1024, 1024)
    CS0 = r5(2048, 2048)
    CS1 = r5(4096, 2048)
    XT = r4(0, 1024)
    WQS = r4(1024, 1024)

    PB = [st.enter_context(nc.psum_tensor(f"PB{i}", [128, 512])) for i in range(8)]

    def pbank(i, p0, pn, c0, cn, dtype=F32):
        v = PB[i].ap()
        if dtype != F32:
            m = 2 if dtype == BF16 else 1
            return v.bitcast(dtype)[p0:p0 + pn, c0 * m:(c0 + cn) * m]
        return v[p0:p0 + pn, c0:c0 + cn]

    stg = st.enter_context(nc.semaphore("stg"))
    ccsem = st.enter_context(nc.semaphore("ccsem"))
    esems = {n: st.enter_context(nc.semaphore("es_" + n))
             for n in ("sync", "scalar", "vector", "tensor", "gpsimd")}

    def dseq(eng, pairs):
        eng.dma_batch(pairs)
        return "self"

    def p_zero(eng):
        return eng.memset(WQS[:, 256:288], 0.0)
    kb.stage(vector=p_zero)

    def p_loads(eng):
        pairs = [
            (TRI.ap(), tri_d), (ONES.ap(), ones_d), (IDF.ap(), identf_d),
            (IDB.ap(), identb_d), (I16.ap(), i16_d), (MASKB.ap(), maskb_d),
            (BONES.ap(), bones_d), (BCASTP.ap(), bcastp_d),
            (ALOG.ap(), alog_d), (SELV.ap(), selv_d), (SELLT.ap(), sellt_d),
            (INVSEL.ap(), invsel_d),
        ]
        for cc in range(CCH):
            pairs.append((CONVW.ap()[:, cc * 4:(cc + 1) * 4],
                          convw_d[cc * 128:(cc + 1) * 128, :]))
        for k in range(8):
            pairs.append((WQS[:, k * 32:(k + 1) * 32],
                          wbat_d[k * 128:(k + 1) * 128, :]))
        pairs.append((WQS[0:1, 256:288], wbat_d[1024:1025, :]))
        return dseq(eng, pairs)
    kb.stage(sync=p_loads)

    def p_init(eng):
        eng.memset(ONESB.ap(), 1.0)
        eng.memset(CCOL.ap()[:, 0:1], 1e-5)
        eng.memset(CCOL.ap()[:, 1:2], 1e-6)
        eng.memset(CCOL.ap()[:, 2:3], -LN8)
        eng.memset(CCOL.ap()[:, 3:4], 1.0)
        eng.memset(EXPL[0].ap(), 1.0)
        eng.memset(LGDR[0].ap(), 0.0)
        eng.memset(SLOC[0].ap(), 0.0)
        return eng.memset(SLOC[1].ap(), 0.0)
    kb.stage(vector=p_init)

    def p_ea1(eng):
        return eng.activation(out=TMP1.ap()[0:1, :], in_=ALOG.ap(), func=AF.Exp)
    kb.stage(scalar=p_ea1)

    def p_ea2(eng):
        return eng.tensor_scalar(out=TMP2.ap()[0:1, :], in0=TMP1.ap()[0:1, :],
                                 scalar1=-1.0, scalar2=None, op0=ALU.mult)
    kb.stage(vector=p_ea2)

    def p_ea3(eng):
        return eng.matmul(pbank(0, 0, 128, 0, 16), ONES.ap()[0:1, :],
                          TMP2.ap()[0:1, :], start=True, stop=True)
    kb.stage(tensor=p_ea3)

    def p_ea4(eng):
        return eng.tensor_copy(out=NEGEA.ap(), in_=pbank(0, 0, 128, 0, 16))
    kb.stage(vector=p_ea4)

    def p_wba(eng):
        return eng.copy(out=WBAT.ap(), in_=WQS[:, 0:288])
    kb.stage(scalar=p_wba)

    for dcz in range(DC):
        def p_wzl(eng, dcz=dcz):
            return dseq(eng, [(WQS, wzt_d[dcz * 128:(dcz + 1) * 128, :])])
        def p_wzc(eng, dcz=dcz):
            return eng.copy(out=WZT[:, dcz * D:(dcz + 1) * D], in_=WQS)
        kb.stage(sync=p_wzl)
        kb.stage(scalar=p_wzc)

    marks['prologue'] = len(kb.stages)
    def ln_tile(rows, col0):
        def s1(eng):
            with eng.parallel():
                eng.activation(out=r4(2048, 1024, 0, rows), in_=XT[0:rows, :],
                               func=AF.Copy, accum_out=STA.ap()[0:rows, 0:1])
                last = eng.activation(out=r4(3072, 1024, 0, rows),
                                      in_=XT[0:rows, :], func=AF.Square,
                                      accum_out=STA.ap()[0:rows, 1:2])
            return last
        kb.stage(scalar=s1)

        def s2(eng):
            eng.tensor_scalar(out=STA.ap()[0:rows, 2:3],
                              in0=STA.ap()[0:rows, 0:1], scalar1=1.0 / D,
                              scalar2=None, op0=ALU.mult)
            eng.tensor_scalar(out=STA.ap()[0:rows, 3:4],
                              in0=STA.ap()[0:rows, 1:2], scalar1=1.0 / D,
                              scalar2=None, op0=ALU.mult)
            eng.tensor_tensor(out=STA.ap()[0:rows, 4:5],
                              in0=STA.ap()[0:rows, 2:3],
                              in1=STA.ap()[0:rows, 2:3], op=ALU.mult)
            return eng.tensor_tensor(out=STA.ap()[0:rows, 5:6],
                                     in0=STA.ap()[0:rows, 3:4],
                                     in1=STA.ap()[0:rows, 4:5],
                                     op=ALU.subtract)
        kb.stage(vector=s2)

        def s3(eng):
            eng.activation(out=STA.ap()[0:rows, 6:7],
                           in_=STA.ap()[0:rows, 5:6], func=AF.Ln,
                           bias=CCOL.ap()[0:rows, 0:1])
            return eng.activation(out=STA.ap()[0:rows, 6:7],
                                  in_=STA.ap()[0:rows, 6:7], func=AF.Exp,
                                  scale=-0.5)
        kb.stage(scalar=s3)

        def s4(eng):
            return eng.scalar_tensor_tensor(
                out=STA.ap()[0:rows, 7:8], in0=STA.ap()[0:rows, 2:3],
                scalar=-1.0, in1=STA.ap()[0:rows, 6:7],
                op0=ALU.mult, op1=ALU.mult)
        kb.stage(vector=s4)

        def s5(eng):
            return eng.tensor_scalar(out=LNT[0:rows, :], in0=XT[0:rows, :],
                                     scalar1=STA.ap()[0:rows, 6:7],
                                     scalar2=STA.ap()[0:rows, 7:8],
                                     op0=ALU.mult, op1=ALU.add)
        kb.stage(vector=s5)

        def s6(eng):
            last = None
            for dc in range(DC):
                last = eng.transpose(pbank(dc, 0, 128, 0, 64, BF16)[:, 0:rows],
                                     LNT[0:rows, dc * 128:(dc + 1) * 128],
                                     IDB.ap()[0:rows, 0:rows])
            return last
        kb.stage(tensor=s6)

        def s7(eng):
            last = None
            with eng.parallel():
                for dc in range(DC):
                    last = eng.tensor_copy(
                        out=LN1[:, dc * LNW + col0:dc * LNW + col0 + rows],
                        in_=pbank(dc, 0, 128, 0, 64, BF16)[:, 0:rows])
            return last
        kb.stage(vector=s7)

    def l_xp(eng):
        return dseq(eng, [(XT[0:3, :], xprev_d)])
    kb.stage(sync=l_xp)
    ln_tile(3, 0)
    for tt in range(NCH):
        def l_x(eng, tt=tt):
            return dseq(eng, [(XT, x_d[tt * 128:(tt + 1) * 128, :])])
        kb.stage(sync=l_x)
        ln_tile(128, 3 + tt * 128)

    marks['ln'] = len(kb.stages)
    npieces = [(0, 512), (512, 512), (1024, 512), (1536, 512), (2048, 3)]
    qv = None
    for cc in range(CCH):
        def q_l(eng, cc=cc):
            return dseq(eng, [(WQS.rearrange("p (dc c) -> p dc c", c=128),
                               wqkvt_d[cc * DC * 128:(cc + 1) * DC * 128, :]
                               .rearrange("(dc p) c -> p dc c", p=128))])
        i_ql = kb.stage(sync=q_l,
                        after_=(None if qv is None
                                else {"scalar": qv["c"]}))

        def q_c(eng):
            return eng.copy(out=WQSB, in_=WQS)
        i_qc = kb.stage(scalar=q_c,
                        after_=(None if qv is None
                                else {"sync": i_ql, "tensor": qv["mm"]}))

        def q_mm(eng, cc=cc):
            last = None
            for dc in range(DC):
                for pi, (c0, cn) in enumerate(npieces):
                    bank = pi if pi < 4 else 4
                    last = eng.matmul(
                        pbank(bank, 0, 128, 0, cn),
                        WQSB[:, dc * 128:(dc + 1) * 128],
                        LN1[:, dc * LNW + c0:dc * LNW + c0 + cn],
                        start=(dc == 0), stop=(dc == DC - 1))
            return last
        i_qmm = kb.stage(tensor=q_mm,
                         after_=(None if qv is None
                                 else {"scalar": i_qc, "vector": qv["ev"]}))

        def q_ev(eng, cc=cc):
            last = None
            with eng.parallel():
                for pi, (c0, cn) in enumerate(npieces):
                    bank = pi if pi < 4 else 4
                    last = eng.tensor_copy(
                        out=MIXED[:, cc * LNW + c0:cc * LNW + c0 + cn],
                        in_=pbank(bank, 0, 128, 0, cn))
            return last
        i_qev = kb.stage(vector=q_ev,
                         after_=(None if qv is None
                                 else {"tensor": i_qmm}))

        def q_cv2(eng, cc=cc):
            base = cc * LNW
            eng.tensor_scalar(out=CS0, in0=MIXED[:, base:base + 2048],
                              scalar1=CONVW.ap()[:, cc * 4:cc * 4 + 1],
                              scalar2=None, op0=ALU.mult)
            eng.scalar_tensor_tensor(
                out=CS1, in0=MIXED[:, base + 1:base + 2049],
                scalar=CONVW.ap()[:, cc * 4 + 1:cc * 4 + 2],
                in1=CS0, op0=ALU.mult, op1=ALU.add)
            eng.scalar_tensor_tensor(
                out=CS0, in0=MIXED[:, base + 2:base + 2050],
                scalar=CONVW.ap()[:, cc * 4 + 2:cc * 4 + 3],
                in1=CS1, op0=ALU.mult, op1=ALU.add)
            return eng.scalar_tensor_tensor(
                out=CS1, in0=MIXED[:, base + 3:base + 2051],
                scalar=CONVW.ap()[:, cc * 4 + 3:cc * 4 + 4],
                in1=CS0, op0=ALU.mult, op1=ALU.add)
        i_cv2 = kb.stage(vector=q_cv2,
                         after_=(None if qv is None
                                 else {"scalar": qv["cv3"]}))

        def q_cv3(eng, cc=cc):
            return eng.activation(out=MIXED[:, cc * LNW:cc * LNW + 2048],
                                  in_=CS1, func=AF.Silu)
        i_cv3 = kb.stage(scalar=q_cv3,
                         after_=(None if qv is None
                                 else {"vector": i_cv2}))
        qv = {"c": i_qc, "mm": i_qmm, "ev": i_qev, "cv3": i_cv3}

    marks['qkv'] = len(kb.stages)
    BETA2 = [BETA, sb("BETAb", [128, H])]
    GSB2 = [GSB, sb("GSBb", [128, H])]
    NEGG2 = [NEGG, sb("NEGGb", [128, H])]
    GT2 = [GT, sb("GTb", [16, 128])]
    E22 = [E2, sb("E2b", [128, H])]
    EXPG2 = [EXPG, sb("EXPGb", [128, H])]
    EB1282 = [EB128, sb("EB128b", [128, H])]
    M2Q2 = [M2Q, sb("M2Qb", [16, 128], BF16)]
    M2K2 = [M2K, sb("M2Kb", [16, 128], BF16)]
    T1F = r4(0, 1024)

    prev_head_last = None
    prev_tail17 = [None, None]
    for c in range(NCH):
        pc = c % 2
        ct = c * 128
        lt = 3 + c * 128
        BETAc, GSBc, NEGGc = BETA2[pc].ap(), GSB2[pc].ap(), NEGG2[pc].ap()
        GTc, E2c, EXPGc = GT2[pc].ap(), E22[pc].ap(), EXPG2[pc].ap()
        EB128c = EB1282[pc].ap()
        M2Qc, M2Kc = M2Q2[pc].ap(), M2K2[pc].ap()

        def a_s1(eng, lt=lt):
            last = None
            for dc in range(DC):
                last = eng.matmul(
                    pbank(6, 0, 128, 0, 32),
                    LN1[:, dc * LNW + lt:dc * LNW + lt + 128],
                    WBAT.ap()[:, dc * 32:(dc + 1) * 32],
                    start=(dc == 0), stop=False)
            return eng.matmul(pbank(6, 0, 128, 0, 32), ONESB.ap(),
                              WBAT.ap()[0:1, 256:288], start=False, stop=True)
        h_first = kb.stage(tensor=a_s1, after_=prev_head_last)

        def a_s2(eng, BETAc=BETAc):
            eng.activation(out=BETAc, in_=pbank(6, 0, 128, 0, 16),
                           func=AF.Sigmoid)
            eng.activation(out=TMP1.ap(), in_=pbank(6, 0, 128, 16, 16),
                           func=AF.Abs)
            eng.activation(out=TMP2.ap(), in_=TMP1.ap(), func=AF.Exp,
                           scale=-1.0)
            eng.activation(out=TMP1.ap(), in_=TMP2.ap(), func=AF.Ln,
                           bias=CCOL.ap()[:, 3:4])
            return eng.activation(out=TMP3.ap(), in_=pbank(6, 0, 128, 16, 16),
                                  func=AF.Relu)
        kb.stage(scalar=a_s2)

        def a_s3(eng):
            eng.tensor_tensor(out=TMP4.ap(), in0=TMP3.ap(), in1=TMP1.ap(),
                              op=ALU.add)
            return eng.tensor_tensor(out=TMP2.ap(), in0=TMP4.ap(),
                                     in1=NEGEA.ap(), op=ALU.mult)
        kb.stage(vector=a_s3)

        def a_s4(eng):
            eng.transpose(pbank(7, 0, 16, 0, 128), TMP2.ap(), IDF.ap())
            eng.matmul(pbank(6, 0, 128, 32, 16), TRI.ap(), TMP2.ap(),
                       start=True, stop=True)
            return eng.matmul(pbank(6, 0, 128, 48, 16), ONES.ap(), TMP2.ap(),
                              start=True, stop=True)
        kb.stage(tensor=a_s4)

        def a_s5v(eng, pc=pc, GSBc=GSBc, NEGGc=NEGGc):
            eng.tensor_copy(out=GSBc, in_=pbank(6, 0, 128, 32, 16))
            with eng.parallel():
                eng.tensor_copy(out=GTS.ap(), in_=pbank(7, 0, 16, 0, 128))
                eng.tensor_copy(out=B128S.ap(), in_=pbank(6, 0, 128, 48, 16))
                eng.tensor_scalar(out=NEGGc, in0=pbank(6, 0, 128, 32, 16),
                                  scalar1=-1.0, scalar2=None, op0=ALU.mult)
                eng.tensor_tensor(out=E2A.ap(), in0=pbank(6, 0, 128, 48, 16),
                                  in1=GSBc, op=ALU.subtract)
                last = eng.tensor_tensor(out=LGDR[1 - pc].ap(),
                                         in0=LGDR[pc].ap(),
                                         in1=pbank(6, 0, 1, 48, 16),
                                         op=ALU.add)
            return last

        def a_s5s(eng, ct=ct):
            last = None
            with eng.parallel():
                for qc in range(16):
                    last = eng.activation(
                        out=SQ[:, qc * 128:(qc + 1) * 128],
                        in_=MIXED[:, qc * LNW + ct:qc * LNW + ct + 128],
                        func=AF.Square)
            return last
        kb.stage(vector=a_s5v, scalar=a_s5s)

        def a_s6v(eng, GTc=GTc):
            return eng.tensor_tensor_scan(out=GTc, data0=GTS.ap(),
                                          data1=GTS.ap(), initial=0.0,
                                          op0=ALU.add, op1=ALU.bypass)

        def a_s6s(eng, GSBc=GSBc, EXPGc=EXPGc, EB128c=EB128c, E2c=E2c):
            with eng.parallel():
                eng.activation(out=EXPGc, in_=GSBc, func=AF.Exp)
                eng.activation(out=EB128c, in_=B128S.ap(), func=AF.Exp)
                last = eng.activation(out=E2c, in_=E2A.ap(), func=AF.Exp)
            return last

        def a_s6t(eng):
            last = None
            for qc in range(8):
                last = eng.matmul(pbank(7, 0, 16, 128, 128),
                                  BONES.ap()[:, qc * 16:(qc + 1) * 16],
                                  SQ[:, qc * 128:(qc + 1) * 128],
                                  start=(qc == 0), stop=(qc == 7))
            for qc in range(8):
                last = eng.matmul(pbank(7, 0, 16, 256, 128),
                                  BONES.ap()[:, qc * 16:(qc + 1) * 16],
                                  SQ[:, (8 + qc) * 128:(9 + qc) * 128],
                                  start=(qc == 0), stop=(qc == 7))
            return last
        kb.stage(vector=a_s6v, scalar=a_s6s, tensor=a_s6t)

        def a_s7s(eng, M2Qc=M2Qc, M2Kc=M2Kc):
            with eng.parallel():
                eng.activation(out=LQ.ap(), in_=pbank(7, 0, 16, 128, 128),
                               func=AF.Ln, bias=CCOL.ap()[0:16, 1:2])
                eng.activation(out=LK.ap(), in_=pbank(7, 0, 16, 256, 128),
                               func=AF.Ln, bias=CCOL.ap()[0:16, 1:2])
            with eng.parallel():
                eng.activation(out=M2Qc, in_=LQ.ap(), func=AF.Exp,
                               scale=-0.5, bias=CCOL.ap()[0:16, 2:3])
                last = eng.activation(out=M2Kc, in_=LK.ap(), func=AF.Exp,
                                      scale=-0.5)
            return last

        def a_s7v(eng, c=c, pc=pc, EXPGc=EXPGc, EB128c=EB128c):
            with eng.parallel():
                eng.tensor_tensor(out=ELGD.ap()[:, c * H:(c + 1) * H],
                                  in0=EXPGc, in1=EXPL[pc].ap(), op=ALU.mult)
                last = eng.tensor_tensor(out=EXPL[1 - pc].ap(),
                                         in0=EXPL[pc].ap(),
                                         in1=EB128c, op=ALU.mult)
            return last
        h_last = kb.stage(scalar=a_s7s, vector=a_s7v)
        prev_head_last = h_last

        def a_s8t(eng, M2Qc=M2Qc, M2Kc=M2Kc):
            last = None
            for qc in range(8):
                last = eng.matmul(pbank(qc % 4, 0, 128, 128 * (qc // 4), 128),
                                  BCASTP.ap()[:, qc * 128:(qc + 1) * 128],
                                  M2Qc, start=True, stop=True)
            for qc in range(8):
                last = eng.matmul(
                    pbank(qc % 4, 0, 128, 256 + 128 * (qc // 4), 128),
                    BCASTP.ap()[:, qc * 128:(qc + 1) * 128],
                    M2Kc, start=True, stop=True)
            return last
        kb.stage(tensor=a_s8t)

        def a_s9v(eng, ct=ct):
            last = None
            with eng.parallel():
             for qc in range(8):
                last = eng.tensor_tensor(
                    out=QNF[:, qc * 128:(qc + 1) * 128],
                    in0=MIXED[:, qc * LNW + ct:qc * LNW + ct + 128],
                    in1=pbank(qc % 4, 0, 128, 128 * (qc // 4), 128),
                    op=ALU.mult)
             for qc in range(8):
                last = eng.tensor_tensor(
                    out=KNF[:, qc * 128:(qc + 1) * 128],
                    in0=MIXED[:, (8 + qc) * LNW + ct:(8 + qc) * LNW + ct + 128],
                    in1=pbank(qc % 4, 0, 128, 256 + 128 * (qc // 4), 128),
                    op=ALU.mult)
            return last
        kb.stage(vector=a_s9v)

        def a_s10t(eng, ct=ct, GTc=GTc):
            last = None
            for h in range(H):
                last = eng.matmul(
                    pbank(h % 4, 0, 128, 128 * (h // 4), 128),
                    I16.ap()[:, h:h + 1].broadcast_to([16, 128]),
                    GTc, start=True, stop=True)
            for qc in range(8):
                last = eng.transpose(pbank(4, 0, 128, qc * 64, 64, BF16),
                                     KNF[:, qc * 128:(qc + 1) * 128],
                                     IDB.ap())
            for qc in range(8):
                last = eng.transpose(
                    pbank(5, 0, 128, qc * 64, 64, BF16),
                    MIXED[:, (16 + qc) * LNW + ct:(16 + qc) * LNW + ct + 128],
                    IDB.ap())
            return last
        kb.stage(tensor=a_s10t)

        def a_s10v(eng, NEGGc=NEGGc):
            last = None
            with eng.parallel():
             for h in range(H):
                last = eng.tensor_scalar(
                    out=r4(2048, 2048)[:, h * 128:(h + 1) * 128],
                    in0=pbank(h % 4, 0, 128, 128 * (h // 4), 128),
                    scalar1=NEGGc[:, h:h + 1], scalar2=0.0,
                    op0=ALU.add, op1=ALU.min)
            return last
        kb.stage(vector=a_s10v)

        def a_s11s(eng, E2c=E2c, BETAc=BETAc):
            last = None
            with eng.parallel():
             for h in range(H):
                last = eng.activation(
                    out=EM[:, h * 128:(h + 1) * 128],
                    in_=r4(2048, 2048)[:, h * 128:(h + 1) * 128],
                    func=AF.Exp)
             for qc in range(8):
                for hh in range(2):
                    h = 2 * qc + hh
                    last = eng.activation(
                        out=KH[:, h * 64:(h + 1) * 64],
                        in_=pbank(4, 0, 128, qc * 64, 64, BF16)
                        [:, hh * 64:(hh + 1) * 64],
                        func=AF.Copy, scale=E2c[:, h:h + 1])
                    last = eng.activation(
                        out=UTM[:, h * 64:(h + 1) * 64],
                        in_=pbank(5, 0, 128, qc * 64, 64, BF16)
                        [:, hh * 64:(hh + 1) * 64],
                        func=AF.Copy, scale=BETAc[:, h:h + 1])
            return last
        kb.stage(scalar=a_s11s)

        def a_s12v(eng):
            last = None
            with eng.parallel():
             for h in range(H):
                v = EM[:, h * 128:(h + 1) * 128].bitcast(U16)
                last = eng.tensor_tensor(out=v, in0=v, in1=MASKB.ap(),
                                         op=ALU.bitwise_and)
            return last
        kb.stage(vector=a_s12v)

        def a_s13t(eng, lt=lt):
            last = None
            for h in range(H):
                qc, hh = h // 2, h % 2
                last = eng.matmul(
                    pbank(h % 4, 0, 128, 128 * (h // 4), 128),
                    KNF[hh * 64:(hh + 1) * 64, qc * 128:(qc + 1) * 128],
                    QNF[hh * 64:(hh + 1) * 64, qc * 128:(qc + 1) * 128],
                    start=True, stop=True)
            for dc in range(DC):
                for pi in range(2):
                    last = eng.matmul(
                        pbank(4 + pi, 0, 128, 0, 512),
                        LN1[:, dc * LNW + lt:dc * LNW + lt + 128],
                        WZT[:, dc * D + pi * 512:dc * D + (pi + 1) * 512],
                        start=(dc == 0), stop=(dc == DC - 1))
            return last
        kb.stage(tensor=a_s13t)

        def a_s14v(eng):
            last = None
            with eng.parallel():
             for h in range(H):
                last = eng.tensor_tensor(
                    out=EM[:, h * 128:(h + 1) * 128],
                    in0=pbank(h % 4, 0, 128, 128 * (h // 4), 128),
                    in1=EM[:, h * 128:(h + 1) * 128], op=ALU.mult)
             for pi in range(2):
                last = eng.tensor_copy(out=ZB[:, pi * 512:(pi + 1) * 512],
                                       in_=pbank(4 + pi, 0, 128, 0, 512))
            return last
        kb.stage(vector=a_s14v)

        def a_s15t(eng, pc=pc):
            last = None
            for h in range(H):
                last = eng.matmul(pbank(h % 2, 0, 128, (h // 2) * 64, 64),
                                  EM[:, h * 128:(h + 1) * 128],
                                  UTM[:, h * 64:(h + 1) * 64],
                                  start=True, stop=True)
            for h in range(H):
                qc, hh = h // 2, h % 2
                last = eng.matmul(
                    pbank(2 + h % 2, 0, 128, (h // 2) * 64, 64),
                    QNF[hh * 64:(hh + 1) * 64, qc * 128:(qc + 1) * 128],
                    SLOC[pc].ap()[hh * 64:(hh + 1) * 64,
                                  h * 64:(h + 1) * 64],
                    start=True, stop=True)
            for h in range(H):
                last = eng.matmul(pbank(4 + h % 2, 0, 64, (h // 2) * 64, 64),
                                  KH[:, h * 64:(h + 1) * 64],
                                  UTM[:, h * 64:(h + 1) * 64],
                                  start=True, stop=True)
            for h in range(H):
                last = eng.matmul(pbank(4 + h % 2, 64, 64, (h // 2) * 64, 64),
                                  KH[:, h * 64:(h + 1) * 64],
                                  UTM[:, h * 64:(h + 1) * 64],
                                  start=True, stop=True,
                                  tile_position=(0, 64))
            return last
        kb.stage(tensor=a_s15t)

        def a_s16s(eng, EXPGc=EXPGc):
            last = None
            with eng.parallel():
             for h in range(H):
                last = eng.activation(
                    out=T1F[:, h * 64:(h + 1) * 64],
                    in_=pbank(2 + h % 2, 0, 128, (h // 2) * 64, 64),
                    func=AF.Copy, scale=EXPGc[:, h:h + 1])
            return last
        kb.stage(scalar=a_s16s)

        def a_s17v(eng, pc=pc, EB128c=EB128c):
            last = None
            with eng.parallel():
             for h in range(H):
                last = eng.tensor_tensor(
                    out=OIB[:, h * 64:(h + 1) * 64],
                    in0=T1F[:, h * 64:(h + 1) * 64],
                    in1=pbank(h % 2, 0, 128, (h // 2) * 64, 64), op=ALU.add)
             for h in range(H):
                for b0 in (0, 64):
                    last = eng.scalar_tensor_tensor(
                        out=SLOC[1 - pc].ap()[b0:b0 + 64,
                                              h * 64:(h + 1) * 64],
                        in0=SLOC[pc].ap()[b0:b0 + 64, h * 64:(h + 1) * 64],
                        scalar=EB128c[b0:b0 + 64, h:h + 1],
                        in1=pbank(4 + h % 2, b0, 64, (h // 2) * 64, 64),
                        op0=ALU.mult, op1=ALU.add)
            return last
        t17 = kb.stage(vector=a_s17v)
        prev_tail17[pc] = t17

        def a_s18(eng, c=c):
            return dseq(eng, [
                (sp_oi[c * 128:(c + 1) * 128, :], OIB),
                (sp_qn[c * 128:(c + 1) * 128, :], QNF),
                (sp_z[c * 128:(c + 1) * 128, :], ZB),
            ])
        kb.stage(sync=a_s18)

    marks['attn'] = len(kb.stages)
    def quant_chain(eng, src, F, out_bf, sa, sb_, sc, bm, bm2):
        nb = F // 16
        eng.tensor_reduce(out=bm, in_=src.rearrange("p (b s) -> p b s", s=16),
                          axis=AX.X, op=ALU.max, apply_absolute_value=True)
        eng.tensor_scalar(out=bm2, in0=bm, scalar1=1.0 / 6.0, scalar2=1e-12,
                          op0=ALU.mult, op1=ALU.max)
        eng.reciprocal(out=bm, in_=bm2)
        eng.tensor_tensor(out=sa.rearrange("p (b s) -> p b s", s=16),
                          in0=src.rearrange("p (b s) -> p b s", s=16),
                          in1=bm.unsqueeze(2).broadcast_to([128, nb, 16]),
                          op=ALU.mult)
        with eng.parallel():
            eng.tensor_scalar(out=sb_, in0=sa, scalar1=2.0, scalar2=-2.0,
                              op0=ALU.min, op1=ALU.max)
            eng.tensor_scalar(out=sc, in0=sa, scalar1=4.0, scalar2=-4.0,
                              op0=ALU.min, op1=ALU.max)
        eng.scalar_tensor_tensor(out=sb_, in0=sa, scalar=0.5, in1=sb_,
                                 op0=ALU.mult, op1=ALU.add)
        eng.scalar_tensor_tensor(out=sc, in0=sc, scalar=0.5, in1=sb_,
                                 op0=ALU.mult, op1=ALU.add)
        eng.tensor_scalar(out=sa, in0=sc, scalar1=MAGIC, scalar2=MAGIC,
                          op0=ALU.add, op1=ALU.subtract)
        with eng.parallel():
            eng.tensor_scalar(out=sb_, in0=sa, scalar1=4.0, scalar2=-4.0,
                              op0=ALU.min, op1=ALU.max)
            eng.tensor_scalar(out=sc, in0=sa, scalar1=6.0, scalar2=-6.0,
                              op0=ALU.min, op1=ALU.max)
        eng.scalar_tensor_tensor(out=sa, in0=sa, scalar=2.0, in1=sc,
                                 op0=ALU.mult, op1=ALU.subtract)
        eng.scalar_tensor_tensor(out=sb_, in0=sb_, scalar=-0.5, in1=sa,
                                 op0=ALU.mult, op1=ALU.add)
        return eng.tensor_tensor(
            out=out_bf.rearrange("p (b s) -> p b s", s=16),
            in0=sb_.rearrange("p (b s) -> p b s", s=16),
            in1=bm2.unsqueeze(2).broadcast_to([128, nb, 16]), op=ALU.mult)

    def emit_fq(part):
        FSRC = r4(0, 1408)
        FSA = r4(1408, 1408)
        FSB = r4(2816, 1408)
        FSC = r4(4224, 1408)
        FBM = r4(5632, 88)
        FBM2 = r4(5720, 88)
        FOUT = r5(0, 1408)

        def fq_seg(src_dram, rows0, cols0, F, dst_fn, sub0):
            def f_l(eng):
                return dseq(eng, [(FSRC[:, 0:F],
                                   src_dram[rows0:rows0 + 128, cols0:cols0 + F])])
            kb.stage(sync=f_l)

            def f_q(eng):
                return quant_chain(eng, FSRC[:, 0:F], F, FOUT[:, 0:F],
                                   FSA[:, 0:F], FSB[:, 0:F], FSC[:, 0:F],
                                   FBM[:, 0:F // 16], FBM2[:, 0:F // 16])
            kb.stage(vector=f_q)

            nsub = F // 128
            for s0 in range(0, nsub, 8):
                def f_tr(eng, s0=s0, nsub=nsub):
                    last = None
                    for k in range(s0, min(s0 + 8, nsub)):
                        last = eng.transpose(pbank(k % 8, 0, 128, 0, 64, BF16),
                                             FOUT[:, k * 128:(k + 1) * 128],
                                             IDB.ap())
                    return last
                kb.stage(tensor=f_tr)

                def f_ev(eng, s0=s0, nsub=nsub, sub0=sub0):
                    last = None
                    with eng.parallel():
                     for k in range(s0, min(s0 + 8, nsub)):
                        last = eng.tensor_copy(
                            out=dst_fn(sub0 + k),
                            in_=pbank(k % 8, 0, 128, 0, 64, BF16))
                    return last
                kb.stage(vector=f_ev)

        if part == "main":
            for wsrc, wdst in ((wg_d, wgT_q), (wu_d, wuT_q)):
                for nt in range(IF // 128):
                    def mk(wdst=wdst, nt=nt):
                        return lambda k: wdst[:, k * IF + nt * 128:
                                              k * IF + nt * 128 + 128]
                    fq_seg(wsrc, nt * 128, 0, D, mk(), 0)
            for dt in range(DC):
                def mkd(dt=dt):
                    return lambda k: wdT_chunk(k)[:, dt * 128:dt * 128 + 128]
                fq_seg(wd_d, dt * 128, 0, 1408, mkd(), 0)
        else:
            for dt in range(DC):
                def mkd(dt=dt):
                    return lambda k: wdT_chunk(k)[:, dt * 128:dt * 128 + 128]
                fq_seg(wd_d, dt * 128, 1408, 1408, mkd(), 11)


    SSTG = r4(0, 1024, 0, 65)
    MSKT = r4(1024, 1024, 0, 65)
    RFIN = r4(2048, 1024, 0, 64)
    REP = r4(3072, 1024, 0, 16)
    ACC = r4(4096, 1024, 0, 64)
    WSTG = r4(5120, 1024)
    DTOTS = r4(6144, 16, 0, 8)
    DPT = r4(6160, 16, 0, 8)
    LNDP = r4(6176, 16, 0, 8)
    COEF = r4(6192, 16, 0, 8)
    COEFM = r4(6208, 16, 0, 8)
    COEFT = r4(6224, 8, 0, 16)
    SU8S = r4(6232, 8, 0, 8)

    def x_s1(eng):
        eng.memset(SSTG[64:65, :], 0.0)
        return eng.tensor_copy(out=SSTG[0:64, :], in_=SLOC[0].ap()[0:64, :])
    kb.stage(vector=x_s1)

    def x_s2(eng):
        return eng.activation(out=TMP1.ap()[0:1, 0:16], in_=LGDR[0].ap(),
                              func=AF.Exp)
    kb.stage(scalar=x_s2)

    def x_s2b(eng):
        eng.dma_batch([(SSTG[64:65, 0:16], TMP1.ap()[0:1, 0:16])])
        return "self"
    kb.stage(gpsimd=x_s2b)

    for j in range(8):
        def x_mask(eng, j=j):
            return eng.tensor_scalar(out=MSKT, in0=SSTG,
                                     scalar1=SELV.ap()[0:65, j:j + 1],
                                     scalar2=None, op0=ALU.mult)
        kb.stage(vector=x_mask)

        def x_send(eng, j=j):
            return dseq(eng, [(cc_send[j * 65:(j + 1) * 65, :], MSKT)])
        kb.stage(sync=x_send)

    def x_cc(eng):
        if nocc:
            eng.dma_batch([(cc_recv[j * 65:(j + 1) * 65, :], MSKT)
                           for j in range(8)])
            eng.sem_inc(ccsem, 1)
        else:
            eng.collective_async(ccsem, "AllReduce", ALU.add,
                                 replica_groups=[[0, 1, 2, 3, 4, 5, 6, 7]],
                                 ins=[cc_send.opt()], outs=[cc_recv.opt()])
        return "self"
    kb.stage(gpsimd=x_cc)

    emit_fq("main")

    def x_rl(eng):
        eng.wait_ge(ccsem, 1)
        return dseq(eng, [
            (DTOTS, cc_recv.rearrange("(j r) c -> j r c", r=65)[:, 64, 0:16]),
            (REP, rep16_d),
            (SU8S, su8_d),
        ])
    kb.stage(sync=x_rl)

    def x_accz(eng):
        return eng.memset(ACC, 0.0)
    kb.stage(vector=x_accz)

    def x_dp(eng):
        return eng.scalar_tensor_tensor(out=DPT, in0=DTOTS,
                                        scalar=SELLT.ap(), in1=INVSEL.ap(),
                                        op0=ALU.mult, op1=ALU.add)
    kb.stage(vector=x_dp)

    def x_lnc(eng):
        return eng.tensor_scalar(out=DPT, in0=DPT, scalar1=1e-38,
                                 scalar2=None, op0=ALU.max)
    kb.stage(vector=x_lnc)

    def x_ln(eng):
        return eng.activation(out=LNDP, in_=DPT, func=AF.Ln)
    kb.stage(scalar=x_ln)

    def x_mm(eng):
        return eng.matmul(pbank(0, 0, 8, 0, 16), SU8S, LNDP,
                          start=True, stop=True)
    kb.stage(tensor=x_mm)

    def x_coef(eng):
        return eng.activation(out=COEF, in_=pbank(0, 0, 8, 0, 16),
                              func=AF.Exp)
    kb.stage(scalar=x_coef)

    def x_coefm(eng):
        return eng.tensor_scalar(out=COEFM, in0=COEF, scalar1=SELLT.ap(),
                                 scalar2=None, op0=ALU.mult)
    kb.stage(vector=x_coefm)

    def x_ct(eng):
        return eng.transpose(pbank(1, 0, 16, 0, 8), COEFM, IDF.ap()[0:8, 0:8])
    kb.stage(tensor=x_ct)

    def x_cte(eng):
        return eng.tensor_copy(out=COEFT, in_=pbank(1, 0, 16, 0, 8))
    kb.stage(vector=x_cte)

    for j in range(8):
        def x_rf(eng, j=j):
            return dseq(eng, [(RFIN, cc_recv[j * 65:j * 65 + 64, :])])
        kb.stage(sync=x_rf)

        def x_bc(eng, j=j):
            eng.matmul(pbank(2, 0, 64, 0, 512),
                       COEFT[:, j:j + 1].broadcast_to([16, 64]),
                       REP[:, 0:512], start=True, stop=True)
            return eng.matmul(pbank(3, 0, 64, 0, 512),
                              COEFT[:, j:j + 1].broadcast_to([16, 64]),
                              REP[:, 512:1024], start=True, stop=True)
        kb.stage(tensor=x_bc)

        def x_acc(eng, j=j):
            eng.tensor_tensor(out=MSKT[0:64, 0:512],
                              in0=pbank(2, 0, 64, 0, 512),
                              in1=RFIN[:, 0:512], op=ALU.mult)
            eng.tensor_tensor(out=MSKT[0:64, 512:1024],
                              in0=pbank(3, 0, 64, 0, 512),
                              in1=RFIN[:, 512:1024], op=ALU.mult)
            return eng.tensor_tensor(out=ACC, in0=ACC, in1=MSKT[0:64, :],
                                     op=ALU.add)
        kb.stage(vector=x_acc)

    def x_cast(eng):
        return eng.tensor_copy(out=SINIT.ap()[0:64, :], in_=ACC)
    kb.stage(vector=x_cast)

    if dbg:
        def x_dbg(eng):
            pairs = []
            for jj in range(8):
                pairs.append((dbg_s[jj * 65:(jj + 1) * 65, :],
                              cc_recv[jj * 65:(jj + 1) * 65, :]))
            return dseq(eng, pairs)
        kb.stage(sync=x_dbg)

    def x_cast2(eng):
        eng.dma_batch([(SINIT.ap()[64:128, :], SINIT.ap()[0:64, :])])
        return "self"
    kb.stage(gpsimd=x_cast2)

    for dcz in range(DC):
        def w_l(eng, dcz=dcz):
            return dseq(eng, [(WSTG, woutt_d[dcz * 128:(dcz + 1) * 128, :])])
        def w_c(eng, dcz=dcz):
            return eng.copy(out=WOUTT[:, dcz * D:(dcz + 1) * D], in_=WSTG)
        kb.stage(sync=w_l)
        kb.stage(scalar=w_c)

    marks['exch'] = len(kb.stages)
    XR = r4(0, 1024)
    OFULL = r4(1024, 1024)
    T2 = r4(2048, 1024)
    X2T = r4(3072, 1024)
    LN2T = r4(4096, 1024)
    QSA = r4(5120, 1024)
    QSB = r4(6144, 1024)
    QSC = r4(7168, 1024)
    CBM = r4(2048, 64)
    CBM2 = r4(2112, 64)
    OIR = r5(0, 1024)
    QNR = r5(1024, 1024)
    ZR = r5(2048, 1024)
    SLZ = r5(3072, 1024)
    OGD = r5(4096, 1024)
    OGF = r5(5120, 1024)
    XQB = r5(6144, 1024)

    for c in range(NCH):
        def c_s1(eng, c=c):
            return dseq(eng, [
                (OIR, sp_oi[c * 128:(c + 1) * 128, :]),
                (QNR, sp_qn[c * 128:(c + 1) * 128, :]),
                (ZR, sp_z[c * 128:(c + 1) * 128, :]),
                (XR, x_d[c * 128:(c + 1) * 128, :]),
            ])
        kb.stage(sync=c_s1)

        def c_s2(eng):
            last = None
            for h in range(H):
                qc, hh = h // 2, h % 2
                last = eng.matmul(
                    pbank(h % 2, 0, 128, (h // 2) * 64, 64),
                    QNR[hh * 64:(hh + 1) * 64, qc * 128:(qc + 1) * 128],
                    SINIT.ap()[hh * 64:(hh + 1) * 64, h * 64:(h + 1) * 64],
                    start=True, stop=True)
            return last
        kb.stage(tensor=c_s2)

        def c_s3(eng, c=c):
            last = eng.activation(out=SLZ, in_=ZR, func=AF.Silu)
            for h in range(H):
                last = eng.activation(
                    out=T2[:, h * 64:(h + 1) * 64],
                    in_=pbank(h % 2, 0, 128, (h // 2) * 64, 64),
                    func=AF.Copy, scale=ELGD.ap()[:, c * H + h:c * H + h + 1])
            return last
        kb.stage(scalar=c_s3)

        def c_s4(eng):
            return eng.tensor_tensor(out=OFULL, in0=T2, in1=OIR, op=ALU.add)
        kb.stage(vector=c_s4)

        def c_s5(eng):
            last = None
            for h in range(H):
                last = eng.activation(out=T2[:, 0:64],
                                      in_=OFULL[:, h * 64:(h + 1) * 64],
                                      func=AF.Square,
                                      accum_out=MSQ.ap()[:, h:h + 1])
            return last
        kb.stage(scalar=c_s5)

        def c_s6(eng):
            return eng.tensor_scalar(out=RSTD.ap(), in0=MSQ.ap(),
                                     scalar1=1.0 / HD, scalar2=1e-6,
                                     op0=ALU.mult, op1=ALU.add)
        kb.stage(vector=c_s6)

        def c_s7(eng):
            eng.activation(out=MSQ.ap(), in_=RSTD.ap(), func=AF.Ln)
            return eng.activation(out=RSTD.ap(), in_=MSQ.ap(), func=AF.Exp,
                                  scale=-0.5)
        kb.stage(scalar=c_s7)

        def c_s8(eng):
            last = None
            for h in range(H):
                last = eng.scalar_tensor_tensor(
                    out=OGD[:, h * 64:(h + 1) * 64],
                    in0=OFULL[:, h * 64:(h + 1) * 64],
                    scalar=RSTD.ap()[:, h:h + 1],
                    in1=SLZ[:, h * 64:(h + 1) * 64],
                    op0=ALU.mult, op1=ALU.mult)
            return last
        kb.stage(vector=c_s8)

        def c_s9(eng):
            last = None
            for dc in range(DC):
                last = eng.transpose(pbank(dc, 0, 128, 0, 64, BF16),
                                     OGD[:, dc * 128:(dc + 1) * 128],
                                     IDB.ap())
            return last
        kb.stage(tensor=c_s9)

        def c_s10(eng):
            last = None
            with eng.parallel():
             for dc in range(DC):
                last = eng.tensor_copy(out=OGF[:, dc * 128:(dc + 1) * 128],
                                       in_=pbank(dc, 0, 128, 0, 64, BF16))
            return last
        kb.stage(vector=c_s10)

        def c_s11(eng):
            last = None
            for dc in range(DC):
                for pi in range(2):
                    last = eng.matmul(
                        pbank(4 + pi, 0, 128, 0, 512),
                        OGF[:, dc * 128:(dc + 1) * 128],
                        WOUTT[:, dc * D + pi * 512:dc * D + (pi + 1) * 512],
                        start=(dc == 0), stop=(dc == DC - 1))
            return last
        kb.stage(tensor=c_s11)

        def c_s12(eng):
            with eng.parallel():
                eng.tensor_tensor(out=X2T[:, 0:512],
                                  in0=pbank(4, 0, 128, 0, 512),
                                  in1=XR[:, 0:512], op=ALU.add)
                last = eng.tensor_tensor(out=X2T[:, 512:1024],
                                         in0=pbank(5, 0, 128, 0, 512),
                                         in1=XR[:, 512:1024], op=ALU.add)
            return last
        kb.stage(vector=c_s12)

        def c_s13(eng):
            with eng.parallel():
                eng.activation(out=T2[:, 0:512], in_=X2T[:, 0:512],
                               func=AF.Copy, accum_out=STA.ap()[:, 2:3])
                eng.activation(out=T2[:, 512:1024], in_=X2T[:, 512:1024],
                               func=AF.Copy, accum_out=STA.ap()[:, 3:4])
                eng.activation(out=LN2T[:, 0:512], in_=X2T[:, 0:512],
                               func=AF.Square, accum_out=STA.ap()[:, 4:5])
                last = eng.activation(out=LN2T[:, 512:1024],
                                      in_=X2T[:, 512:1024],
                                      func=AF.Square,
                                      accum_out=STA.ap()[:, 5:6])
            return last
        kb.stage(scalar=c_s13)

        def c_s14(eng):
            with eng.parallel():
                eng.tensor_tensor(out=STA.ap()[:, 0:1], in0=STA.ap()[:, 2:3],
                                  in1=STA.ap()[:, 3:4], op=ALU.add)
                eng.tensor_tensor(out=STA.ap()[:, 1:2], in0=STA.ap()[:, 4:5],
                                  in1=STA.ap()[:, 5:6], op=ALU.add)
            eng.tensor_scalar(out=STA.ap()[:, 2:3], in0=STA.ap()[:, 0:1],
                              scalar1=1.0 / D, scalar2=None, op0=ALU.mult)
            eng.tensor_scalar(out=STA.ap()[:, 3:4], in0=STA.ap()[:, 1:2],
                              scalar1=1.0 / D, scalar2=None, op0=ALU.mult)
            eng.tensor_tensor(out=STA.ap()[:, 4:5], in0=STA.ap()[:, 2:3],
                              in1=STA.ap()[:, 2:3], op=ALU.mult)
            return eng.tensor_tensor(out=STA.ap()[:, 5:6],
                                     in0=STA.ap()[:, 3:4],
                                     in1=STA.ap()[:, 4:5], op=ALU.subtract)
        kb.stage(vector=c_s14)

        def c_s15(eng):
            eng.activation(out=STA.ap()[:, 6:7], in_=STA.ap()[:, 5:6],
                           func=AF.Ln, bias=CCOL.ap()[:, 0:1])
            return eng.activation(out=STA.ap()[:, 6:7], in_=STA.ap()[:, 6:7],
                                  func=AF.Exp, scale=-0.5)
        kb.stage(scalar=c_s15)

        def c_s16(eng):
            return eng.scalar_tensor_tensor(
                out=STA.ap()[:, 7:8], in0=STA.ap()[:, 2:3], scalar=-1.0,
                in1=STA.ap()[:, 6:7], op0=ALU.mult, op1=ALU.mult)
        kb.stage(vector=c_s16)

        def c_s17(eng):
            return eng.tensor_scalar(out=LN2T, in0=X2T,
                                     scalar1=STA.ap()[:, 6:7],
                                     scalar2=STA.ap()[:, 7:8],
                                     op0=ALU.mult, op1=ALU.add)
        kb.stage(vector=c_s17)

        def c_s18(eng):
            return quant_chain(eng, LN2T, D, XQB, QSA, QSB, QSC, CBM, CBM2)
        kb.stage(vector=c_s18)

        def c_s19(eng):
            last = None
            for dc in range(DC):
                last = eng.transpose(pbank(dc, 0, 128, 0, 64, BF16),
                                     XQB[:, dc * 128:(dc + 1) * 128],
                                     IDB.ap())
            return last
        kb.stage(tensor=c_s19)

        def c_s20(eng):
            last = None
            with eng.parallel():
             for dc in range(DC):
                last = eng.tensor_copy(out=OGF[:, dc * 128:(dc + 1) * 128],
                                       in_=pbank(dc, 0, 128, 0, 64, BF16))
            return last
        kb.stage(vector=c_s20)

        def c_s21(eng, c=c):
            return dseq(eng, [
                (sp_xqf[c * 128:(c + 1) * 128, :], OGF),
                (sp_x2[c * 128:(c + 1) * 128, :], X2T),
            ])
        kb.stage(sync=c_s21)

    marks['c'] = len(kb.stages)
    emit_fq("tail")

    marks['fq'] = len(kb.stages)
    DX2R = r4(0, 1024)
    DHB = r4(1024, 2816)
    DSA = r4(3840, 1408)
    DSB = r4(5248, 1408)
    DSC = r4(6656, 1408)
    DBM = r4(8064, 88)
    DBM2 = R5.ap().bitcast(F32)[:, 4280:4368]
    XQFR = r5(0, 1024)
    HQ = r5(1024, 2816)
    HQFD = r5(3840, 2816)
    OUTT = R5.ap().bitcast(F32)[:, 3328:4352]
    DBM2 = R5.ap().bitcast(F32)[:, 4352:4440]

    DSL = r4(7712, 352)
    tr_groups = [(0, 4), (4, 8), (8, 11), (11, 15), (15, 19), (19, 22)]

    def mk_mm(np_):
        b = (np_ % 2) * 2
        def d_mm(eng, np_=np_, b=b):
            last = None
            for dc in range(DC):
                last = eng.matmul(
                    pbank(b, 0, 128, 0, 352),
                    XQFR[:, dc * 128:(dc + 1) * 128],
                    wgT_q[:, dc * IF + np_ * 352:
                          dc * IF + (np_ + 1) * 352],
                    start=(dc == 0), stop=(dc == DC - 1))
                last = eng.matmul(
                    pbank(b + 1, 0, 128, 0, 352),
                    XQFR[:, dc * 128:(dc + 1) * 128],
                    wuT_q[:, dc * IF + np_ * 352:
                          dc * IF + (np_ + 1) * 352],
                    start=(dc == 0), stop=(dc == DC - 1))
            return last
        return d_mm

    def mk_si(np_):
        b = (np_ % 2) * 2
        def d_si(eng, np_=np_, b=b):
            return eng.activation(out=DSL, in_=pbank(b, 0, 128, 0, 352),
                                  func=AF.Silu)
        return d_si

    def mk_h(np_):
        b = (np_ % 2) * 2
        def d_h(eng, np_=np_, b=b):
            return eng.tensor_tensor(
                out=DHB[:, np_ * 352:(np_ + 1) * 352],
                in0=DSL, in1=pbank(b + 1, 0, 128, 0, 352), op=ALU.mult)
        return d_h

    def mk_tr(k0, k1):
        def d_tr(eng, k0=k0, k1=k1):
            last = None
            for k in range(k0, k1):
                last = eng.transpose(
                    pbank(4 + (k - k0) % 4, 0, 128, 0, 64, BF16),
                    HQ[:, k * 128:(k + 1) * 128], IDB.ap())
            return last
        return d_tr

    def mk_te(k0, k1):
        def d_te(eng, k0=k0, k1=k1):
            last = None
            with eng.parallel():
             for k in range(k0, k1):
                last = eng.tensor_copy(
                    out=HQFD[:, k * 128:(k + 1) * 128],
                    in_=pbank(4 + (k - k0) % 4, 0, 128, 0, 64, BF16))
            return last
        return d_te

    def d_dn(eng):
        last = None
        for k in range(22):
            for pi in range(2):
                last = eng.matmul(pbank(4 + pi, 0, 128, 0, 512),
                                  HQFD[:, k * 128:(k + 1) * 128],
                                  wdT_chunk(k)[:, pi * 512:(pi + 1) * 512],
                                  start=(k == 0), stop=(k == 21))
        return last

    def d_out(eng):
        with eng.parallel():
            eng.tensor_tensor(out=OUTT[:, 0:512],
                              in0=pbank(4, 0, 128, 0, 512),
                              in1=DX2R[:, 0:512], op=ALU.add)
            last = eng.tensor_tensor(out=OUTT[:, 512:1024],
                                     in0=pbank(5, 0, 128, 0, 512),
                                     in1=DX2R[:, 512:1024], op=ALU.add)
        return last

    def emit_tail_tr(pv, c):
        first = pv.get("out") is None
        last_te = last_tr = None
        for gi, (k0, k1) in enumerate(tr_groups):
            if gi == 0:
                a_tr = pv["q1"] if first else {"vector": pv["out"]}
            else:
                a_tr = last_te if first else {"vector": last_te}
            last_tr = kb.stage(tensor=mk_tr(k0, k1), after_=a_tr)
            last_te = kb.stage(vector=mk_te(k0, k1),
                               after_=(last_tr if first
                                       else {"tensor": last_tr}))
        return last_tr, last_te

    def emit_tail_dn(pv, c, last_te):
        first = pv.get("st") is None
        i_dn = kb.stage(tensor=d_dn,
                        after_=(last_te if first
                                else {"vector": last_te}))
        if first:
            a_out = i_dn
        else:
            a_out = {"tensor": i_dn,
                     "sync": max(pv["s1b"], pv["st"])}
        i_out = kb.stage(vector=d_out, after_=a_out)

        def d_st(eng, c=c):
            return dseq(eng, [(y_d[c * 128:(c + 1) * 128, :], OUTT)])
        i_st = kb.stage(sync=d_st,
                        after_=(i_out if first else {"vector": i_out}))
        return i_out, i_st

    pv = None
    for c in range(NCH):
        def d_s1(eng, c=c):
            return dseq(eng, [(XQFR, sp_xqf[c * 128:(c + 1) * 128, :])])
        i_s1 = kb.stage(sync=d_s1,
                        after_=(None if pv is None
                                else {"tensor": pv["mm7"]}))

        i_mm = [None] * 8
        i_si = [None] * 8
        i_h = [None] * 8
        for np_ in range(8):
            if pv is None:
                amm = i_s1 if np_ == 0 else (
                    i_mm[0] if np_ == 1 else i_h[np_ - 2])
            elif np_ == 0:
                amm = {"sync": i_s1, "scalar": pv["si6"],
                       "vector": pv["h6"]}
            elif np_ == 1:
                amm = {"sync": i_s1, "scalar": pv["si7"],
                       "vector": pv["h7"]}
            else:
                amm = {"sync": i_s1, "scalar": i_si[np_ - 2],
                       "vector": i_h[np_ - 2]}
            i_mm[np_] = kb.stage(tensor=mk_mm(np_), after_=amm)
            if pv is None:
                asi = (max(i_h[np_ - 1], i_mm[np_]) if np_ > 0
                       else i_mm[0])
            elif np_ == 0:
                asi = {"tensor": i_mm[0], "vector": pv["q1"]}
            else:
                asi = {"tensor": i_mm[np_], "vector": i_h[np_ - 1]}
            i_si[np_] = kb.stage(scalar=mk_si(np_), after_=asi)
            ah = (i_si[np_] if pv is None else
                  {"scalar": i_si[np_], "tensor": i_mm[np_]})
            i_h[np_] = kb.stage(vector=mk_h(np_), after_=ah)

        if pv is not None:
            last_tr, last_te = emit_tail_tr(pv, c - 1)
        else:
            last_tr = last_te = None

        def d_q0(eng):
            return quant_chain(eng, DHB[:, 0:1408], 1408, HQ[:, 0:1408],
                               DSA, DSB, DSC, DBM, DBM2)
        i_q0 = kb.stage(vector=d_q0,
                        after_=(i_h[3] if pv is None
                                else {"tensor": last_tr}))

        def d_q1(eng):
            return quant_chain(eng, DHB[:, 1408:2816], 1408,
                               HQ[:, 1408:2816], DSA, DSB, DSC, DBM, DBM2)
        i_q1 = kb.stage(vector=d_q1,
                        after_=(i_h[7] if pv is None else {}))

        if pv is not None:
            i_out, i_st = emit_tail_dn(pv, c - 1, last_te)
        else:
            i_out = i_st = None

        def d_s1b(eng, c=c):
            return dseq(eng, [(DX2R, sp_x2[c * 128:(c + 1) * 128, :])])
        i_s1b = kb.stage(sync=d_s1b,
                         after_=(i_h[7] if pv is None
                                 else {"vector": i_out}))

        pv = {"mm7": i_mm[7], "si6": i_si[6], "si7": i_si[7],
              "h6": i_h[6], "h7": i_h[7], "q1": i_q1,
              "out": i_out, "st": i_st, "s1b": i_s1b}

    last_tr, last_te = emit_tail_tr(pv, NCH - 1)
    emit_tail_dn(pv, NCH - 1, last_te)

    marks['all'] = len(kb.stages)
    if max_stage is not None:
        if isinstance(max_stage, str):
            max_stage = marks[max_stage]
        kb.stages = kb.stages[:max_stage]
    nc._phase_marks = marks
    stgs = {n: st.enter_context(nc.semaphore("st_" + n))
            for n in ("sync", "scalar", "vector", "tensor", "gpsimd")}
    with nc.Block() as block:
        block.sync(kb.body(stgs, esems["sync"], "sync"))
        block.scalar(kb.body(stgs, esems["scalar"], "scalar"))
        block.vector(kb.body(stgs, esems["vector"], "vector"))
        block.tensor(kb.body(stgs, esems["tensor"], "tensor"))
        block.gpsimd(kb.body(stgs, esems["gpsimd"], "gpsimd"))

    st.close()
    return nc



def kernel(L0_attn_x, L0_attn_W_qkv, L0_attn_conv_w, L0_attn_W_z, L0_attn_W_b,
           L0_attn_W_a, L0_attn_dt_bias, L0_attn_A_log, L0_attn_W_out,
           L0_ffn_w_gate, L0_ffn_w_up, L0_ffn_w_down):
    x = np.ascontiguousarray(np.asarray(L0_attn_x, np.float32).reshape(T, D))
    wqkv = np.asarray(L0_attn_W_qkv, np.float32)
    convw = np.ascontiguousarray(
        np.asarray(L0_attn_conv_w, np.float32).reshape(CONV, 4))
    wz = np.asarray(L0_attn_W_z, np.float32)
    wb = np.asarray(L0_attn_W_b, np.float32)
    wa = np.asarray(L0_attn_W_a, np.float32)
    dtb = np.asarray(L0_attn_dt_bias, np.float32)
    alog = np.asarray(L0_attn_A_log, np.float32)
    wout = np.asarray(L0_attn_W_out, np.float32)
    wg = np.ascontiguousarray(np.asarray(L0_ffn_w_gate, np.float32))
    wu = np.ascontiguousarray(np.asarray(L0_ffn_w_up, np.float32))
    wd = np.ascontiguousarray(np.asarray(L0_ffn_w_down, np.float32))

    wqkvt = np.ascontiguousarray(wqkv.T)
    wqkvt_tiled = np.ascontiguousarray(
        wqkvt.reshape(DC, 128, CCH, 128).transpose(2, 0, 1, 3)
    ).reshape(CCH * DC * 128, 128)
    wzt = np.ascontiguousarray(wz.T)
    woutt = np.ascontiguousarray(wout.T)
    wbat = np.zeros((D + 1, 32), np.float32)
    wbat[:D, 0:16] = wb.T
    wbat[:D, 16:32] = wa.T
    wbat[D, 16:32] = dtb

    s_ar = np.arange(128)
    tri = (s_ar[:, None] <= s_ar[None, :]).astype(np.float32)
    ones = np.ones((128, 128), np.float32)
    identf = np.eye(128, dtype=np.float32)
    identb = np.eye(128).astype(BF)
    i16 = np.eye(16, dtype=np.float32)
    maskb = np.where(s_ar[:, None] <= s_ar[None, :], 0xFFFF, 0).astype(np.uint16)
    bones = np.zeros((128, 8 * 16), np.float32)
    for qc in range(8):
        for p in range(128):
            bones[p, qc * 16 + 2 * qc + (1 if p >= 64 else 0)] = 1
    bcastp = np.zeros((16, 8 * 128), np.float32)
    for qc in range(8):
        for p in range(128):
            bcastp[2 * qc + (1 if p >= 64 else 0), qc * 128 + p] = 1
    rep16 = np.zeros((16, D), np.float32)
    for h in range(16):
        rep16[h, h * 64:(h + 1) * 64] = 1
    su8 = (np.arange(8)[:, None] > np.arange(8)[None, :]).astype(np.float32)

    shared = dict(
        wqkvt=wqkvt_tiled, wzt=wzt, wbat=wbat, alog=alog.reshape(1, H),
        convw=convw, woutt=woutt, wg=wg, wu=wu, wd=wd,
        tri=tri, ones=ones, identf=identf, identb=identb, i16=i16,
        maskb=maskb, bones=bones.astype(BF), bcastp=bcastp.astype(BF),
        rep16=rep16, su8=su8,
    )

    in_maps = []
    for r in range(NCORE):
        selv = np.zeros((128, 8), np.float32)
        selv[:, r] = 1.0
        sellt = (np.arange(8) < r).astype(np.float32).reshape(8, 1)
        invsel = np.tile(1.0 - sellt, (1, 16)).astype(np.float32)
        xprev = np.zeros((3, D), np.float32)
        if r > 0:
            xprev = x[r * TLOC - 3:r * TLOC]
        m = dict(shared)
        m.update(x=np.ascontiguousarray(x[r * TLOC:(r + 1) * TLOC]),
                 xprev=np.ascontiguousarray(xprev),
                 selv=selv, sellt=sellt, invsel=invsel)
        in_maps.append(m)

    nc = build_program()
    res = run_bass_kernel_spmd(nc, in_maps, core_ids=list(range(NCORE)))
    out = np.concatenate(
        [np.asarray(res.results[r]["y"]) for r in range(NCORE)], axis=0)
    return out.reshape(1, T, D).astype(np.float32)


if __name__ == "__main__":
    import sys
    if "--build" in sys.argv:
        nc = build_program()
        n = sum(1 for _ in nc.all_instructions())
        print("built ok, instructions:", n)


# version 17
# speedup vs baseline: 1.1084x; 1.0495x over previous
import numpy as np
import ml_dtypes
import concourse.bass as bass
import concourse.mybir as mybir
from concourse.bass_utils import run_bass_kernel_spmd
from contextlib import ExitStack

F32 = mybir.dt.float32
BF16 = mybir.dt.bfloat16
U16 = mybir.dt.uint16
AF = mybir.ActivationFunctionType
ALU = mybir.AluOpType
AX = mybir.AxisListType

NCORE = 8
T = 16384
D = 1024
H = 16
HD = 64
CONV = 3072
IF = 2816
TLOC = T // NCORE
CK = 128
NCH = TLOC // CK
DC = D // 128
CCH = CONV // 128
LNW = TLOC + 3
MAGIC = float(1.5 * 2.0 ** 23)
LN8 = float(np.log(8.0))
BF = ml_dtypes.bfloat16


class SyncEng:

    _PASS = {"wait_ge", "sem_inc", "then_inc"}

    def __init__(self, eng, sem, selfsync=True):
        self.eng = eng
        self.sem = sem
        self.selfsync = selfsync
        self.count = 0
        self.par = False
        self.pending = None

    def _flush_pending(self):
        if self.pending is not None:
            inst, amt = self.pending
            inst.then_inc(self.sem, amt)
            self.count += amt
            self.pending = None
            return True
        return False

    def _pre_op(self):
        if not self.selfsync:
            return
        if self._flush_pending():
            if not self.par:
                self.eng.wait_ge(self.sem, self.count)

    def parallel(self):
        import contextlib

        @contextlib.contextmanager
        def cm():
            self._pre_op()
            self.par = True
            try:
                yield
            finally:
                self.par = False
        return cm()

    def close_stage(self, stg_sem, inc):
        assert self.pending is not None, "stage had no ops?"
        inst, _ = self.pending
        inst.then_inc(stg_sem, inc)
        self.pending = None

    def dma_batch(self, pairs):
        self._flush_pending()
        for dst, srcp in pairs:
            self.eng.dma_start(out=dst, in_=srcp).then_inc(self.sem, 16)
            self.count += 16
        self.eng.wait_ge(self.sem, self.count)

    def collective(self, *a, **kw):
        self._flush_pending()
        self.eng.collective_compute(*a, **kw).then_inc(self.sem, 1)
        self.count += 1
        self.eng.wait_ge(self.sem, self.count)

    def collective_async(self, done_sem, *a, **kw):
        self._flush_pending()
        self.eng.collective_compute(*a, **kw).then_inc(done_sem, 1)

    def wait_ge(self, *a, **kw):
        return self.eng.wait_ge(*a, **kw)

    def sem_inc(self, *a, **kw):
        return self.eng.sem_inc(*a, **kw)

    def __getattr__(self, name):
        f = getattr(self.eng, name)
        if not callable(f) or name.startswith("_"):
            return f

        def wrapped(*a, **kw):
            self._pre_op()
            inst = f(*a, **kw)
            amt = 16 if name == "dma_start" else 1
            self.pending = (inst, amt)
            return inst
        return wrapped


class KB:

    def __init__(self):
        self.stages = []
        self.afters = []

    def stage(self, after_=None, **fns):
        self.stages.append(fns)
        self.afters.append(after_)
        return len(self.stages) - 1

    def body(self, stg_sems, esem, eng_name):
        engs = list(stg_sems.keys())
        ncum = {e: [] for e in engs}
        cnt = {e: 0 for e in engs}
        for s in self.stages:
            for e in engs:
                if e in s:
                    cnt[e] += 1
                ncum[e].append(cnt[e])

        def run(eng):
            w = SyncEng(eng, esem,
                        selfsync=(eng_name not in ("tensor", "sync")))
            waited = {e: 0 for e in engs}

            def waitup(e, idx):
                need = ncum[e][idx]
                if need > waited[e]:
                    eng.wait_ge(stg_sems[e], need)
                    waited[e] = need

            for si, s in enumerate(self.stages):
                if eng_name not in s:
                    continue
                a = self.afters[si]
                if isinstance(a, dict):
                    if w.selfsync and si > 0:
                        waitup(eng_name, si - 1)
                    for e, idx in a.items():
                        if idx is not None and idx >= 0:
                            waitup(e, idx)
                else:
                    if a is None:
                        a = si - 1
                    if a >= 0:
                        for e in engs:
                            waitup(e, a)
                res = s[eng_name](w)
                if res == "self":
                    eng.sem_inc(stg_sems[eng_name], 1)
                else:
                    w.close_stage(stg_sems[eng_name], 1)
        return run


def build_program(max_stage=None, dbg=False, nocc=False):
    nc = bass.Bass("TRN2", target_bir_lowering=False,
                   dynamic_dma_scratch_size=4096)

    def din(name, shape, dtype=F32):
        return nc.dram_tensor(name, shape, dtype, kind="ExternalInput").ap()

    x_d = din("x", [TLOC, D])
    xprev_d = din("xprev", [3, D])
    selv_d = din("selv", [128, 8])
    sellt_d = din("sellt", [8, 1])
    invsel_d = din("invsel", [8, 16])
    wqkvt_d = din("wqkvt", [CCH * DC * 128, 128])
    wzt_d = din("wzt", [D, D])
    wbat_d = din("wbat", [D + 1, 32])
    alog_d = din("alog", [1, H])
    convw_d = din("convw", [CONV, 4])
    woutt_d = din("woutt", [D, D])
    wg_d = din("wg", [IF, D])
    wu_d = din("wu", [IF, D])
    wd_d = din("wd", [D, IF])
    tri_d = din("tri", [128, 128])
    ones_d = din("ones", [128, 128])
    identf_d = din("identf", [128, 128])
    identb_d = din("identb", [128, 128], BF16)
    i16_d = din("i16", [16, 16])
    maskb_d = din("maskb", [128, 128], U16)
    bones_d = din("bones", [128, 8 * 16], BF16)
    bcastp_d = din("bcastp", [16, 8 * 128], BF16)
    rep16_d = din("rep16", [16, D])
    su8_d = din("su8", [8, 8])

    y_d = nc.dram_tensor("y", [TLOC, D], F32, kind="ExternalOutput").ap()

    spk = "ExternalOutput" if dbg else "Internal"
    sp_oi = nc.dram_tensor("sp_oi", [TLOC, D], BF16, kind=spk).ap()
    sp_qn = nc.dram_tensor("sp_qn", [TLOC, D], BF16, kind=spk).ap()
    sp_z = nc.dram_tensor("sp_z", [TLOC, D], BF16, kind=spk).ap()
    sp_x2 = nc.dram_tensor("sp_x2", [TLOC, D], F32, kind=spk).ap()
    sp_xqf = nc.dram_tensor("sp_xqf", [TLOC, D], BF16, kind=spk).ap()
    if dbg:
        dbg_s = nc.dram_tensor("dbg_s", [8 * 65, D], F32,
                               kind="ExternalOutput").ap()
        dbg_a = nc.dram_tensor("dbg_a", [128, 8192], BF16,
                               kind="ExternalOutput").ap()
        dbg_b = nc.dram_tensor("dbg_b", [128, 64], F32,
                               kind="ExternalOutput").ap()
        dbg_c = nc.dram_tensor("dbg_c", [16, 128], F32,
                               kind="ExternalOutput").ap()
        dbg_d = nc.dram_tensor("dbg_d", [16, 256], BF16,
                               kind="ExternalOutput").ap()
        dbg_e = nc.dram_tensor("dbg_e", [128, 160], F32,
                               kind="ExternalOutput").ap()
    cc_send = nc.dram_tensor("cc_send", [8 * 65, D], F32, kind="Internal").ap()
    cc_recv = nc.dram_tensor("cc_recv", [8 * 65, D], F32, kind="Internal").ap()

    kb = KB()
    marks = {}
    st = ExitStack()

    def sb(name, shape, dtype=F32):
        return st.enter_context(nc.sbuf_tensor(name, shape, dtype))

    R1 = sb("R1", [128, CCH * LNW], BF16)
    R2 = sb("R2", [128, DC * LNW], BF16)
    R3 = sb("R3", [128, DC * D], BF16)
    R4 = sb("R4", [128, 8192])
    R5 = sb("R5", [128, 10240], BF16)

    def r4(c0, cn, p0=0, pn=128):
        return R4.ap()[p0:p0 + pn, c0:c0 + cn]

    def r5(c0, cn, p0=0, pn=128):
        return R5.ap()[p0:p0 + pn, c0:c0 + cn]

    MIXED = R1.ap()
    LN1 = R2.ap()
    WZT = R3.ap()
    WOUTT = R3.ap()
    wgT_q = R1.ap()[:, 0:8 * IF]
    wuT_q = R1.ap()[:, 8 * IF:16 * IF]

    def wdT_chunk(k):
        if k < 4:
            return R1.ap()[:, 16 * IF + k * D:16 * IF + (k + 1) * D]
        if k < 20:
            return R2.ap()[:, (k - 4) * D:(k - 3) * D]
        return R3.ap()[:, (k - 20) * D:(k - 19) * D]

    TRI = sb("TRI", [128, 128])
    ONES = sb("ONES", [128, 128])
    IDF = sb("IDF", [128, 128])
    IDB = sb("IDB", [128, 128], BF16)
    I16 = sb("I16", [16, 16])
    MASKB = sb("MASKB", [128, 128], U16)
    BONES = sb("BONES", [128, 8 * 16], BF16)
    BCASTP = sb("BCASTP", [16, 8 * 128], BF16)
    CONVW = sb("CONVW", [128, CCH * 4])
    WBAT = sb("WBAT", [128, 9 * 32], BF16)
    ALOG = sb("ALOG", [1, H])
    SELV = sb("SELV", [128, 8])
    SELLT = sb("SELLT", [8, 1])
    INVSEL = sb("INVSEL", [8, 16])
    ELGD = sb("ELGD", [128, NCH * H])
    EXPL = [sb("EXPL0", [128, H]), sb("EXPL1", [128, H])]
    LGDR = [sb("LGDR0", [1, H]), sb("LGDR1", [1, H])]
    NEGEA = sb("NEGEA", [128, H])
    SLOC = [sb("SLOC0", [128, H * HD], BF16), sb("SLOC1", [128, H * HD], BF16)]
    SINIT = sb("SINIT", [128, H * HD], BF16)
    BETA = sb("BETA", [128, H])
    TMP1 = sb("TMP1", [128, H])
    TMP2 = sb("TMP2", [128, H])
    TMP3 = sb("TMP3", [128, H])
    TMP4 = sb("TMP4", [128, H])
    GSB = sb("GSB", [128, H])
    NEGG = sb("NEGG", [128, H])
    GT = sb("GT", [16, 128])
    GTS = sb("GTS", [16, 128])
    E2A = sb("E2A", [128, H])
    E2 = sb("E2", [128, H])
    EXPG = sb("EXPG", [128, H])
    EB128 = sb("EB128", [128, H])
    B128S = sb("B128S", [128, H])
    LQ = sb("LQ", [16, 128])
    LK = sb("LK", [16, 128])
    M2Q = sb("M2Q", [16, 128], BF16)
    M2K = sb("M2K", [16, 128], BF16)
    MSQ = sb("MSQ", [128, H])
    RSTD = sb("RSTD", [128, H])
    STA = sb("STA", [128, 8])
    CCOL = sb("CCOL", [128, 4])
    ONESB = sb("ONESB", [1, 128], BF16)

    SQ = r5(0, 2048)
    SQF = R5.ap().bitcast(F32)[:, 0:1024]
    KH = r5(2048, 1024)
    UTM = r5(3072, 1024)
    QNF = r5(4096, 1024)
    KNF = r5(5120, 1024)
    EM = r5(6144, 2048)
    OIB = r5(8192, 1024)
    ZB = r5(9216, 1024)
    LNT = r5(0, 1024)
    WQSB = r5(1024, 1024)
    CS0 = r5(2048, 2048)
    CS1 = r5(4096, 2048)
    XT = r4(0, 1024)
    WQS = r4(1024, 1024)

    PB = [st.enter_context(nc.psum_tensor(f"PB{i}", [128, 512])) for i in range(8)]

    def pbank(i, p0, pn, c0, cn, dtype=F32):
        v = PB[i].ap()
        if dtype != F32:
            m = 2 if dtype == BF16 else 1
            return v.bitcast(dtype)[p0:p0 + pn, c0 * m:(c0 + cn) * m]
        return v[p0:p0 + pn, c0:c0 + cn]

    stg = st.enter_context(nc.semaphore("stg"))
    ccsem = st.enter_context(nc.semaphore("ccsem"))
    esems = {n: st.enter_context(nc.semaphore("es_" + n))
             for n in ("sync", "scalar", "vector", "tensor", "gpsimd")}

    def dseq(eng, pairs):
        eng.dma_batch(pairs)
        return "self"

    def p_zero(eng):
        return eng.memset(WQS[:, 256:288], 0.0)
    kb.stage(vector=p_zero)

    def p_loads(eng):
        pairs = [
            (TRI.ap(), tri_d), (ONES.ap(), ones_d), (IDF.ap(), identf_d),
            (IDB.ap(), identb_d), (I16.ap(), i16_d), (MASKB.ap(), maskb_d),
            (BONES.ap(), bones_d), (BCASTP.ap(), bcastp_d),
            (ALOG.ap(), alog_d), (SELV.ap(), selv_d), (SELLT.ap(), sellt_d),
            (INVSEL.ap(), invsel_d),
        ]
        for cc in range(CCH):
            pairs.append((CONVW.ap()[:, cc * 4:(cc + 1) * 4],
                          convw_d[cc * 128:(cc + 1) * 128, :]))
        for k in range(8):
            pairs.append((WQS[:, k * 32:(k + 1) * 32],
                          wbat_d[k * 128:(k + 1) * 128, :]))
        pairs.append((WQS[0:1, 256:288], wbat_d[1024:1025, :]))
        return dseq(eng, pairs)
    kb.stage(sync=p_loads)

    def p_init(eng):
        eng.memset(ONESB.ap(), 1.0)
        eng.memset(CCOL.ap()[:, 0:1], 1e-5)
        eng.memset(CCOL.ap()[:, 1:2], 1e-6)
        eng.memset(CCOL.ap()[:, 2:3], -LN8)
        eng.memset(CCOL.ap()[:, 3:4], 1.0)
        eng.memset(EXPL[0].ap(), 1.0)
        eng.memset(LGDR[0].ap(), 0.0)
        eng.memset(SLOC[0].ap(), 0.0)
        return eng.memset(SLOC[1].ap(), 0.0)
    kb.stage(vector=p_init)

    def p_ea1(eng):
        return eng.activation(out=TMP1.ap()[0:1, :], in_=ALOG.ap(), func=AF.Exp)
    kb.stage(scalar=p_ea1)

    def p_ea2(eng):
        return eng.tensor_scalar(out=TMP2.ap()[0:1, :], in0=TMP1.ap()[0:1, :],
                                 scalar1=-1.0, scalar2=None, op0=ALU.mult)
    kb.stage(vector=p_ea2)

    def p_ea3(eng):
        return eng.matmul(pbank(0, 0, 128, 0, 16), ONES.ap()[0:1, :],
                          TMP2.ap()[0:1, :], start=True, stop=True)
    kb.stage(tensor=p_ea3)

    def p_ea4(eng):
        return eng.tensor_copy(out=NEGEA.ap(), in_=pbank(0, 0, 128, 0, 16))
    kb.stage(vector=p_ea4)

    def p_wba(eng):
        return eng.copy(out=WBAT.ap(), in_=WQS[:, 0:288])
    kb.stage(scalar=p_wba)

    for dcz in range(DC):
        def p_wzl(eng, dcz=dcz):
            return dseq(eng, [(WQS, wzt_d[dcz * 128:(dcz + 1) * 128, :])])
        def p_wzc(eng, dcz=dcz):
            return eng.copy(out=WZT[:, dcz * D:(dcz + 1) * D], in_=WQS)
        kb.stage(sync=p_wzl)
        kb.stage(scalar=p_wzc)

    marks['prologue'] = len(kb.stages)
    def ln_tile(rows, col0):
        def s1(eng):
            with eng.parallel():
                eng.activation(out=r4(2048, 1024, 0, rows), in_=XT[0:rows, :],
                               func=AF.Copy, accum_out=STA.ap()[0:rows, 0:1])
                last = eng.activation(out=r4(3072, 1024, 0, rows),
                                      in_=XT[0:rows, :], func=AF.Square,
                                      accum_out=STA.ap()[0:rows, 1:2])
            return last
        kb.stage(scalar=s1)

        def s2(eng):
            eng.tensor_scalar(out=STA.ap()[0:rows, 2:3],
                              in0=STA.ap()[0:rows, 0:1], scalar1=1.0 / D,
                              scalar2=None, op0=ALU.mult)
            eng.tensor_scalar(out=STA.ap()[0:rows, 3:4],
                              in0=STA.ap()[0:rows, 1:2], scalar1=1.0 / D,
                              scalar2=None, op0=ALU.mult)
            eng.tensor_tensor(out=STA.ap()[0:rows, 4:5],
                              in0=STA.ap()[0:rows, 2:3],
                              in1=STA.ap()[0:rows, 2:3], op=ALU.mult)
            return eng.tensor_tensor(out=STA.ap()[0:rows, 5:6],
                                     in0=STA.ap()[0:rows, 3:4],
                                     in1=STA.ap()[0:rows, 4:5],
                                     op=ALU.subtract)
        kb.stage(vector=s2)

        def s3(eng):
            eng.activation(out=STA.ap()[0:rows, 6:7],
                           in_=STA.ap()[0:rows, 5:6], func=AF.Ln,
                           bias=CCOL.ap()[0:rows, 0:1])
            return eng.activation(out=STA.ap()[0:rows, 6:7],
                                  in_=STA.ap()[0:rows, 6:7], func=AF.Exp,
                                  scale=-0.5)
        kb.stage(scalar=s3)

        def s4(eng):
            return eng.scalar_tensor_tensor(
                out=STA.ap()[0:rows, 7:8], in0=STA.ap()[0:rows, 2:3],
                scalar=-1.0, in1=STA.ap()[0:rows, 6:7],
                op0=ALU.mult, op1=ALU.mult)
        kb.stage(vector=s4)

        def s5(eng):
            return eng.tensor_scalar(out=LNT[0:rows, :], in0=XT[0:rows, :],
                                     scalar1=STA.ap()[0:rows, 6:7],
                                     scalar2=STA.ap()[0:rows, 7:8],
                                     op0=ALU.mult, op1=ALU.add)
        kb.stage(vector=s5)

        def s6(eng):
            last = None
            for dc in range(DC):
                last = eng.transpose(pbank(dc, 0, 128, 0, 64, BF16)[:, 0:rows],
                                     LNT[0:rows, dc * 128:(dc + 1) * 128],
                                     IDB.ap()[0:rows, 0:rows])
            return last
        kb.stage(tensor=s6)

        def s7(eng):
            last = None
            with eng.parallel():
                for dc in range(DC):
                    last = eng.tensor_copy(
                        out=LN1[:, dc * LNW + col0:dc * LNW + col0 + rows],
                        in_=pbank(dc, 0, 128, 0, 64, BF16)[:, 0:rows])
            return last
        kb.stage(vector=s7)

    def l_xp(eng):
        return dseq(eng, [(XT[0:3, :], xprev_d)])
    kb.stage(sync=l_xp)
    ln_tile(3, 0)
    for tt in range(NCH):
        def l_x(eng, tt=tt):
            return dseq(eng, [(XT, x_d[tt * 128:(tt + 1) * 128, :])])
        kb.stage(sync=l_x)
        ln_tile(128, 3 + tt * 128)

    marks['ln'] = len(kb.stages)
    npieces = [(0, 512), (512, 512), (1024, 512), (1536, 512), (2048, 3)]
    qv = None
    for cc in range(CCH):
        def q_l(eng, cc=cc):
            return dseq(eng, [(WQS.rearrange("p (dc c) -> p dc c", c=128),
                               wqkvt_d[cc * DC * 128:(cc + 1) * DC * 128, :]
                               .rearrange("(dc p) c -> p dc c", p=128))])
        i_ql = kb.stage(sync=q_l,
                        after_=(None if qv is None
                                else {"scalar": qv["c"]}))

        def q_c(eng):
            return eng.copy(out=WQSB, in_=WQS)
        i_qc = kb.stage(scalar=q_c,
                        after_=(None if qv is None
                                else {"sync": i_ql, "tensor": qv["mm"]}))

        def q_mm(eng, cc=cc):
            last = None
            for dc in range(DC):
                for pi, (c0, cn) in enumerate(npieces):
                    bank = pi if pi < 4 else 4
                    last = eng.matmul(
                        pbank(bank, 0, 128, 0, cn),
                        WQSB[:, dc * 128:(dc + 1) * 128],
                        LN1[:, dc * LNW + c0:dc * LNW + c0 + cn],
                        start=(dc == 0), stop=(dc == DC - 1))
            return last
        i_qmm = kb.stage(tensor=q_mm,
                         after_=(None if qv is None
                                 else {"scalar": i_qc, "vector": qv["ev"]}))

        def q_ev(eng, cc=cc):
            last = None
            with eng.parallel():
                for pi, (c0, cn) in enumerate(npieces):
                    bank = pi if pi < 4 else 4
                    last = eng.tensor_copy(
                        out=MIXED[:, cc * LNW + c0:cc * LNW + c0 + cn],
                        in_=pbank(bank, 0, 128, 0, cn))
            return last
        i_qev = kb.stage(vector=q_ev,
                         after_=(None if qv is None
                                 else {"tensor": i_qmm}))

        def q_cv2(eng, cc=cc):
            base = cc * LNW
            eng.tensor_scalar(out=CS0, in0=MIXED[:, base:base + 2048],
                              scalar1=CONVW.ap()[:, cc * 4:cc * 4 + 1],
                              scalar2=None, op0=ALU.mult)
            eng.scalar_tensor_tensor(
                out=CS1, in0=MIXED[:, base + 1:base + 2049],
                scalar=CONVW.ap()[:, cc * 4 + 1:cc * 4 + 2],
                in1=CS0, op0=ALU.mult, op1=ALU.add)
            eng.scalar_tensor_tensor(
                out=CS0, in0=MIXED[:, base + 2:base + 2050],
                scalar=CONVW.ap()[:, cc * 4 + 2:cc * 4 + 3],
                in1=CS1, op0=ALU.mult, op1=ALU.add)
            return eng.scalar_tensor_tensor(
                out=CS1, in0=MIXED[:, base + 3:base + 2051],
                scalar=CONVW.ap()[:, cc * 4 + 3:cc * 4 + 4],
                in1=CS0, op0=ALU.mult, op1=ALU.add)
        i_cv2 = kb.stage(vector=q_cv2,
                         after_=(None if qv is None
                                 else {"scalar": qv["cv3"]}))

        def q_cv3(eng, cc=cc):
            return eng.activation(out=MIXED[:, cc * LNW:cc * LNW + 2048],
                                  in_=CS1, func=AF.Silu)
        i_cv3 = kb.stage(scalar=q_cv3,
                         after_=(None if qv is None
                                 else {"vector": i_cv2}))
        qv = {"c": i_qc, "mm": i_qmm, "ev": i_qev, "cv3": i_cv3}

    marks['qkv'] = len(kb.stages)
    BETA2 = [BETA, sb("BETAb", [128, H])]
    GSB2 = [GSB, sb("GSBb", [128, H])]
    NEGG2 = [NEGG, sb("NEGGb", [128, H])]
    GT2 = [GT, sb("GTb", [16, 128])]
    E22 = [E2, sb("E2b", [128, H])]
    EXPG2 = [EXPG, sb("EXPGb", [128, H])]
    EB1282 = [EB128, sb("EB128b", [128, H])]
    M2Q2 = [M2Q, sb("M2Qb", [16, 128], BF16)]
    M2K2 = [M2K, sb("M2Kb", [16, 128], BF16)]
    T1F = r4(0, 1024)

    prev_head_last = None
    prev_tail17 = [None, None]
    for c in range(NCH):
        pc = c % 2
        ct = c * 128
        lt = 3 + c * 128
        BETAc, GSBc, NEGGc = BETA2[pc].ap(), GSB2[pc].ap(), NEGG2[pc].ap()
        GTc, E2c, EXPGc = GT2[pc].ap(), E22[pc].ap(), EXPG2[pc].ap()
        EB128c = EB1282[pc].ap()
        M2Qc, M2Kc = M2Q2[pc].ap(), M2K2[pc].ap()

        def a_s1(eng, lt=lt):
            last = None
            for dc in range(DC):
                last = eng.matmul(
                    pbank(6, 0, 128, 0, 32),
                    LN1[:, dc * LNW + lt:dc * LNW + lt + 128],
                    WBAT.ap()[:, dc * 32:(dc + 1) * 32],
                    start=(dc == 0), stop=False)
            return eng.matmul(pbank(6, 0, 128, 0, 32), ONESB.ap(),
                              WBAT.ap()[0:1, 256:288], start=False, stop=True)
        h_first = kb.stage(tensor=a_s1, after_=prev_head_last)

        def a_s2(eng, BETAc=BETAc):
            eng.activation(out=BETAc, in_=pbank(6, 0, 128, 0, 16),
                           func=AF.Sigmoid)
            eng.activation(out=TMP1.ap(), in_=pbank(6, 0, 128, 16, 16),
                           func=AF.Abs)
            eng.activation(out=TMP2.ap(), in_=TMP1.ap(), func=AF.Exp,
                           scale=-1.0)
            eng.activation(out=TMP1.ap(), in_=TMP2.ap(), func=AF.Ln,
                           bias=CCOL.ap()[:, 3:4])
            return eng.activation(out=TMP3.ap(), in_=pbank(6, 0, 128, 16, 16),
                                  func=AF.Relu)
        kb.stage(scalar=a_s2)

        def a_s3(eng):
            eng.tensor_tensor(out=TMP4.ap(), in0=TMP3.ap(), in1=TMP1.ap(),
                              op=ALU.add)
            return eng.tensor_tensor(out=TMP2.ap(), in0=TMP4.ap(),
                                     in1=NEGEA.ap(), op=ALU.mult)
        kb.stage(vector=a_s3)

        def a_s4(eng):
            eng.transpose(pbank(7, 0, 16, 0, 128), TMP2.ap(), IDF.ap())
            eng.matmul(pbank(6, 0, 128, 32, 16), TRI.ap(), TMP2.ap(),
                       start=True, stop=True)
            return eng.matmul(pbank(6, 0, 128, 48, 16), ONES.ap(), TMP2.ap(),
                              start=True, stop=True)
        kb.stage(tensor=a_s4)

        def a_s5v(eng, pc=pc, GSBc=GSBc, NEGGc=NEGGc):
            eng.tensor_copy(out=GSBc, in_=pbank(6, 0, 128, 32, 16))
            with eng.parallel():
                eng.tensor_copy(out=GTS.ap(), in_=pbank(7, 0, 16, 0, 128))
                eng.tensor_copy(out=B128S.ap(), in_=pbank(6, 0, 128, 48, 16))
                eng.tensor_scalar(out=NEGGc, in0=pbank(6, 0, 128, 32, 16),
                                  scalar1=-1.0, scalar2=None, op0=ALU.mult)
                eng.tensor_tensor(out=E2A.ap(), in0=pbank(6, 0, 128, 48, 16),
                                  in1=GSBc, op=ALU.subtract)
                last = eng.tensor_tensor(out=LGDR[1 - pc].ap(),
                                         in0=LGDR[pc].ap(),
                                         in1=pbank(6, 0, 1, 48, 16),
                                         op=ALU.add)
            return last

        def a_s5s(eng, ct=ct):
            last = None
            with eng.parallel():
                for qc in range(16):
                    last = eng.activation(
                        out=SQ[:, qc * 128:(qc + 1) * 128],
                        in_=MIXED[:, qc * LNW + ct:qc * LNW + ct + 128],
                        func=AF.Square)
            return last
        kb.stage(vector=a_s5v, scalar=a_s5s)

        def a_s6v(eng, GTc=GTc):
            return eng.tensor_tensor_scan(out=GTc, data0=GTS.ap(),
                                          data1=GTS.ap(), initial=0.0,
                                          op0=ALU.add, op1=ALU.bypass)

        def a_s6s(eng, GSBc=GSBc, EXPGc=EXPGc, EB128c=EB128c, E2c=E2c):
            with eng.parallel():
                eng.activation(out=EXPGc, in_=GSBc, func=AF.Exp)
                eng.activation(out=EB128c, in_=B128S.ap(), func=AF.Exp)
                last = eng.activation(out=E2c, in_=E2A.ap(), func=AF.Exp)
            return last

        def a_s6t(eng):
            last = None
            for qc in range(8):
                last = eng.matmul(pbank(7, 0, 16, 128, 128),
                                  BONES.ap()[:, qc * 16:(qc + 1) * 16],
                                  SQ[:, qc * 128:(qc + 1) * 128],
                                  start=(qc == 0), stop=(qc == 7))
            for qc in range(8):
                last = eng.matmul(pbank(7, 0, 16, 256, 128),
                                  BONES.ap()[:, qc * 16:(qc + 1) * 16],
                                  SQ[:, (8 + qc) * 128:(9 + qc) * 128],
                                  start=(qc == 0), stop=(qc == 7))
            return last
        kb.stage(vector=a_s6v, scalar=a_s6s, tensor=a_s6t)

        def a_s7s(eng, M2Qc=M2Qc, M2Kc=M2Kc):
            with eng.parallel():
                eng.activation(out=LQ.ap(), in_=pbank(7, 0, 16, 128, 128),
                               func=AF.Ln, bias=CCOL.ap()[0:16, 1:2])
                eng.activation(out=LK.ap(), in_=pbank(7, 0, 16, 256, 128),
                               func=AF.Ln, bias=CCOL.ap()[0:16, 1:2])
            with eng.parallel():
                eng.activation(out=M2Qc, in_=LQ.ap(), func=AF.Exp,
                               scale=-0.5, bias=CCOL.ap()[0:16, 2:3])
                last = eng.activation(out=M2Kc, in_=LK.ap(), func=AF.Exp,
                                      scale=-0.5)
            return last

        def a_s7v(eng, c=c, pc=pc, EXPGc=EXPGc, EB128c=EB128c):
            with eng.parallel():
                eng.tensor_tensor(out=ELGD.ap()[:, c * H:(c + 1) * H],
                                  in0=EXPGc, in1=EXPL[pc].ap(), op=ALU.mult)
                last = eng.tensor_tensor(out=EXPL[1 - pc].ap(),
                                         in0=EXPL[pc].ap(),
                                         in1=EB128c, op=ALU.mult)
            return last
        h_last = kb.stage(scalar=a_s7s, vector=a_s7v)
        prev_head_last = h_last

        def a_s8t(eng, M2Qc=M2Qc, M2Kc=M2Kc):
            last = None
            for qc in range(8):
                last = eng.matmul(pbank(qc % 4, 0, 128, 128 * (qc // 4), 128),
                                  BCASTP.ap()[:, qc * 128:(qc + 1) * 128],
                                  M2Qc, start=True, stop=True)
            for qc in range(8):
                last = eng.matmul(
                    pbank(qc % 4, 0, 128, 256 + 128 * (qc // 4), 128),
                    BCASTP.ap()[:, qc * 128:(qc + 1) * 128],
                    M2Kc, start=True, stop=True)
            return last
        kb.stage(tensor=a_s8t)

        def a_s9v(eng, ct=ct):
            last = None
            with eng.parallel():
             for qc in range(8):
                last = eng.tensor_tensor(
                    out=QNF[:, qc * 128:(qc + 1) * 128],
                    in0=MIXED[:, qc * LNW + ct:qc * LNW + ct + 128],
                    in1=pbank(qc % 4, 0, 128, 128 * (qc // 4), 128),
                    op=ALU.mult)
             for qc in range(8):
                last = eng.tensor_tensor(
                    out=KNF[:, qc * 128:(qc + 1) * 128],
                    in0=MIXED[:, (8 + qc) * LNW + ct:(8 + qc) * LNW + ct + 128],
                    in1=pbank(qc % 4, 0, 128, 256 + 128 * (qc // 4), 128),
                    op=ALU.mult)
            return last
        kb.stage(vector=a_s9v)

        def a_s10t(eng, ct=ct, GTc=GTc):
            last = None
            for h in range(H):
                last = eng.matmul(
                    pbank(h % 4, 0, 128, 128 * (h // 4), 128),
                    I16.ap()[:, h:h + 1].broadcast_to([16, 128]),
                    GTc, start=True, stop=True)
            for qc in range(8):
                last = eng.transpose(pbank(4, 0, 128, qc * 64, 64, BF16),
                                     KNF[:, qc * 128:(qc + 1) * 128],
                                     IDB.ap())
            for qc in range(8):
                last = eng.transpose(
                    pbank(5, 0, 128, qc * 64, 64, BF16),
                    MIXED[:, (16 + qc) * LNW + ct:(16 + qc) * LNW + ct + 128],
                    IDB.ap())
            return last
        kb.stage(tensor=a_s10t)

        def a_s10v(eng, NEGGc=NEGGc):
            last = None
            with eng.parallel():
             for h in range(H):
                last = eng.tensor_scalar(
                    out=r4(2048, 2048)[:, h * 128:(h + 1) * 128],
                    in0=pbank(h % 4, 0, 128, 128 * (h // 4), 128),
                    scalar1=NEGGc[:, h:h + 1], scalar2=0.0,
                    op0=ALU.add, op1=ALU.min)
            return last
        kb.stage(vector=a_s10v)

        def a_s11s(eng, E2c=E2c, BETAc=BETAc):
            last = None
            with eng.parallel():
             for h in range(H):
                last = eng.activation(
                    out=EM[:, h * 128:(h + 1) * 128],
                    in_=r4(2048, 2048)[:, h * 128:(h + 1) * 128],
                    func=AF.Exp)
             for qc in range(8):
                for hh in range(2):
                    h = 2 * qc + hh
                    last = eng.activation(
                        out=KH[:, h * 64:(h + 1) * 64],
                        in_=pbank(4, 0, 128, qc * 64, 64, BF16)
                        [:, hh * 64:(hh + 1) * 64],
                        func=AF.Copy, scale=E2c[:, h:h + 1])
                    last = eng.activation(
                        out=UTM[:, h * 64:(h + 1) * 64],
                        in_=pbank(5, 0, 128, qc * 64, 64, BF16)
                        [:, hh * 64:(hh + 1) * 64],
                        func=AF.Copy, scale=BETAc[:, h:h + 1])
            return last
        kb.stage(scalar=a_s11s)

        def a_s12v(eng):
            last = None
            with eng.parallel():
             for h in range(H):
                v = EM[:, h * 128:(h + 1) * 128].bitcast(U16)
                last = eng.tensor_tensor(out=v, in0=v, in1=MASKB.ap(),
                                         op=ALU.bitwise_and)
            return last
        kb.stage(vector=a_s12v)

        def a_s13t(eng, lt=lt):
            last = None
            for h in range(H):
                qc, hh = h // 2, h % 2
                last = eng.matmul(
                    pbank(h % 4, 0, 128, 128 * (h // 4), 128),
                    KNF[hh * 64:(hh + 1) * 64, qc * 128:(qc + 1) * 128],
                    QNF[hh * 64:(hh + 1) * 64, qc * 128:(qc + 1) * 128],
                    start=True, stop=True)
            for dc in range(DC):
                for pi in range(2):
                    last = eng.matmul(
                        pbank(4 + pi, 0, 128, 0, 512),
                        LN1[:, dc * LNW + lt:dc * LNW + lt + 128],
                        WZT[:, dc * D + pi * 512:dc * D + (pi + 1) * 512],
                        start=(dc == 0), stop=(dc == DC - 1))
            return last
        kb.stage(tensor=a_s13t)

        def a_s14v(eng):
            last = None
            with eng.parallel():
             for h in range(H):
                last = eng.tensor_tensor(
                    out=EM[:, h * 128:(h + 1) * 128],
                    in0=pbank(h % 4, 0, 128, 128 * (h // 4), 128),
                    in1=EM[:, h * 128:(h + 1) * 128], op=ALU.mult)
             for pi in range(2):
                last = eng.tensor_copy(out=ZB[:, pi * 512:(pi + 1) * 512],
                                       in_=pbank(4 + pi, 0, 128, 0, 512))
            return last
        kb.stage(vector=a_s14v)

        def a_s15t(eng, pc=pc):
            last = None
            for h in range(H):
                last = eng.matmul(pbank(h % 2, 0, 128, (h // 2) * 64, 64),
                                  EM[:, h * 128:(h + 1) * 128],
                                  UTM[:, h * 64:(h + 1) * 64],
                                  start=True, stop=True)
            for h in range(H):
                qc, hh = h // 2, h % 2
                last = eng.matmul(
                    pbank(2 + h % 2, 0, 128, (h // 2) * 64, 64),
                    QNF[hh * 64:(hh + 1) * 64, qc * 128:(qc + 1) * 128],
                    SLOC[pc].ap()[hh * 64:(hh + 1) * 64,
                                  h * 64:(h + 1) * 64],
                    start=True, stop=True)
            for h in range(H):
                last = eng.matmul(pbank(4 + h % 2, 0, 64, (h // 2) * 64, 64),
                                  KH[:, h * 64:(h + 1) * 64],
                                  UTM[:, h * 64:(h + 1) * 64],
                                  start=True, stop=True)
            for h in range(H):
                last = eng.matmul(pbank(4 + h % 2, 64, 64, (h // 2) * 64, 64),
                                  KH[:, h * 64:(h + 1) * 64],
                                  UTM[:, h * 64:(h + 1) * 64],
                                  start=True, stop=True,
                                  tile_position=(0, 64))
            return last
        kb.stage(tensor=a_s15t)

        def a_s16s(eng, EXPGc=EXPGc):
            last = None
            with eng.parallel():
             for h in range(H):
                last = eng.activation(
                    out=T1F[:, h * 64:(h + 1) * 64],
                    in_=pbank(2 + h % 2, 0, 128, (h // 2) * 64, 64),
                    func=AF.Copy, scale=EXPGc[:, h:h + 1])
            return last
        kb.stage(scalar=a_s16s)

        def a_s17v(eng, pc=pc, EB128c=EB128c):
            last = None
            with eng.parallel():
             for h in range(H):
                last = eng.tensor_tensor(
                    out=OIB[:, h * 64:(h + 1) * 64],
                    in0=T1F[:, h * 64:(h + 1) * 64],
                    in1=pbank(h % 2, 0, 128, (h // 2) * 64, 64), op=ALU.add)
             for h in range(H):
                for b0 in (0, 64):
                    last = eng.scalar_tensor_tensor(
                        out=SLOC[1 - pc].ap()[b0:b0 + 64,
                                              h * 64:(h + 1) * 64],
                        in0=SLOC[pc].ap()[b0:b0 + 64, h * 64:(h + 1) * 64],
                        scalar=EB128c[b0:b0 + 64, h:h + 1],
                        in1=pbank(4 + h % 2, b0, 64, (h // 2) * 64, 64),
                        op0=ALU.mult, op1=ALU.add)
            return last
        t17 = kb.stage(vector=a_s17v)
        prev_tail17[pc] = t17

        def a_s18(eng, c=c):
            return dseq(eng, [
                (sp_oi[c * 128:(c + 1) * 128, :], OIB),
                (sp_qn[c * 128:(c + 1) * 128, :], QNF),
                (sp_z[c * 128:(c + 1) * 128, :], ZB),
            ])
        kb.stage(sync=a_s18)

    marks['attn'] = len(kb.stages)
    def quant_chain(eng, src, F, out_bf, sa, sb_, sc, bm, bm2):
        nb = F // 16
        eng.tensor_reduce(out=bm, in_=src.rearrange("p (b s) -> p b s", s=16),
                          axis=AX.X, op=ALU.max, apply_absolute_value=True)
        eng.tensor_scalar(out=bm2, in0=bm, scalar1=1.0 / 6.0, scalar2=1e-12,
                          op0=ALU.mult, op1=ALU.max)
        eng.reciprocal(out=bm, in_=bm2)
        eng.tensor_tensor(out=sa.rearrange("p (b s) -> p b s", s=16),
                          in0=src.rearrange("p (b s) -> p b s", s=16),
                          in1=bm.unsqueeze(2).broadcast_to([128, nb, 16]),
                          op=ALU.mult)
        with eng.parallel():
            eng.tensor_scalar(out=sb_, in0=sa, scalar1=2.0, scalar2=-2.0,
                              op0=ALU.min, op1=ALU.max)
            eng.tensor_scalar(out=sc, in0=sa, scalar1=4.0, scalar2=-4.0,
                              op0=ALU.min, op1=ALU.max)
        eng.scalar_tensor_tensor(out=sb_, in0=sa, scalar=0.5, in1=sb_,
                                 op0=ALU.mult, op1=ALU.add)
        eng.scalar_tensor_tensor(out=sc, in0=sc, scalar=0.5, in1=sb_,
                                 op0=ALU.mult, op1=ALU.add)
        eng.tensor_scalar(out=sa, in0=sc, scalar1=MAGIC, scalar2=MAGIC,
                          op0=ALU.add, op1=ALU.subtract)
        with eng.parallel():
            eng.tensor_scalar(out=sb_, in0=sa, scalar1=4.0, scalar2=-4.0,
                              op0=ALU.min, op1=ALU.max)
            eng.tensor_scalar(out=sc, in0=sa, scalar1=6.0, scalar2=-6.0,
                              op0=ALU.min, op1=ALU.max)
        eng.scalar_tensor_tensor(out=sa, in0=sa, scalar=2.0, in1=sc,
                                 op0=ALU.mult, op1=ALU.subtract)
        eng.scalar_tensor_tensor(out=sb_, in0=sb_, scalar=-0.5, in1=sa,
                                 op0=ALU.mult, op1=ALU.add)
        return eng.tensor_tensor(
            out=out_bf.rearrange("p (b s) -> p b s", s=16),
            in0=sb_.rearrange("p (b s) -> p b s", s=16),
            in1=bm2.unsqueeze(2).broadcast_to([128, nb, 16]), op=ALU.mult)

    def emit_fq(part):
        FSRC = r4(0, 1408)
        FSA = r4(1408, 1408)
        FSB = r4(2816, 1408)
        FSC = r4(4224, 1408)
        FBM = r4(5632, 88)
        FBM2 = r4(5720, 88)
        FOUT = r5(0, 1408)

        def fq_seg(src_dram, rows0, cols0, F, dst_fn, sub0):
            def f_l(eng):
                return dseq(eng, [(FSRC[:, 0:F],
                                   src_dram[rows0:rows0 + 128, cols0:cols0 + F])])
            kb.stage(sync=f_l)

            def f_q(eng):
                return quant_chain(eng, FSRC[:, 0:F], F, FOUT[:, 0:F],
                                   FSA[:, 0:F], FSB[:, 0:F], FSC[:, 0:F],
                                   FBM[:, 0:F // 16], FBM2[:, 0:F // 16])
            kb.stage(vector=f_q)

            nsub = F // 128
            for s0 in range(0, nsub, 8):
                def f_tr(eng, s0=s0, nsub=nsub):
                    last = None
                    for k in range(s0, min(s0 + 8, nsub)):
                        last = eng.transpose(pbank(k % 8, 0, 128, 0, 64, BF16),
                                             FOUT[:, k * 128:(k + 1) * 128],
                                             IDB.ap())
                    return last
                kb.stage(tensor=f_tr)

                def f_ev(eng, s0=s0, nsub=nsub, sub0=sub0):
                    last = None
                    with eng.parallel():
                     for k in range(s0, min(s0 + 8, nsub)):
                        last = eng.tensor_copy(
                            out=dst_fn(sub0 + k),
                            in_=pbank(k % 8, 0, 128, 0, 64, BF16))
                    return last
                kb.stage(vector=f_ev)

        if part == "main":
            for wsrc, wdst in ((wg_d, wgT_q), (wu_d, wuT_q)):
                for nt in range(IF // 128):
                    def mk(wdst=wdst, nt=nt):
                        return lambda k: wdst[:, k * IF + nt * 128:
                                              k * IF + nt * 128 + 128]
                    fq_seg(wsrc, nt * 128, 0, D, mk(), 0)
            for dt in range(DC):
                def mkd(dt=dt):
                    return lambda k: wdT_chunk(k)[:, dt * 128:dt * 128 + 128]
                fq_seg(wd_d, dt * 128, 0, 1408, mkd(), 0)
        else:
            for dt in range(DC):
                def mkd(dt=dt):
                    return lambda k: wdT_chunk(k)[:, dt * 128:dt * 128 + 128]
                fq_seg(wd_d, dt * 128, 1408, 1408, mkd(), 11)


    SSTG = r4(0, 1024, 0, 65)
    MSKT = r4(1024, 1024, 0, 65)
    RFIN = r4(2048, 1024, 0, 64)
    REP = r4(3072, 1024, 0, 16)
    ACC = r4(4096, 1024, 0, 64)
    WSTG = r4(5120, 1024)
    DTOTS = r4(6144, 16, 0, 8)
    DPT = r4(6160, 16, 0, 8)
    LNDP = r4(6176, 16, 0, 8)
    COEF = r4(6192, 16, 0, 8)
    COEFM = r4(6208, 16, 0, 8)
    COEFT = r4(6224, 8, 0, 16)
    SU8S = r4(6232, 8, 0, 8)

    def x_s1(eng):
        eng.memset(SSTG[64:65, :], 0.0)
        return eng.tensor_copy(out=SSTG[0:64, :], in_=SLOC[0].ap()[0:64, :])
    kb.stage(vector=x_s1)

    def x_s2(eng):
        return eng.activation(out=TMP1.ap()[0:1, 0:16], in_=LGDR[0].ap(),
                              func=AF.Exp)
    kb.stage(scalar=x_s2)

    def x_s2b(eng):
        eng.dma_batch([(SSTG[64:65, 0:16], TMP1.ap()[0:1, 0:16])])
        return "self"
    kb.stage(gpsimd=x_s2b)

    for j in range(8):
        def x_mask(eng, j=j):
            return eng.tensor_scalar(out=MSKT, in0=SSTG,
                                     scalar1=SELV.ap()[0:65, j:j + 1],
                                     scalar2=None, op0=ALU.mult)
        kb.stage(vector=x_mask)

        def x_send(eng, j=j):
            return dseq(eng, [(cc_send[j * 65:(j + 1) * 65, :], MSKT)])
        kb.stage(sync=x_send)

    def x_cc(eng):
        if nocc:
            eng.dma_batch([(cc_recv[j * 65:(j + 1) * 65, :], MSKT)
                           for j in range(8)])
            eng.sem_inc(ccsem, 1)
        else:
            eng.collective_async(ccsem, "AllReduce", ALU.add,
                                 replica_groups=[[0, 1, 2, 3, 4, 5, 6, 7]],
                                 ins=[cc_send.opt()], outs=[cc_recv.opt()])
        return "self"
    kb.stage(gpsimd=x_cc)

    emit_fq("main")

    def x_rl(eng):
        eng.wait_ge(ccsem, 1)
        return dseq(eng, [
            (DTOTS, cc_recv.rearrange("(j r) c -> j r c", r=65)[:, 64, 0:16]),
            (REP, rep16_d),
            (SU8S, su8_d),
        ])
    kb.stage(sync=x_rl)

    def x_accz(eng):
        return eng.memset(ACC, 0.0)
    kb.stage(vector=x_accz)

    def x_dp(eng):
        return eng.scalar_tensor_tensor(out=DPT, in0=DTOTS,
                                        scalar=SELLT.ap(), in1=INVSEL.ap(),
                                        op0=ALU.mult, op1=ALU.add)
    kb.stage(vector=x_dp)

    def x_lnc(eng):
        return eng.tensor_scalar(out=DPT, in0=DPT, scalar1=1e-38,
                                 scalar2=None, op0=ALU.max)
    kb.stage(vector=x_lnc)

    def x_ln(eng):
        return eng.activation(out=LNDP, in_=DPT, func=AF.Ln)
    kb.stage(scalar=x_ln)

    def x_mm(eng):
        return eng.matmul(pbank(0, 0, 8, 0, 16), SU8S, LNDP,
                          start=True, stop=True)
    kb.stage(tensor=x_mm)

    def x_coef(eng):
        return eng.activation(out=COEF, in_=pbank(0, 0, 8, 0, 16),
                              func=AF.Exp)
    kb.stage(scalar=x_coef)

    def x_coefm(eng):
        return eng.tensor_scalar(out=COEFM, in0=COEF, scalar1=SELLT.ap(),
                                 scalar2=None, op0=ALU.mult)
    kb.stage(vector=x_coefm)

    def x_ct(eng):
        return eng.transpose(pbank(1, 0, 16, 0, 8), COEFM, IDF.ap()[0:8, 0:8])
    kb.stage(tensor=x_ct)

    def x_cte(eng):
        return eng.tensor_copy(out=COEFT, in_=pbank(1, 0, 16, 0, 8))
    kb.stage(vector=x_cte)

    for j in range(8):
        def x_rf(eng, j=j):
            return dseq(eng, [(RFIN, cc_recv[j * 65:j * 65 + 64, :])])
        kb.stage(sync=x_rf)

        def x_bc(eng, j=j):
            eng.matmul(pbank(2, 0, 64, 0, 512),
                       COEFT[:, j:j + 1].broadcast_to([16, 64]),
                       REP[:, 0:512], start=True, stop=True)
            return eng.matmul(pbank(3, 0, 64, 0, 512),
                              COEFT[:, j:j + 1].broadcast_to([16, 64]),
                              REP[:, 512:1024], start=True, stop=True)
        kb.stage(tensor=x_bc)

        def x_acc(eng, j=j):
            eng.tensor_tensor(out=MSKT[0:64, 0:512],
                              in0=pbank(2, 0, 64, 0, 512),
                              in1=RFIN[:, 0:512], op=ALU.mult)
            eng.tensor_tensor(out=MSKT[0:64, 512:1024],
                              in0=pbank(3, 0, 64, 0, 512),
                              in1=RFIN[:, 512:1024], op=ALU.mult)
            return eng.tensor_tensor(out=ACC, in0=ACC, in1=MSKT[0:64, :],
                                     op=ALU.add)
        kb.stage(vector=x_acc)

    def x_cast(eng):
        return eng.tensor_copy(out=SINIT.ap()[0:64, :], in_=ACC)
    kb.stage(vector=x_cast)

    if dbg:
        def x_dbg(eng):
            pairs = []
            for jj in range(8):
                pairs.append((dbg_s[jj * 65:(jj + 1) * 65, :],
                              cc_recv[jj * 65:(jj + 1) * 65, :]))
            return dseq(eng, pairs)
        kb.stage(sync=x_dbg)

    def x_cast2(eng):
        eng.dma_batch([(SINIT.ap()[64:128, :], SINIT.ap()[0:64, :])])
        return "self"
    kb.stage(gpsimd=x_cast2)

    for dcz in range(DC):
        def w_l(eng, dcz=dcz):
            return dseq(eng, [(WSTG, woutt_d[dcz * 128:(dcz + 1) * 128, :])])
        def w_c(eng, dcz=dcz):
            return eng.copy(out=WOUTT[:, dcz * D:(dcz + 1) * D], in_=WSTG)
        kb.stage(sync=w_l)
        kb.stage(scalar=w_c)

    marks['exch'] = len(kb.stages)
    XR = r4(0, 1024)
    OFULL = r4(1024, 1024)
    T2 = r4(2048, 1024)
    X2T = r4(3072, 1024)
    LN2T = r4(4096, 1024)
    QSA = r4(5120, 1024)
    QSB = r4(6144, 1024)
    QSC = r4(7168, 1024)
    CBM = R5.ap().bitcast(F32)[:, 4608:4672]
    CBM2 = R5.ap().bitcast(F32)[:, 4672:4736]
    OIR = r5(0, 1024)
    QNR = r5(1024, 1024)
    ZR = r5(2048, 1024)
    SLZ = r5(3072, 1024)
    OGD = r5(4096, 1024)
    OGF = r5(5120, 1024)
    XQB = r5(6144, 1024)

    def mk_c1(c):
        def c_s1(eng, c=c):
            return dseq(eng, [
                (OIR, sp_oi[c * 128:(c + 1) * 128, :]),
                (QNR, sp_qn[c * 128:(c + 1) * 128, :]),
                (ZR, sp_z[c * 128:(c + 1) * 128, :]),
                (XR, x_d[c * 128:(c + 1) * 128, :]),
            ])
        return c_s1

    def c_s2f(eng):
        last = None
        for h in range(H):
            qc, hh = h // 2, h % 2
            last = eng.matmul(
                pbank(h % 2, 0, 128, (h // 2) * 64, 64),
                QNR[hh * 64:(hh + 1) * 64, qc * 128:(qc + 1) * 128],
                SINIT.ap()[hh * 64:(hh + 1) * 64, h * 64:(h + 1) * 64],
                start=True, stop=True)
        return last

    def mk_c3(c):
        def c_s3(eng, c=c):
            last = eng.activation(out=SLZ, in_=ZR, func=AF.Silu)
            for h in range(H):
                last = eng.activation(
                    out=T2[:, h * 64:(h + 1) * 64],
                    in_=pbank(h % 2, 0, 128, (h // 2) * 64, 64),
                    func=AF.Copy,
                    scale=ELGD.ap()[:, c * H + h:c * H + h + 1])
            return last
        return c_s3

    def c_s4f(eng):
        return eng.tensor_tensor(out=OFULL, in0=T2, in1=OIR, op=ALU.add)

    def c_s5f(eng):
        last = None
        for h in range(H):
            last = eng.activation(out=T2[:, 0:64],
                                  in_=OFULL[:, h * 64:(h + 1) * 64],
                                  func=AF.Square,
                                  accum_out=MSQ.ap()[:, h:h + 1])
        return last

    def c_s6f(eng):
        return eng.tensor_scalar(out=RSTD.ap(), in0=MSQ.ap(),
                                 scalar1=1.0 / HD, scalar2=1e-6,
                                 op0=ALU.mult, op1=ALU.add)

    def c_s7f(eng):
        eng.activation(out=MSQ.ap(), in_=RSTD.ap(), func=AF.Ln)
        return eng.activation(out=RSTD.ap(), in_=MSQ.ap(), func=AF.Exp,
                              scale=-0.5)

    def c_s8f(eng):
        last = None
        for h in range(H):
            last = eng.scalar_tensor_tensor(
                out=OGD[:, h * 64:(h + 1) * 64],
                in0=OFULL[:, h * 64:(h + 1) * 64],
                scalar=RSTD.ap()[:, h:h + 1],
                in1=SLZ[:, h * 64:(h + 1) * 64],
                op0=ALU.mult, op1=ALU.mult)
        return last

    def c_s9f(eng):
        last = None
        for dc in range(DC):
            last = eng.transpose(pbank(dc, 0, 128, 0, 64, BF16),
                                 OGD[:, dc * 128:(dc + 1) * 128],
                                 IDB.ap())
        return last

    def c_s10f(eng):
        last = None
        with eng.parallel():
         for dc in range(DC):
            last = eng.tensor_copy(out=OGF[:, dc * 128:(dc + 1) * 128],
                                   in_=pbank(dc, 0, 128, 0, 64, BF16))
        return last

    def c_s11f(eng):
        last = None
        for dc in range(DC):
            for pi in range(2):
                last = eng.matmul(
                    pbank(4 + pi, 0, 128, 0, 512),
                    OGF[:, dc * 128:(dc + 1) * 128],
                    WOUTT[:, dc * D + pi * 512:dc * D + (pi + 1) * 512],
                    start=(dc == 0), stop=(dc == DC - 1))
        return last

    def c_s12f(eng):
        with eng.parallel():
            eng.tensor_tensor(out=X2T[:, 0:512],
                              in0=pbank(4, 0, 128, 0, 512),
                              in1=XR[:, 0:512], op=ALU.add)
            last = eng.tensor_tensor(out=X2T[:, 512:1024],
                                     in0=pbank(5, 0, 128, 0, 512),
                                     in1=XR[:, 512:1024], op=ALU.add)
        return last

    def c_s13f(eng):
        with eng.parallel():
            eng.activation(out=T2[:, 0:512], in_=X2T[:, 0:512],
                           func=AF.Copy, accum_out=STA.ap()[:, 2:3])
            eng.activation(out=T2[:, 512:1024], in_=X2T[:, 512:1024],
                           func=AF.Copy, accum_out=STA.ap()[:, 3:4])
            eng.activation(out=LN2T[:, 0:512], in_=X2T[:, 0:512],
                           func=AF.Square, accum_out=STA.ap()[:, 4:5])
            last = eng.activation(out=LN2T[:, 512:1024],
                                  in_=X2T[:, 512:1024],
                                  func=AF.Square,
                                  accum_out=STA.ap()[:, 5:6])
        return last

    def c_s14f(eng):
        with eng.parallel():
            eng.tensor_tensor(out=STA.ap()[:, 0:1], in0=STA.ap()[:, 2:3],
                              in1=STA.ap()[:, 3:4], op=ALU.add)
            eng.tensor_tensor(out=STA.ap()[:, 1:2], in0=STA.ap()[:, 4:5],
                              in1=STA.ap()[:, 5:6], op=ALU.add)
        eng.tensor_scalar(out=STA.ap()[:, 2:3], in0=STA.ap()[:, 0:1],
                          scalar1=1.0 / D, scalar2=None, op0=ALU.mult)
        eng.tensor_scalar(out=STA.ap()[:, 3:4], in0=STA.ap()[:, 1:2],
                          scalar1=1.0 / D, scalar2=None, op0=ALU.mult)
        eng.tensor_tensor(out=STA.ap()[:, 4:5], in0=STA.ap()[:, 2:3],
                          in1=STA.ap()[:, 2:3], op=ALU.mult)
        return eng.tensor_tensor(out=STA.ap()[:, 5:6],
                                 in0=STA.ap()[:, 3:4],
                                 in1=STA.ap()[:, 4:5], op=ALU.subtract)

    def c_s15f(eng):
        eng.activation(out=STA.ap()[:, 6:7], in_=STA.ap()[:, 5:6],
                       func=AF.Ln, bias=CCOL.ap()[:, 0:1])
        return eng.activation(out=STA.ap()[:, 6:7], in_=STA.ap()[:, 6:7],
                              func=AF.Exp, scale=-0.5)

    def c_s16f(eng):
        return eng.scalar_tensor_tensor(
            out=STA.ap()[:, 7:8], in0=STA.ap()[:, 2:3], scalar=-1.0,
            in1=STA.ap()[:, 6:7], op0=ALU.mult, op1=ALU.mult)

    def c_s17f(eng):
        return eng.tensor_scalar(out=LN2T, in0=X2T,
                                 scalar1=STA.ap()[:, 6:7],
                                 scalar2=STA.ap()[:, 7:8],
                                 op0=ALU.mult, op1=ALU.add)

    def c_s18f(eng):
        return quant_chain(eng, LN2T, D, XQB, QSA, QSB, QSC, CBM, CBM2)

    def mk_c19(d0, d1):
        def c_s19(eng, d0=d0, d1=d1):
            last = None
            for dc in range(d0, d1):
                last = eng.transpose(pbank(2 + (dc - d0) % 6, 0, 128, 0, 64,
                                           BF16),
                                     XQB[:, dc * 128:(dc + 1) * 128],
                                     IDB.ap())
            return last
        return c_s19

    def mk_c20(d0, d1):
        def c_s20(eng, d0=d0, d1=d1):
            last = None
            with eng.parallel():
             for dc in range(d0, d1):
                last = eng.tensor_copy(
                    out=OGF[:, dc * 128:(dc + 1) * 128],
                    in_=pbank(2 + (dc - d0) % 6, 0, 128, 0, 64, BF16))
            return last
        return c_s20

    def mk_c21(c):
        def c_s21(eng, c=c):
            return dseq(eng, [
                (sp_xqf[c * 128:(c + 1) * 128, :], OGF),
                (sp_x2[c * 128:(c + 1) * 128, :], X2T),
            ])
        return c_s21

    def emit_c_tail(cv, c):
        i19a = kb.stage(tensor=mk_c19(0, 6),
                        after_={"vector": cv["s18"]})
        a20 = {"tensor": i19a}
        if cv.get("s21") is not None:
            a20 = {"tensor": i19a, "sync": cv["s21"]}
        i20a = kb.stage(vector=mk_c20(0, 6), after_=a20)
        i19b = kb.stage(tensor=mk_c19(6, 8), after_={"vector": i20a})
        i20b = kb.stage(vector=mk_c20(6, 8), after_={"tensor": i19b})
        i21 = kb.stage(sync=mk_c21(c), after_={"vector": i20b})
        return i19b, i21

    i_s1c = kb.stage(sync=mk_c1(0))
    cv = None
    for c in range(NCH):
        if cv is None:
            i2 = kb.stage(tensor=c_s2f)
            i3 = kb.stage(scalar=mk_c3(c))
            i21 = i19b_prev = None
        else:
            i2 = kb.stage(tensor=c_s2f,
                          after_={"sync": i_s1c, "vector": cv["s10"],
                                  "scalar": cv["s3"]})
            i3 = kb.stage(scalar=mk_c3(c),
                          after_={"tensor": i2, "vector": cv["s8"]})
            i19b_prev, i21 = emit_c_tail(cv, c - 1)
        if cv is None:
            i4 = kb.stage(vector=c_s4f)
            i5 = kb.stage(scalar=c_s5f)
            i6 = kb.stage(vector=c_s6f)
            i7 = kb.stage(scalar=c_s7f)
            i8 = kb.stage(vector=c_s8f)
            i9 = kb.stage(tensor=c_s9f)
            i10 = kb.stage(vector=c_s10f)
            i11 = kb.stage(tensor=c_s11f)
            i12 = kb.stage(vector=c_s12f)
            i13 = kb.stage(scalar=c_s13f)
            i14 = kb.stage(vector=c_s14f)
            i15 = kb.stage(scalar=c_s15f)
            i16 = kb.stage(vector=c_s16f)
            i17 = kb.stage(vector=c_s17f)
            i18 = kb.stage(vector=c_s18f)
        else:
            i4 = kb.stage(vector=c_s4f,
                          after_={"scalar": i3, "sync": i_s1c})
            i5 = kb.stage(scalar=c_s5f, after_={"vector": i4})
            i6 = kb.stage(vector=c_s6f, after_={"scalar": i5})
            i7 = kb.stage(scalar=c_s7f, after_={"vector": i6})
            i8 = kb.stage(vector=c_s8f,
                          after_={"scalar": i7, "tensor": cv["s9"]})
            i9 = kb.stage(tensor=c_s9f,
                          after_={"vector": i8, "scalar": i3})
            i10 = kb.stage(vector=c_s10f,
                           after_={"tensor": i9, "sync": i21})
            i11 = kb.stage(tensor=c_s11f, after_={"vector": i10})
            i12 = kb.stage(vector=c_s12f,
                           after_={"tensor": i11, "sync": i21})
            i13 = kb.stage(scalar=c_s13f, after_={"vector": i12})
            i14 = kb.stage(vector=c_s14f, after_={"scalar": i13})
            i15 = kb.stage(scalar=c_s15f, after_={"vector": i14})
            i16 = kb.stage(vector=c_s16f, after_={"scalar": i15})
            i17 = kb.stage(vector=c_s17f, after_={"scalar": i15})
            i18 = kb.stage(vector=c_s18f,
                           after_={"tensor": i19b_prev})
        if c + 1 < NCH:
            i_s1n = kb.stage(
                sync=mk_c1(c + 1),
                after_=(None if cv is None else
                        {"vector": i12, "tensor": i2, "scalar": i3}))
        else:
            i_s1n = None
        cv = {"s3": i3, "s8": i8, "s9": i9, "s10": i10, "s18": i18,
              "s21": i21}
        i_s1c = i_s1n
    emit_c_tail(cv, NCH - 1)

    marks['c'] = len(kb.stages)
    emit_fq("tail")

    marks['fq'] = len(kb.stages)
    DX2R = r4(0, 1024)
    DHB = r4(1024, 2816)
    DSA = r4(3840, 1408)
    DSB = r4(5248, 1408)
    DSC = r4(6656, 1408)
    DBM = r4(8064, 88)
    DBM2 = R5.ap().bitcast(F32)[:, 4280:4368]
    XQFR = r5(0, 1024)
    HQ = r5(1024, 2816)
    HQFD = r5(3840, 2816)
    OUTT = R5.ap().bitcast(F32)[:, 3328:4352]
    DBM2 = R5.ap().bitcast(F32)[:, 4352:4440]

    DSL = r4(7712, 352)
    tr_groups = [(0, 4), (4, 8), (8, 11), (11, 15), (15, 19), (19, 22)]

    def mk_mm(np_):
        b = (np_ % 2) * 2
        def d_mm(eng, np_=np_, b=b):
            last = None
            for dc in range(DC):
                last = eng.matmul(
                    pbank(b, 0, 128, 0, 352),
                    XQFR[:, dc * 128:(dc + 1) * 128],
                    wgT_q[:, dc * IF + np_ * 352:
                          dc * IF + (np_ + 1) * 352],
                    start=(dc == 0), stop=(dc == DC - 1))
                last = eng.matmul(
                    pbank(b + 1, 0, 128, 0, 352),
                    XQFR[:, dc * 128:(dc + 1) * 128],
                    wuT_q[:, dc * IF + np_ * 352:
                          dc * IF + (np_ + 1) * 352],
                    start=(dc == 0), stop=(dc == DC - 1))
            return last
        return d_mm

    def mk_si(np_):
        b = (np_ % 2) * 2
        def d_si(eng, np_=np_, b=b):
            return eng.activation(out=DSL, in_=pbank(b, 0, 128, 0, 352),
                                  func=AF.Silu)
        return d_si

    def mk_h(np_):
        b = (np_ % 2) * 2
        def d_h(eng, np_=np_, b=b):
            return eng.tensor_tensor(
                out=DHB[:, np_ * 352:(np_ + 1) * 352],
                in0=DSL, in1=pbank(b + 1, 0, 128, 0, 352), op=ALU.mult)
        return d_h

    def mk_tr(k0, k1):
        def d_tr(eng, k0=k0, k1=k1):
            last = None
            for k in range(k0, k1):
                last = eng.transpose(
                    pbank(4 + (k - k0) % 4, 0, 128, 0, 64, BF16),
                    HQ[:, k * 128:(k + 1) * 128], IDB.ap())
            return last
        return d_tr

    def mk_te(k0, k1):
        def d_te(eng, k0=k0, k1=k1):
            last = None
            with eng.parallel():
             for k in range(k0, k1):
                last = eng.tensor_copy(
                    out=HQFD[:, k * 128:(k + 1) * 128],
                    in_=pbank(4 + (k - k0) % 4, 0, 128, 0, 64, BF16))
            return last
        return d_te

    def d_dn(eng):
        last = None
        for k in range(22):
            for pi in range(2):
                last = eng.matmul(pbank(4 + pi, 0, 128, 0, 512),
                                  HQFD[:, k * 128:(k + 1) * 128],
                                  wdT_chunk(k)[:, pi * 512:(pi + 1) * 512],
                                  start=(k == 0), stop=(k == 21))
        return last

    def d_out(eng):
        with eng.parallel():
            eng.tensor_tensor(out=OUTT[:, 0:512],
                              in0=pbank(4, 0, 128, 0, 512),
                              in1=DX2R[:, 0:512], op=ALU.add)
            last = eng.tensor_tensor(out=OUTT[:, 512:1024],
                                     in0=pbank(5, 0, 128, 0, 512),
                                     in1=DX2R[:, 512:1024], op=ALU.add)
        return last

    def emit_tail_tr(pv, c):
        first = pv.get("out") is None
        last_te = last_tr = None
        for gi, (k0, k1) in enumerate(tr_groups):
            if gi == 0:
                a_tr = pv["q1"] if first else {"vector": pv["out"]}
            else:
                a_tr = last_te if first else {"vector": last_te}
            last_tr = kb.stage(tensor=mk_tr(k0, k1), after_=a_tr)
            last_te = kb.stage(vector=mk_te(k0, k1),
                               after_=(last_tr if first
                                       else {"tensor": last_tr}))
        return last_tr, last_te

    def emit_tail_dn(pv, c, last_te):
        first = pv.get("st") is None
        i_dn = kb.stage(tensor=d_dn,
                        after_=(last_te if first
                                else {"vector": last_te}))
        if first:
            a_out = i_dn
        else:
            a_out = {"tensor": i_dn,
                     "sync": max(pv["s1b"], pv["st"])}
        i_out = kb.stage(vector=d_out, after_=a_out)

        def d_st(eng, c=c):
            return dseq(eng, [(y_d[c * 128:(c + 1) * 128, :], OUTT)])
        i_st = kb.stage(sync=d_st,
                        after_=(i_out if first else {"vector": i_out}))
        return i_out, i_st

    pv = None
    for c in range(NCH):
        def d_s1(eng, c=c):
            return dseq(eng, [(XQFR, sp_xqf[c * 128:(c + 1) * 128, :])])
        i_s1 = kb.stage(sync=d_s1,
                        after_=(None if pv is None
                                else {"tensor": pv["mm7"]}))

        i_mm = [None] * 8
        i_si = [None] * 8
        i_h = [None] * 8
        for np_ in range(8):
            if pv is None:
                amm = i_s1 if np_ == 0 else (
                    i_mm[0] if np_ == 1 else i_h[np_ - 2])
            elif np_ == 0:
                amm = {"sync": i_s1, "scalar": pv["si6"],
                       "vector": pv["h6"]}
            elif np_ == 1:
                amm = {"sync": i_s1, "scalar": pv["si7"],
                       "vector": pv["h7"]}
            else:
                amm = {"sync": i_s1, "scalar": i_si[np_ - 2],
                       "vector": i_h[np_ - 2]}
            i_mm[np_] = kb.stage(tensor=mk_mm(np_), after_=amm)
            if pv is None:
                asi = (max(i_h[np_ - 1], i_mm[np_]) if np_ > 0
                       else i_mm[0])
            elif np_ == 0:
                asi = {"tensor": i_mm[0], "vector": pv["q1"]}
            else:
                asi = {"tensor": i_mm[np_], "vector": i_h[np_ - 1]}
            i_si[np_] = kb.stage(scalar=mk_si(np_), after_=asi)
            ah = (i_si[np_] if pv is None else
                  {"scalar": i_si[np_], "tensor": i_mm[np_]})
            i_h[np_] = kb.stage(vector=mk_h(np_), after_=ah)

        if pv is not None:
            last_tr, last_te = emit_tail_tr(pv, c - 1)
        else:
            last_tr = last_te = None

        def d_q0(eng):
            return quant_chain(eng, DHB[:, 0:1408], 1408, HQ[:, 0:1408],
                               DSA, DSB, DSC, DBM, DBM2)
        i_q0 = kb.stage(vector=d_q0,
                        after_=(i_h[3] if pv is None
                                else {"tensor": last_tr}))

        def d_q1(eng):
            return quant_chain(eng, DHB[:, 1408:2816], 1408,
                               HQ[:, 1408:2816], DSA, DSB, DSC, DBM, DBM2)
        i_q1 = kb.stage(vector=d_q1,
                        after_=(i_h[7] if pv is None else {}))

        if pv is not None:
            i_out, i_st = emit_tail_dn(pv, c - 1, last_te)
        else:
            i_out = i_st = None

        def d_s1b(eng, c=c):
            return dseq(eng, [(DX2R, sp_x2[c * 128:(c + 1) * 128, :])])
        i_s1b = kb.stage(sync=d_s1b,
                         after_=(i_h[7] if pv is None
                                 else {"vector": i_out}))

        pv = {"mm7": i_mm[7], "si6": i_si[6], "si7": i_si[7],
              "h6": i_h[6], "h7": i_h[7], "q1": i_q1,
              "out": i_out, "st": i_st, "s1b": i_s1b}

    last_tr, last_te = emit_tail_tr(pv, NCH - 1)
    emit_tail_dn(pv, NCH - 1, last_te)

    marks['all'] = len(kb.stages)
    if max_stage is not None:
        if isinstance(max_stage, str):
            max_stage = marks[max_stage]
        kb.stages = kb.stages[:max_stage]
    nc._phase_marks = marks
    stgs = {n: st.enter_context(nc.semaphore("st_" + n))
            for n in ("sync", "scalar", "vector", "tensor", "gpsimd")}
    with nc.Block() as block:
        block.sync(kb.body(stgs, esems["sync"], "sync"))
        block.scalar(kb.body(stgs, esems["scalar"], "scalar"))
        block.vector(kb.body(stgs, esems["vector"], "vector"))
        block.tensor(kb.body(stgs, esems["tensor"], "tensor"))
        block.gpsimd(kb.body(stgs, esems["gpsimd"], "gpsimd"))

    st.close()
    return nc



def kernel(L0_attn_x, L0_attn_W_qkv, L0_attn_conv_w, L0_attn_W_z, L0_attn_W_b,
           L0_attn_W_a, L0_attn_dt_bias, L0_attn_A_log, L0_attn_W_out,
           L0_ffn_w_gate, L0_ffn_w_up, L0_ffn_w_down):
    x = np.ascontiguousarray(np.asarray(L0_attn_x, np.float32).reshape(T, D))
    wqkv = np.asarray(L0_attn_W_qkv, np.float32)
    convw = np.ascontiguousarray(
        np.asarray(L0_attn_conv_w, np.float32).reshape(CONV, 4))
    wz = np.asarray(L0_attn_W_z, np.float32)
    wb = np.asarray(L0_attn_W_b, np.float32)
    wa = np.asarray(L0_attn_W_a, np.float32)
    dtb = np.asarray(L0_attn_dt_bias, np.float32)
    alog = np.asarray(L0_attn_A_log, np.float32)
    wout = np.asarray(L0_attn_W_out, np.float32)
    wg = np.ascontiguousarray(np.asarray(L0_ffn_w_gate, np.float32))
    wu = np.ascontiguousarray(np.asarray(L0_ffn_w_up, np.float32))
    wd = np.ascontiguousarray(np.asarray(L0_ffn_w_down, np.float32))

    wqkvt = np.ascontiguousarray(wqkv.T)
    wqkvt_tiled = np.ascontiguousarray(
        wqkvt.reshape(DC, 128, CCH, 128).transpose(2, 0, 1, 3)
    ).reshape(CCH * DC * 128, 128)
    wzt = np.ascontiguousarray(wz.T)
    woutt = np.ascontiguousarray(wout.T)
    wbat = np.zeros((D + 1, 32), np.float32)
    wbat[:D, 0:16] = wb.T
    wbat[:D, 16:32] = wa.T
    wbat[D, 16:32] = dtb

    s_ar = np.arange(128)
    tri = (s_ar[:, None] <= s_ar[None, :]).astype(np.float32)
    ones = np.ones((128, 128), np.float32)
    identf = np.eye(128, dtype=np.float32)
    identb = np.eye(128).astype(BF)
    i16 = np.eye(16, dtype=np.float32)
    maskb = np.where(s_ar[:, None] <= s_ar[None, :], 0xFFFF, 0).astype(np.uint16)
    bones = np.zeros((128, 8 * 16), np.float32)
    for qc in range(8):
        for p in range(128):
            bones[p, qc * 16 + 2 * qc + (1 if p >= 64 else 0)] = 1
    bcastp = np.zeros((16, 8 * 128), np.float32)
    for qc in range(8):
        for p in range(128):
            bcastp[2 * qc + (1 if p >= 64 else 0), qc * 128 + p] = 1
    rep16 = np.zeros((16, D), np.float32)
    for h in range(16):
        rep16[h, h * 64:(h + 1) * 64] = 1
    su8 = (np.arange(8)[:, None] > np.arange(8)[None, :]).astype(np.float32)

    shared = dict(
        wqkvt=wqkvt_tiled, wzt=wzt, wbat=wbat, alog=alog.reshape(1, H),
        convw=convw, woutt=woutt, wg=wg, wu=wu, wd=wd,
        tri=tri, ones=ones, identf=identf, identb=identb, i16=i16,
        maskb=maskb, bones=bones.astype(BF), bcastp=bcastp.astype(BF),
        rep16=rep16, su8=su8,
    )

    in_maps = []
    for r in range(NCORE):
        selv = np.zeros((128, 8), np.float32)
        selv[:, r] = 1.0
        sellt = (np.arange(8) < r).astype(np.float32).reshape(8, 1)
        invsel = np.tile(1.0 - sellt, (1, 16)).astype(np.float32)
        xprev = np.zeros((3, D), np.float32)
        if r > 0:
            xprev = x[r * TLOC - 3:r * TLOC]
        m = dict(shared)
        m.update(x=np.ascontiguousarray(x[r * TLOC:(r + 1) * TLOC]),
                 xprev=np.ascontiguousarray(xprev),
                 selv=selv, sellt=sellt, invsel=invsel)
        in_maps.append(m)

    nc = build_program()
    res = run_bass_kernel_spmd(nc, in_maps, core_ids=list(range(NCORE)))
    out = np.concatenate(
        [np.asarray(res.results[r]["y"]) for r in range(NCORE)], axis=0)
    return out.reshape(1, T, D).astype(np.float32)


if __name__ == "__main__":
    import sys
    if "--build" in sys.argv:
        nc = build_program()
        n = sum(1 for _ in nc.all_instructions())
        print("built ok, instructions:", n)


# version 18 (final)
# speedup vs baseline: 1.1321x; 1.0214x over previous
import numpy as np
import ml_dtypes
import concourse.bass as bass
import concourse.mybir as mybir
from concourse.bass_utils import run_bass_kernel_spmd
from contextlib import ExitStack

F32 = mybir.dt.float32
BF16 = mybir.dt.bfloat16
U16 = mybir.dt.uint16
AF = mybir.ActivationFunctionType
ALU = mybir.AluOpType
AX = mybir.AxisListType

NCORE = 8
T = 16384
D = 1024
H = 16
HD = 64
CONV = 3072
IF = 2816
TLOC = T // NCORE
CK = 128
NCH = TLOC // CK
DC = D // 128
CCH = CONV // 128
LNW = TLOC + 3
MAGIC = float(1.5 * 2.0 ** 23)
LN8 = float(np.log(8.0))
BF = ml_dtypes.bfloat16


class SyncEng:

    _PASS = {"wait_ge", "sem_inc", "then_inc"}

    def __init__(self, eng, sem, selfsync=True):
        self.eng = eng
        self.sem = sem
        self.selfsync = selfsync
        self.count = 0
        self.par = False
        self.pending = None

    def _flush_pending(self):
        if self.pending is not None:
            inst, amt = self.pending
            inst.then_inc(self.sem, amt)
            self.count += amt
            self.pending = None
            return True
        return False

    def _pre_op(self):
        if not self.selfsync:
            return
        if self._flush_pending():
            if not self.par:
                self.eng.wait_ge(self.sem, self.count)

    def parallel(self):
        import contextlib

        @contextlib.contextmanager
        def cm():
            self._pre_op()
            self.par = True
            try:
                yield
            finally:
                self.par = False
        return cm()

    def close_stage(self, stg_sem, inc):
        assert self.pending is not None, "stage had no ops?"
        inst, _ = self.pending
        inst.then_inc(stg_sem, inc)
        self.pending = None

    def dma_batch(self, pairs):
        self._flush_pending()
        for dst, srcp in pairs:
            self.eng.dma_start(out=dst, in_=srcp).then_inc(self.sem, 16)
            self.count += 16
        self.eng.wait_ge(self.sem, self.count)

    def collective(self, *a, **kw):
        self._flush_pending()
        self.eng.collective_compute(*a, **kw).then_inc(self.sem, 1)
        self.count += 1
        self.eng.wait_ge(self.sem, self.count)

    def collective_async(self, done_sem, *a, **kw):
        self._flush_pending()
        self.eng.collective_compute(*a, **kw).then_inc(done_sem, 1)

    def wait_ge(self, *a, **kw):
        return self.eng.wait_ge(*a, **kw)

    def sem_inc(self, *a, **kw):
        return self.eng.sem_inc(*a, **kw)

    def __getattr__(self, name):
        f = getattr(self.eng, name)
        if not callable(f) or name.startswith("_"):
            return f

        def wrapped(*a, **kw):
            self._pre_op()
            inst = f(*a, **kw)
            amt = 16 if name == "dma_start" else 1
            self.pending = (inst, amt)
            return inst
        return wrapped


class KB:

    def __init__(self):
        self.stages = []
        self.afters = []

    def stage(self, after_=None, **fns):
        self.stages.append(fns)
        self.afters.append(after_)
        return len(self.stages) - 1

    def body(self, stg_sems, esem, eng_name):
        engs = list(stg_sems.keys())
        ncum = {e: [] for e in engs}
        cnt = {e: 0 for e in engs}
        for s in self.stages:
            for e in engs:
                if e in s:
                    cnt[e] += 1
                ncum[e].append(cnt[e])

        def run(eng):
            w = SyncEng(eng, esem,
                        selfsync=(eng_name not in ("tensor", "sync")))
            waited = {e: 0 for e in engs}

            def waitup(e, idx):
                need = ncum[e][idx]
                if need > waited[e]:
                    eng.wait_ge(stg_sems[e], need)
                    waited[e] = need

            for si, s in enumerate(self.stages):
                if eng_name not in s:
                    continue
                a = self.afters[si]
                if isinstance(a, dict):
                    if w.selfsync and si > 0:
                        waitup(eng_name, si - 1)
                    for e, idx in a.items():
                        if idx is not None and idx >= 0:
                            waitup(e, idx)
                else:
                    if a is None:
                        a = si - 1
                    if a >= 0:
                        for e in engs:
                            waitup(e, a)
                res = s[eng_name](w)
                if res == "self":
                    eng.sem_inc(stg_sems[eng_name], 1)
                else:
                    w.close_stage(stg_sems[eng_name], 1)
        return run


def build_program(max_stage=None, dbg=False, nocc=False):
    nc = bass.Bass("TRN2", target_bir_lowering=False,
                   dynamic_dma_scratch_size=4096)

    def din(name, shape, dtype=F32):
        return nc.dram_tensor(name, shape, dtype, kind="ExternalInput").ap()

    x_d = din("x", [TLOC, D])
    xprev_d = din("xprev", [3, D])
    selv_d = din("selv", [128, 8])
    sellt_d = din("sellt", [8, 1])
    invsel_d = din("invsel", [8, 16])
    wqkvt_d = din("wqkvt", [CCH * DC * 128, 128])
    wzt_d = din("wzt", [D, D])
    wbat_d = din("wbat", [D + 1, 32])
    alog_d = din("alog", [1, H])
    convw_d = din("convw", [CONV, 4])
    woutt_d = din("woutt", [D, D])
    wg_d = din("wg", [IF, D])
    wu_d = din("wu", [IF, D])
    wd_d = din("wd", [D, IF])
    tri_d = din("tri", [128, 128])
    ones_d = din("ones", [128, 128])
    identf_d = din("identf", [128, 128])
    identb_d = din("identb", [128, 128], BF16)
    i16_d = din("i16", [16, 16])
    maskb_d = din("maskb", [128, 128], U16)
    bones_d = din("bones", [128, 8 * 16], BF16)
    bcastp_d = din("bcastp", [16, 8 * 128], BF16)
    rep16_d = din("rep16", [16, D])
    su8_d = din("su8", [8, 8])

    y_d = nc.dram_tensor("y", [TLOC, D], F32, kind="ExternalOutput").ap()

    spk = "ExternalOutput" if dbg else "Internal"
    sp_oi = nc.dram_tensor("sp_oi", [TLOC, D], BF16, kind=spk).ap()
    sp_qn = nc.dram_tensor("sp_qn", [TLOC, D], BF16, kind=spk).ap()
    sp_z = nc.dram_tensor("sp_z", [TLOC, D], BF16, kind=spk).ap()
    sp_x2 = nc.dram_tensor("sp_x2", [TLOC, D], F32, kind=spk).ap()
    sp_xqf = nc.dram_tensor("sp_xqf", [TLOC, D], BF16, kind=spk).ap()
    if dbg:
        dbg_s = nc.dram_tensor("dbg_s", [8 * 65, D], F32,
                               kind="ExternalOutput").ap()
        dbg_a = nc.dram_tensor("dbg_a", [128, 8192], BF16,
                               kind="ExternalOutput").ap()
        dbg_b = nc.dram_tensor("dbg_b", [128, 64], F32,
                               kind="ExternalOutput").ap()
        dbg_c = nc.dram_tensor("dbg_c", [16, 128], F32,
                               kind="ExternalOutput").ap()
        dbg_d = nc.dram_tensor("dbg_d", [16, 256], BF16,
                               kind="ExternalOutput").ap()
        dbg_e = nc.dram_tensor("dbg_e", [128, 160], F32,
                               kind="ExternalOutput").ap()
    cc_send = nc.dram_tensor("cc_send", [8 * 65, D], F32, kind="Internal").ap()
    cc_recv = nc.dram_tensor("cc_recv", [8 * 65, D], F32, kind="Internal").ap()

    kb = KB()
    marks = {}
    st = ExitStack()

    def sb(name, shape, dtype=F32):
        return st.enter_context(nc.sbuf_tensor(name, shape, dtype))

    R1 = sb("R1", [128, CCH * LNW], BF16)
    R2 = sb("R2", [128, DC * LNW], BF16)
    R3 = sb("R3", [128, DC * D], BF16)
    R4 = sb("R4", [128, 8192])
    R5 = sb("R5", [128, 10240], BF16)

    def r4(c0, cn, p0=0, pn=128):
        return R4.ap()[p0:p0 + pn, c0:c0 + cn]

    def r5(c0, cn, p0=0, pn=128):
        return R5.ap()[p0:p0 + pn, c0:c0 + cn]

    MIXED = R1.ap()
    LN1 = R2.ap()
    WZT = R3.ap()
    WOUTT = R3.ap()
    wgT_q = R1.ap()[:, 0:8 * IF]
    wuT_q = R1.ap()[:, 8 * IF:16 * IF]

    def wdT_chunk(k):
        if k < 4:
            return R1.ap()[:, 16 * IF + k * D:16 * IF + (k + 1) * D]
        if k < 20:
            return R2.ap()[:, (k - 4) * D:(k - 3) * D]
        return R3.ap()[:, (k - 20) * D:(k - 19) * D]

    TRI = sb("TRI", [128, 128])
    ONES = sb("ONES", [128, 128])
    IDF = sb("IDF", [128, 128])
    IDB = sb("IDB", [128, 128], BF16)
    I16 = sb("I16", [16, 16])
    MASKB = sb("MASKB", [128, 128], U16)
    BONES = sb("BONES", [128, 8 * 16], BF16)
    BCASTP = sb("BCASTP", [16, 8 * 128], BF16)
    CONVW = sb("CONVW", [128, CCH * 4])
    WBAT = sb("WBAT", [128, 9 * 32], BF16)
    ALOG = sb("ALOG", [1, H])
    SELV = sb("SELV", [128, 8])
    SELLT = sb("SELLT", [8, 1])
    INVSEL = sb("INVSEL", [8, 16])
    ELGD = sb("ELGD", [128, NCH * H])
    EXPL = [sb("EXPL0", [128, H]), sb("EXPL1", [128, H])]
    LGDR = [sb("LGDR0", [1, H]), sb("LGDR1", [1, H])]
    NEGEA = sb("NEGEA", [128, H])
    SLOC = [sb("SLOC0", [128, H * HD], BF16), sb("SLOC1", [128, H * HD], BF16)]
    SINIT = sb("SINIT", [128, H * HD], BF16)
    BETA = sb("BETA", [128, H])
    TMP1 = sb("TMP1", [128, H])
    TMP2 = sb("TMP2", [128, H])
    TMP3 = sb("TMP3", [128, H])
    TMP4 = sb("TMP4", [128, H])
    GSB = sb("GSB", [128, H])
    NEGG = sb("NEGG", [128, H])
    GT = sb("GT", [16, 128])
    GTS = sb("GTS", [16, 128])
    E2A = sb("E2A", [128, H])
    E2 = sb("E2", [128, H])
    EXPG = sb("EXPG", [128, H])
    EB128 = sb("EB128", [128, H])
    B128S = sb("B128S", [128, H])
    LQ = sb("LQ", [16, 128])
    LK = sb("LK", [16, 128])
    M2Q = sb("M2Q", [16, 128], BF16)
    M2K = sb("M2K", [16, 128], BF16)
    MSQ = sb("MSQ", [128, H])
    RSTD = sb("RSTD", [128, H])
    STA = sb("STA", [128, 8])
    CCOL = sb("CCOL", [128, 4])
    ONESB = sb("ONESB", [1, 128], BF16)

    SQ = r5(0, 2048)
    SQF = R5.ap().bitcast(F32)[:, 0:1024]
    KH = r5(2048, 1024)
    UTM = r5(3072, 1024)
    QNF = r5(4096, 1024)
    KNF = r5(5120, 1024)
    EM = r5(6144, 2048)
    OIB = r5(8192, 1024)
    ZB = r5(9216, 1024)
    LNT = r5(0, 1024)
    WQSB = r5(1024, 1024)
    CS0 = r5(2048, 2048)
    CS1 = r5(4096, 2048)
    XT = r4(0, 1024)
    WQS = r4(1024, 1024)

    PB = [st.enter_context(nc.psum_tensor(f"PB{i}", [128, 512])) for i in range(8)]

    def pbank(i, p0, pn, c0, cn, dtype=F32):
        v = PB[i].ap()
        if dtype != F32:
            m = 2 if dtype == BF16 else 1
            return v.bitcast(dtype)[p0:p0 + pn, c0 * m:(c0 + cn) * m]
        return v[p0:p0 + pn, c0:c0 + cn]

    stg = st.enter_context(nc.semaphore("stg"))
    ccsem = st.enter_context(nc.semaphore("ccsem"))
    esems = {n: st.enter_context(nc.semaphore("es_" + n))
             for n in ("sync", "scalar", "vector", "tensor", "gpsimd")}

    def dseq(eng, pairs):
        eng.dma_batch(pairs)
        return "self"

    def p_zero(eng):
        return eng.memset(WQS[:, 256:288], 0.0)
    kb.stage(vector=p_zero)

    def p_loads(eng):
        pairs = [
            (TRI.ap(), tri_d), (ONES.ap(), ones_d), (IDF.ap(), identf_d),
            (IDB.ap(), identb_d), (I16.ap(), i16_d), (MASKB.ap(), maskb_d),
            (BONES.ap(), bones_d), (BCASTP.ap(), bcastp_d),
            (ALOG.ap(), alog_d), (SELV.ap(), selv_d), (SELLT.ap(), sellt_d),
            (INVSEL.ap(), invsel_d),
        ]
        for cc in range(CCH):
            pairs.append((CONVW.ap()[:, cc * 4:(cc + 1) * 4],
                          convw_d[cc * 128:(cc + 1) * 128, :]))
        for k in range(8):
            pairs.append((WQS[:, k * 32:(k + 1) * 32],
                          wbat_d[k * 128:(k + 1) * 128, :]))
        pairs.append((WQS[0:1, 256:288], wbat_d[1024:1025, :]))
        return dseq(eng, pairs)
    kb.stage(sync=p_loads)

    def p_init(eng):
        eng.memset(ONESB.ap(), 1.0)
        eng.memset(CCOL.ap()[:, 0:1], 1e-5)
        eng.memset(CCOL.ap()[:, 1:2], 1e-6)
        eng.memset(CCOL.ap()[:, 2:3], -LN8)
        eng.memset(CCOL.ap()[:, 3:4], 1.0)
        eng.memset(EXPL[0].ap(), 1.0)
        eng.memset(LGDR[0].ap(), 0.0)
        eng.memset(SLOC[0].ap(), 0.0)
        return eng.memset(SLOC[1].ap(), 0.0)
    kb.stage(vector=p_init)

    def p_ea1(eng):
        return eng.activation(out=TMP1.ap()[0:1, :], in_=ALOG.ap(), func=AF.Exp)
    kb.stage(scalar=p_ea1)

    def p_ea2(eng):
        return eng.tensor_scalar(out=TMP2.ap()[0:1, :], in0=TMP1.ap()[0:1, :],
                                 scalar1=-1.0, scalar2=None, op0=ALU.mult)
    kb.stage(vector=p_ea2)

    def p_ea3(eng):
        return eng.matmul(pbank(0, 0, 128, 0, 16), ONES.ap()[0:1, :],
                          TMP2.ap()[0:1, :], start=True, stop=True)
    kb.stage(tensor=p_ea3)

    def p_ea4(eng):
        return eng.tensor_copy(out=NEGEA.ap(), in_=pbank(0, 0, 128, 0, 16))
    kb.stage(vector=p_ea4)

    def p_wba(eng):
        return eng.copy(out=WBAT.ap(), in_=WQS[:, 0:288])
    kb.stage(scalar=p_wba)

    for dcz in range(DC):
        def p_wzl(eng, dcz=dcz):
            return dseq(eng, [(WQS, wzt_d[dcz * 128:(dcz + 1) * 128, :])])
        def p_wzc(eng, dcz=dcz):
            return eng.copy(out=WZT[:, dcz * D:(dcz + 1) * D], in_=WQS)
        kb.stage(sync=p_wzl)
        kb.stage(scalar=p_wzc)

    marks['prologue'] = len(kb.stages)
    def ln_tile(rows, col0):
        def s1(eng):
            with eng.parallel():
                eng.activation(out=r4(2048, 1024, 0, rows), in_=XT[0:rows, :],
                               func=AF.Copy, accum_out=STA.ap()[0:rows, 0:1])
                last = eng.activation(out=r4(3072, 1024, 0, rows),
                                      in_=XT[0:rows, :], func=AF.Square,
                                      accum_out=STA.ap()[0:rows, 1:2])
            return last
        kb.stage(scalar=s1)

        def s2(eng):
            eng.tensor_scalar(out=STA.ap()[0:rows, 2:3],
                              in0=STA.ap()[0:rows, 0:1], scalar1=1.0 / D,
                              scalar2=None, op0=ALU.mult)
            eng.tensor_scalar(out=STA.ap()[0:rows, 3:4],
                              in0=STA.ap()[0:rows, 1:2], scalar1=1.0 / D,
                              scalar2=None, op0=ALU.mult)
            eng.tensor_tensor(out=STA.ap()[0:rows, 4:5],
                              in0=STA.ap()[0:rows, 2:3],
                              in1=STA.ap()[0:rows, 2:3], op=ALU.mult)
            return eng.tensor_tensor(out=STA.ap()[0:rows, 5:6],
                                     in0=STA.ap()[0:rows, 3:4],
                                     in1=STA.ap()[0:rows, 4:5],
                                     op=ALU.subtract)
        kb.stage(vector=s2)

        def s3(eng):
            eng.activation(out=STA.ap()[0:rows, 6:7],
                           in_=STA.ap()[0:rows, 5:6], func=AF.Ln,
                           bias=CCOL.ap()[0:rows, 0:1])
            return eng.activation(out=STA.ap()[0:rows, 6:7],
                                  in_=STA.ap()[0:rows, 6:7], func=AF.Exp,
                                  scale=-0.5)
        kb.stage(scalar=s3)

        def s4(eng):
            return eng.scalar_tensor_tensor(
                out=STA.ap()[0:rows, 7:8], in0=STA.ap()[0:rows, 2:3],
                scalar=-1.0, in1=STA.ap()[0:rows, 6:7],
                op0=ALU.mult, op1=ALU.mult)
        kb.stage(vector=s4)

        def s5(eng):
            return eng.tensor_scalar(out=LNT[0:rows, :], in0=XT[0:rows, :],
                                     scalar1=STA.ap()[0:rows, 6:7],
                                     scalar2=STA.ap()[0:rows, 7:8],
                                     op0=ALU.mult, op1=ALU.add)
        kb.stage(vector=s5)

        def s6(eng):
            last = None
            for dc in range(DC):
                last = eng.transpose(pbank(dc, 0, 128, 0, 64, BF16)[:, 0:rows],
                                     LNT[0:rows, dc * 128:(dc + 1) * 128],
                                     IDB.ap()[0:rows, 0:rows])
            return last
        kb.stage(tensor=s6)

        def s7(eng):
            last = None
            with eng.parallel():
                for dc in range(DC):
                    last = eng.tensor_copy(
                        out=LN1[:, dc * LNW + col0:dc * LNW + col0 + rows],
                        in_=pbank(dc, 0, 128, 0, 64, BF16)[:, 0:rows])
            return last
        kb.stage(vector=s7)

    def l_xp(eng):
        return dseq(eng, [(XT[0:3, :], xprev_d)])
    kb.stage(sync=l_xp)
    ln_tile(3, 0)
    for tt in range(NCH):
        def l_x(eng, tt=tt):
            return dseq(eng, [(XT, x_d[tt * 128:(tt + 1) * 128, :])])
        kb.stage(sync=l_x)
        ln_tile(128, 3 + tt * 128)

    marks['ln'] = len(kb.stages)
    npieces = [(0, 512), (512, 512), (1024, 512), (1536, 512), (2048, 3)]
    qv = None
    for cc in range(CCH):
        def q_l(eng, cc=cc):
            return dseq(eng, [(WQS.rearrange("p (dc c) -> p dc c", c=128),
                               wqkvt_d[cc * DC * 128:(cc + 1) * DC * 128, :]
                               .rearrange("(dc p) c -> p dc c", p=128))])
        i_ql = kb.stage(sync=q_l,
                        after_=(None if qv is None
                                else {"scalar": qv["c"]}))

        def q_c(eng):
            return eng.copy(out=WQSB, in_=WQS)
        i_qc = kb.stage(scalar=q_c,
                        after_=(None if qv is None
                                else {"sync": i_ql, "tensor": qv["mm"]}))

        def q_mm(eng, cc=cc):
            last = None
            for dc in range(DC):
                for pi, (c0, cn) in enumerate(npieces):
                    bank = pi if pi < 4 else 4
                    last = eng.matmul(
                        pbank(bank, 0, 128, 0, cn),
                        WQSB[:, dc * 128:(dc + 1) * 128],
                        LN1[:, dc * LNW + c0:dc * LNW + c0 + cn],
                        start=(dc == 0), stop=(dc == DC - 1))
            return last
        i_qmm = kb.stage(tensor=q_mm,
                         after_=(None if qv is None
                                 else {"scalar": i_qc, "vector": qv["ev"]}))

        def q_ev(eng, cc=cc):
            last = None
            with eng.parallel():
                for pi, (c0, cn) in enumerate(npieces):
                    bank = pi if pi < 4 else 4
                    last = eng.tensor_copy(
                        out=MIXED[:, cc * LNW + c0:cc * LNW + c0 + cn],
                        in_=pbank(bank, 0, 128, 0, cn))
            return last
        i_qev = kb.stage(vector=q_ev,
                         after_=(None if qv is None
                                 else {"tensor": i_qmm}))

        def q_cv2(eng, cc=cc):
            base = cc * LNW
            eng.tensor_scalar(out=CS0, in0=MIXED[:, base:base + 2048],
                              scalar1=CONVW.ap()[:, cc * 4:cc * 4 + 1],
                              scalar2=None, op0=ALU.mult)
            eng.scalar_tensor_tensor(
                out=CS1, in0=MIXED[:, base + 1:base + 2049],
                scalar=CONVW.ap()[:, cc * 4 + 1:cc * 4 + 2],
                in1=CS0, op0=ALU.mult, op1=ALU.add)
            eng.scalar_tensor_tensor(
                out=CS0, in0=MIXED[:, base + 2:base + 2050],
                scalar=CONVW.ap()[:, cc * 4 + 2:cc * 4 + 3],
                in1=CS1, op0=ALU.mult, op1=ALU.add)
            return eng.scalar_tensor_tensor(
                out=CS1, in0=MIXED[:, base + 3:base + 2051],
                scalar=CONVW.ap()[:, cc * 4 + 3:cc * 4 + 4],
                in1=CS0, op0=ALU.mult, op1=ALU.add)
        i_cv2 = kb.stage(vector=q_cv2,
                         after_=(None if qv is None
                                 else {"scalar": qv["cv3"]}))

        def q_cv3(eng, cc=cc):
            return eng.activation(out=MIXED[:, cc * LNW:cc * LNW + 2048],
                                  in_=CS1, func=AF.Silu)
        i_cv3 = kb.stage(scalar=q_cv3,
                         after_=(None if qv is None
                                 else {"vector": i_cv2}))
        qv = {"c": i_qc, "mm": i_qmm, "ev": i_qev, "cv3": i_cv3}

    marks['qkv'] = len(kb.stages)
    BETA2 = [BETA, sb("BETAb", [128, H])]
    GSB2 = [GSB, sb("GSBb", [128, H])]
    NEGG2 = [NEGG, sb("NEGGb", [128, H])]
    GT2 = [GT, sb("GTb", [16, 128])]
    E22 = [E2, sb("E2b", [128, H])]
    EXPG2 = [EXPG, sb("EXPGb", [128, H])]
    EB1282 = [EB128, sb("EB128b", [128, H])]
    M2Q2 = [M2Q, sb("M2Qb", [16, 128], BF16)]
    M2K2 = [M2K, sb("M2Kb", [16, 128], BF16)]
    T1F = r4(0, 1024)

    prev_head_last = None
    prev_tail17 = [None, None]
    for c in range(NCH):
        pc = c % 2
        ct = c * 128
        lt = 3 + c * 128
        BETAc, GSBc, NEGGc = BETA2[pc].ap(), GSB2[pc].ap(), NEGG2[pc].ap()
        GTc, E2c, EXPGc = GT2[pc].ap(), E22[pc].ap(), EXPG2[pc].ap()
        EB128c = EB1282[pc].ap()
        M2Qc, M2Kc = M2Q2[pc].ap(), M2K2[pc].ap()

        def a_s1(eng, lt=lt):
            last = None
            for dc in range(DC):
                last = eng.matmul(
                    pbank(6, 0, 128, 0, 32),
                    LN1[:, dc * LNW + lt:dc * LNW + lt + 128],
                    WBAT.ap()[:, dc * 32:(dc + 1) * 32],
                    start=(dc == 0), stop=False)
            return eng.matmul(pbank(6, 0, 128, 0, 32), ONESB.ap(),
                              WBAT.ap()[0:1, 256:288], start=False, stop=True)
        h_first = kb.stage(tensor=a_s1, after_=prev_head_last)

        def a_s2(eng, BETAc=BETAc):
            eng.activation(out=BETAc, in_=pbank(6, 0, 128, 0, 16),
                           func=AF.Sigmoid)
            eng.activation(out=TMP1.ap(), in_=pbank(6, 0, 128, 16, 16),
                           func=AF.Abs)
            eng.activation(out=TMP2.ap(), in_=TMP1.ap(), func=AF.Exp,
                           scale=-1.0)
            eng.activation(out=TMP1.ap(), in_=TMP2.ap(), func=AF.Ln,
                           bias=CCOL.ap()[:, 3:4])
            return eng.activation(out=TMP3.ap(), in_=pbank(6, 0, 128, 16, 16),
                                  func=AF.Relu)
        kb.stage(scalar=a_s2)

        def a_s3(eng):
            eng.tensor_tensor(out=TMP4.ap(), in0=TMP3.ap(), in1=TMP1.ap(),
                              op=ALU.add)
            return eng.tensor_tensor(out=TMP2.ap(), in0=TMP4.ap(),
                                     in1=NEGEA.ap(), op=ALU.mult)
        kb.stage(vector=a_s3)

        def a_s4(eng):
            eng.transpose(pbank(7, 0, 16, 0, 128), TMP2.ap(), IDF.ap())
            eng.matmul(pbank(6, 0, 128, 32, 16), TRI.ap(), TMP2.ap(),
                       start=True, stop=True)
            return eng.matmul(pbank(6, 0, 128, 48, 16), ONES.ap(), TMP2.ap(),
                              start=True, stop=True)
        kb.stage(tensor=a_s4)

        def a_s5v(eng, pc=pc, GSBc=GSBc, NEGGc=NEGGc):
            eng.tensor_copy(out=GSBc, in_=pbank(6, 0, 128, 32, 16))
            with eng.parallel():
                eng.tensor_copy(out=GTS.ap(), in_=pbank(7, 0, 16, 0, 128))
                eng.tensor_copy(out=B128S.ap(), in_=pbank(6, 0, 128, 48, 16))
                eng.tensor_scalar(out=NEGGc, in0=pbank(6, 0, 128, 32, 16),
                                  scalar1=-1.0, scalar2=None, op0=ALU.mult)
                eng.tensor_tensor(out=E2A.ap(), in0=pbank(6, 0, 128, 48, 16),
                                  in1=GSBc, op=ALU.subtract)
                last = eng.tensor_tensor(out=LGDR[1 - pc].ap(),
                                         in0=LGDR[pc].ap(),
                                         in1=pbank(6, 0, 1, 48, 16),
                                         op=ALU.add)
            return last

        def a_s5s(eng, ct=ct):
            last = None
            with eng.parallel():
                for qc in range(16):
                    last = eng.activation(
                        out=SQ[:, qc * 128:(qc + 1) * 128],
                        in_=MIXED[:, qc * LNW + ct:qc * LNW + ct + 128],
                        func=AF.Square)
            return last
        kb.stage(vector=a_s5v, scalar=a_s5s)

        def a_s6v(eng, GTc=GTc):
            return eng.tensor_tensor_scan(out=GTc, data0=GTS.ap(),
                                          data1=GTS.ap(), initial=0.0,
                                          op0=ALU.add, op1=ALU.bypass)

        def a_s6s(eng, GSBc=GSBc, EXPGc=EXPGc, EB128c=EB128c, E2c=E2c):
            with eng.parallel():
                eng.activation(out=EXPGc, in_=GSBc, func=AF.Exp)
                eng.activation(out=EB128c, in_=B128S.ap(), func=AF.Exp)
                last = eng.activation(out=E2c, in_=E2A.ap(), func=AF.Exp)
            return last

        def a_s6t(eng):
            last = None
            for qc in range(8):
                last = eng.matmul(pbank(7, 0, 16, 128, 128),
                                  BONES.ap()[:, qc * 16:(qc + 1) * 16],
                                  SQ[:, qc * 128:(qc + 1) * 128],
                                  start=(qc == 0), stop=(qc == 7))
            for qc in range(8):
                last = eng.matmul(pbank(7, 0, 16, 256, 128),
                                  BONES.ap()[:, qc * 16:(qc + 1) * 16],
                                  SQ[:, (8 + qc) * 128:(9 + qc) * 128],
                                  start=(qc == 0), stop=(qc == 7))
            return last
        kb.stage(vector=a_s6v, scalar=a_s6s, tensor=a_s6t)

        def a_s7s(eng, M2Qc=M2Qc, M2Kc=M2Kc):
            with eng.parallel():
                eng.activation(out=LQ.ap(), in_=pbank(7, 0, 16, 128, 128),
                               func=AF.Ln, bias=CCOL.ap()[0:16, 1:2])
                eng.activation(out=LK.ap(), in_=pbank(7, 0, 16, 256, 128),
                               func=AF.Ln, bias=CCOL.ap()[0:16, 1:2])
            with eng.parallel():
                eng.activation(out=M2Qc, in_=LQ.ap(), func=AF.Exp,
                               scale=-0.5, bias=CCOL.ap()[0:16, 2:3])
                last = eng.activation(out=M2Kc, in_=LK.ap(), func=AF.Exp,
                                      scale=-0.5)
            return last

        def a_s7v(eng, c=c, pc=pc, EXPGc=EXPGc, EB128c=EB128c):
            with eng.parallel():
                eng.tensor_tensor(out=ELGD.ap()[:, c * H:(c + 1) * H],
                                  in0=EXPGc, in1=EXPL[pc].ap(), op=ALU.mult)
                last = eng.tensor_tensor(out=EXPL[1 - pc].ap(),
                                         in0=EXPL[pc].ap(),
                                         in1=EB128c, op=ALU.mult)
            return last
        h_last = kb.stage(scalar=a_s7s, vector=a_s7v)
        prev_head_last = h_last

        def a_s8t(eng, M2Qc=M2Qc, M2Kc=M2Kc):
            last = None
            for qc in range(8):
                last = eng.matmul(pbank(qc % 4, 0, 128, 128 * (qc // 4), 128),
                                  BCASTP.ap()[:, qc * 128:(qc + 1) * 128],
                                  M2Qc, start=True, stop=True)
            for qc in range(8):
                last = eng.matmul(
                    pbank(qc % 4, 0, 128, 256 + 128 * (qc // 4), 128),
                    BCASTP.ap()[:, qc * 128:(qc + 1) * 128],
                    M2Kc, start=True, stop=True)
            return last
        kb.stage(tensor=a_s8t)

        def a_s9v(eng, ct=ct):
            last = None
            with eng.parallel():
             for qc in range(8):
                last = eng.tensor_tensor(
                    out=QNF[:, qc * 128:(qc + 1) * 128],
                    in0=MIXED[:, qc * LNW + ct:qc * LNW + ct + 128],
                    in1=pbank(qc % 4, 0, 128, 128 * (qc // 4), 128),
                    op=ALU.mult)
             for qc in range(8):
                last = eng.tensor_tensor(
                    out=KNF[:, qc * 128:(qc + 1) * 128],
                    in0=MIXED[:, (8 + qc) * LNW + ct:(8 + qc) * LNW + ct + 128],
                    in1=pbank(qc % 4, 0, 128, 256 + 128 * (qc // 4), 128),
                    op=ALU.mult)
            return last
        kb.stage(vector=a_s9v)

        def a_s10t(eng, ct=ct, GTc=GTc):
            last = None
            for h in range(H):
                last = eng.matmul(
                    pbank(h % 4, 0, 128, 128 * (h // 4), 128),
                    I16.ap()[:, h:h + 1].broadcast_to([16, 128]),
                    GTc, start=True, stop=True)
            for qc in range(8):
                last = eng.transpose(pbank(4, 0, 128, qc * 64, 64, BF16),
                                     KNF[:, qc * 128:(qc + 1) * 128],
                                     IDB.ap())
            for qc in range(8):
                last = eng.transpose(
                    pbank(5, 0, 128, qc * 64, 64, BF16),
                    MIXED[:, (16 + qc) * LNW + ct:(16 + qc) * LNW + ct + 128],
                    IDB.ap())
            return last
        kb.stage(tensor=a_s10t)

        def a_s10v(eng, NEGGc=NEGGc):
            last = None
            with eng.parallel():
             for h in range(H):
                last = eng.tensor_scalar(
                    out=r4(2048, 2048)[:, h * 128:(h + 1) * 128],
                    in0=pbank(h % 4, 0, 128, 128 * (h // 4), 128),
                    scalar1=NEGGc[:, h:h + 1], scalar2=0.0,
                    op0=ALU.add, op1=ALU.min)
            return last
        kb.stage(vector=a_s10v)

        def a_s11s(eng, E2c=E2c, BETAc=BETAc):
            last = None
            with eng.parallel():
             for h in range(H):
                last = eng.activation(
                    out=EM[:, h * 128:(h + 1) * 128],
                    in_=r4(2048, 2048)[:, h * 128:(h + 1) * 128],
                    func=AF.Exp)
             for qc in range(8):
                for hh in range(2):
                    h = 2 * qc + hh
                    last = eng.activation(
                        out=KH[:, h * 64:(h + 1) * 64],
                        in_=pbank(4, 0, 128, qc * 64, 64, BF16)
                        [:, hh * 64:(hh + 1) * 64],
                        func=AF.Copy, scale=E2c[:, h:h + 1])
                    last = eng.activation(
                        out=UTM[:, h * 64:(h + 1) * 64],
                        in_=pbank(5, 0, 128, qc * 64, 64, BF16)
                        [:, hh * 64:(hh + 1) * 64],
                        func=AF.Copy, scale=BETAc[:, h:h + 1])
            return last
        kb.stage(scalar=a_s11s)

        def a_s12v(eng):
            last = None
            with eng.parallel():
             for h in range(H):
                v = EM[:, h * 128:(h + 1) * 128].bitcast(U16)
                last = eng.tensor_tensor(out=v, in0=v, in1=MASKB.ap(),
                                         op=ALU.bitwise_and)
            return last
        kb.stage(vector=a_s12v)

        def a_s13t(eng, lt=lt):
            last = None
            for h in range(H):
                qc, hh = h // 2, h % 2
                last = eng.matmul(
                    pbank(h % 4, 0, 128, 128 * (h // 4), 128),
                    KNF[hh * 64:(hh + 1) * 64, qc * 128:(qc + 1) * 128],
                    QNF[hh * 64:(hh + 1) * 64, qc * 128:(qc + 1) * 128],
                    start=True, stop=True)
            for dc in range(DC):
                for pi in range(2):
                    last = eng.matmul(
                        pbank(4 + pi, 0, 128, 0, 512),
                        LN1[:, dc * LNW + lt:dc * LNW + lt + 128],
                        WZT[:, dc * D + pi * 512:dc * D + (pi + 1) * 512],
                        start=(dc == 0), stop=(dc == DC - 1))
            return last
        kb.stage(tensor=a_s13t)

        def a_s14v(eng):
            last = None
            with eng.parallel():
             for h in range(H):
                last = eng.tensor_tensor(
                    out=EM[:, h * 128:(h + 1) * 128],
                    in0=pbank(h % 4, 0, 128, 128 * (h // 4), 128),
                    in1=EM[:, h * 128:(h + 1) * 128], op=ALU.mult)
             for pi in range(2):
                last = eng.tensor_copy(out=ZB[:, pi * 512:(pi + 1) * 512],
                                       in_=pbank(4 + pi, 0, 128, 0, 512))
            return last
        kb.stage(vector=a_s14v)

        def a_s15t(eng, pc=pc):
            last = None
            for h in range(H):
                last = eng.matmul(pbank(h % 2, 0, 128, (h // 2) * 64, 64),
                                  EM[:, h * 128:(h + 1) * 128],
                                  UTM[:, h * 64:(h + 1) * 64],
                                  start=True, stop=True)
            for h in range(H):
                qc, hh = h // 2, h % 2
                last = eng.matmul(
                    pbank(2 + h % 2, 0, 128, (h // 2) * 64, 64),
                    QNF[hh * 64:(hh + 1) * 64, qc * 128:(qc + 1) * 128],
                    SLOC[pc].ap()[hh * 64:(hh + 1) * 64,
                                  h * 64:(h + 1) * 64],
                    start=True, stop=True)
            for h in range(H):
                last = eng.matmul(pbank(4 + h % 2, 0, 64, (h // 2) * 64, 64),
                                  KH[:, h * 64:(h + 1) * 64],
                                  UTM[:, h * 64:(h + 1) * 64],
                                  start=True, stop=True)
            for h in range(H):
                last = eng.matmul(pbank(4 + h % 2, 64, 64, (h // 2) * 64, 64),
                                  KH[:, h * 64:(h + 1) * 64],
                                  UTM[:, h * 64:(h + 1) * 64],
                                  start=True, stop=True,
                                  tile_position=(0, 64))
            return last
        kb.stage(tensor=a_s15t)

        def a_s16s(eng, EXPGc=EXPGc):
            last = None
            with eng.parallel():
             for h in range(H):
                last = eng.activation(
                    out=T1F[:, h * 64:(h + 1) * 64],
                    in_=pbank(2 + h % 2, 0, 128, (h // 2) * 64, 64),
                    func=AF.Copy, scale=EXPGc[:, h:h + 1])
            return last
        kb.stage(scalar=a_s16s)

        def a_s17v(eng, pc=pc, EB128c=EB128c):
            last = None
            with eng.parallel():
             for h in range(H):
                last = eng.tensor_tensor(
                    out=OIB[:, h * 64:(h + 1) * 64],
                    in0=T1F[:, h * 64:(h + 1) * 64],
                    in1=pbank(h % 2, 0, 128, (h // 2) * 64, 64), op=ALU.add)
             for h in range(H):
                for b0 in (0, 64):
                    last = eng.scalar_tensor_tensor(
                        out=SLOC[1 - pc].ap()[b0:b0 + 64,
                                              h * 64:(h + 1) * 64],
                        in0=SLOC[pc].ap()[b0:b0 + 64, h * 64:(h + 1) * 64],
                        scalar=EB128c[b0:b0 + 64, h:h + 1],
                        in1=pbank(4 + h % 2, b0, 64, (h // 2) * 64, 64),
                        op0=ALU.mult, op1=ALU.add)
            return last
        t17 = kb.stage(vector=a_s17v)
        prev_tail17[pc] = t17

        def a_s18(eng, c=c):
            return dseq(eng, [
                (sp_oi[c * 128:(c + 1) * 128, :], OIB),
                (sp_qn[c * 128:(c + 1) * 128, :], QNF),
                (sp_z[c * 128:(c + 1) * 128, :], ZB),
            ])
        kb.stage(sync=a_s18)

    marks['attn'] = len(kb.stages)
    def quant_chain(eng, src, F, out_bf, sa, sb_, sc, bm, bm2):
        nb = F // 16
        eng.tensor_reduce(out=bm, in_=src.rearrange("p (b s) -> p b s", s=16),
                          axis=AX.X, op=ALU.max, apply_absolute_value=True)
        eng.tensor_scalar(out=bm2, in0=bm, scalar1=1.0 / 6.0, scalar2=1e-12,
                          op0=ALU.mult, op1=ALU.max)
        eng.reciprocal(out=bm, in_=bm2)
        eng.tensor_tensor(out=sa.rearrange("p (b s) -> p b s", s=16),
                          in0=src.rearrange("p (b s) -> p b s", s=16),
                          in1=bm.unsqueeze(2).broadcast_to([128, nb, 16]),
                          op=ALU.mult)
        with eng.parallel():
            eng.tensor_scalar(out=sb_, in0=sa, scalar1=2.0, scalar2=-2.0,
                              op0=ALU.min, op1=ALU.max)
            eng.tensor_scalar(out=sc, in0=sa, scalar1=4.0, scalar2=-4.0,
                              op0=ALU.min, op1=ALU.max)
        eng.scalar_tensor_tensor(out=sb_, in0=sa, scalar=0.5, in1=sb_,
                                 op0=ALU.mult, op1=ALU.add)
        eng.scalar_tensor_tensor(out=sc, in0=sc, scalar=0.5, in1=sb_,
                                 op0=ALU.mult, op1=ALU.add)
        eng.tensor_scalar(out=sa, in0=sc, scalar1=MAGIC, scalar2=MAGIC,
                          op0=ALU.add, op1=ALU.subtract)
        with eng.parallel():
            eng.tensor_scalar(out=sb_, in0=sa, scalar1=4.0, scalar2=-4.0,
                              op0=ALU.min, op1=ALU.max)
            eng.tensor_scalar(out=sc, in0=sa, scalar1=6.0, scalar2=-6.0,
                              op0=ALU.min, op1=ALU.max)
        eng.scalar_tensor_tensor(out=sa, in0=sa, scalar=2.0, in1=sc,
                                 op0=ALU.mult, op1=ALU.subtract)
        eng.scalar_tensor_tensor(out=sb_, in0=sb_, scalar=-0.5, in1=sa,
                                 op0=ALU.mult, op1=ALU.add)
        return eng.tensor_tensor(
            out=out_bf.rearrange("p (b s) -> p b s", s=16),
            in0=sb_.rearrange("p (b s) -> p b s", s=16),
            in1=bm2.unsqueeze(2).broadcast_to([128, nb, 16]), op=ALU.mult)

    def emit_fq(part):
        FSRC = r4(0, 1408)
        FSA = r4(1408, 1408)
        FSB = r4(2816, 1408)
        FSC = r4(4224, 1408)
        FBM = r4(5632, 88)
        FBM2 = r4(5720, 88)
        FOUT = r5(0, 1408)

        def fq_seg(src_dram, rows0, cols0, F, dst_fn, sub0):
            def f_l(eng):
                return dseq(eng, [(FSRC[:, 0:F],
                                   src_dram[rows0:rows0 + 128, cols0:cols0 + F])])
            kb.stage(sync=f_l)

            def f_q(eng):
                return quant_chain(eng, FSRC[:, 0:F], F, FOUT[:, 0:F],
                                   FSA[:, 0:F], FSB[:, 0:F], FSC[:, 0:F],
                                   FBM[:, 0:F // 16], FBM2[:, 0:F // 16])
            kb.stage(vector=f_q)

            nsub = F // 128
            for s0 in range(0, nsub, 8):
                def f_tr(eng, s0=s0, nsub=nsub):
                    last = None
                    for k in range(s0, min(s0 + 8, nsub)):
                        last = eng.transpose(pbank(k % 8, 0, 128, 0, 64, BF16),
                                             FOUT[:, k * 128:(k + 1) * 128],
                                             IDB.ap())
                    return last
                kb.stage(tensor=f_tr)

                def f_ev(eng, s0=s0, nsub=nsub, sub0=sub0):
                    last = None
                    with eng.parallel():
                     for k in range(s0, min(s0 + 8, nsub)):
                        last = eng.tensor_copy(
                            out=dst_fn(sub0 + k),
                            in_=pbank(k % 8, 0, 128, 0, 64, BF16))
                    return last
                kb.stage(vector=f_ev)

        if part == "main":
            for wsrc, wdst in ((wg_d, wgT_q),):
                for nt in range(IF // 128):
                    def mk(wdst=wdst, nt=nt):
                        return lambda k: wdst[:, k * IF + nt * 128:
                                              k * IF + nt * 128 + 128]
                    fq_seg(wsrc, nt * 128, 0, D, mk(), 0)
            for dt in range(DC):
                def mkd(dt=dt):
                    return lambda k: wdT_chunk(k)[:, dt * 128:dt * 128 + 128]
                fq_seg(wd_d, dt * 128, 1024, 384, mkd(), 8)
        else:
            for dt in range(DC):
                def mkd(dt=dt):
                    return lambda k: wdT_chunk(k)[:, dt * 128:dt * 128 + 128]
                fq_seg(wd_d, dt * 128, 1408, 1408, mkd(), 11)


    SSTG = r4(0, 1024, 0, 65)
    MSKT = r4(1024, 1024, 0, 65)
    RFIN = r4(2048, 1024, 0, 64)
    REP = r4(3072, 1024, 0, 16)
    ACC = r4(4096, 1024, 0, 64)
    WSTG = r4(5120, 1024)
    DTOTS = r4(6144, 16, 0, 8)
    DPT = r4(6160, 16, 0, 8)
    LNDP = r4(6176, 16, 0, 8)
    COEF = r4(6192, 16, 0, 8)
    COEFM = r4(6208, 16, 0, 8)
    COEFT = r4(6224, 8, 0, 16)
    SU8S = r4(6232, 8, 0, 8)

    def x_s1(eng):
        eng.memset(SSTG[64:65, :], 0.0)
        return eng.tensor_copy(out=SSTG[0:64, :], in_=SLOC[0].ap()[0:64, :])
    kb.stage(vector=x_s1)

    def x_s2(eng):
        return eng.activation(out=TMP1.ap()[0:1, 0:16], in_=LGDR[0].ap(),
                              func=AF.Exp)
    kb.stage(scalar=x_s2)

    def x_s2b(eng):
        eng.dma_batch([(SSTG[64:65, 0:16], TMP1.ap()[0:1, 0:16])])
        return "self"
    kb.stage(gpsimd=x_s2b)

    for j in range(8):
        def x_mask(eng, j=j):
            return eng.tensor_scalar(out=MSKT, in0=SSTG,
                                     scalar1=SELV.ap()[0:65, j:j + 1],
                                     scalar2=None, op0=ALU.mult)
        kb.stage(vector=x_mask)

        def x_send(eng, j=j):
            return dseq(eng, [(cc_send[j * 65:(j + 1) * 65, :], MSKT)])
        kb.stage(sync=x_send)

    def x_cc(eng):
        if nocc:
            eng.dma_batch([(cc_recv[j * 65:(j + 1) * 65, :], MSKT)
                           for j in range(8)])
            eng.sem_inc(ccsem, 1)
        else:
            eng.collective_async(ccsem, "AllReduce", ALU.add,
                                 replica_groups=[[0, 1, 2, 3, 4, 5, 6, 7]],
                                 ins=[cc_send.opt()], outs=[cc_recv.opt()])
        return "self"
    kb.stage(gpsimd=x_cc)

    emit_fq("main")

    def x_rl(eng):
        eng.wait_ge(ccsem, 1)
        return dseq(eng, [
            (DTOTS, cc_recv.rearrange("(j r) c -> j r c", r=65)[:, 64, 0:16]),
            (REP, rep16_d),
            (SU8S, su8_d),
        ])
    kb.stage(sync=x_rl)

    def x_accz(eng):
        return eng.memset(ACC, 0.0)
    kb.stage(vector=x_accz)

    def x_dp(eng):
        return eng.scalar_tensor_tensor(out=DPT, in0=DTOTS,
                                        scalar=SELLT.ap(), in1=INVSEL.ap(),
                                        op0=ALU.mult, op1=ALU.add)
    kb.stage(vector=x_dp)

    def x_lnc(eng):
        return eng.tensor_scalar(out=DPT, in0=DPT, scalar1=1e-38,
                                 scalar2=None, op0=ALU.max)
    kb.stage(vector=x_lnc)

    def x_ln(eng):
        return eng.activation(out=LNDP, in_=DPT, func=AF.Ln)
    kb.stage(scalar=x_ln)

    def x_mm(eng):
        return eng.matmul(pbank(0, 0, 8, 0, 16), SU8S, LNDP,
                          start=True, stop=True)
    kb.stage(tensor=x_mm)

    def x_coef(eng):
        return eng.activation(out=COEF, in_=pbank(0, 0, 8, 0, 16),
                              func=AF.Exp)
    kb.stage(scalar=x_coef)

    def x_coefm(eng):
        return eng.tensor_scalar(out=COEFM, in0=COEF, scalar1=SELLT.ap(),
                                 scalar2=None, op0=ALU.mult)
    kb.stage(vector=x_coefm)

    def x_ct(eng):
        return eng.transpose(pbank(1, 0, 16, 0, 8), COEFM, IDF.ap()[0:8, 0:8])
    kb.stage(tensor=x_ct)

    def x_cte(eng):
        return eng.tensor_copy(out=COEFT, in_=pbank(1, 0, 16, 0, 8))
    kb.stage(vector=x_cte)

    for j in range(8):
        def x_rf(eng, j=j):
            return dseq(eng, [(RFIN, cc_recv[j * 65:j * 65 + 64, :])])
        kb.stage(sync=x_rf)

        def x_bc(eng, j=j):
            eng.matmul(pbank(2, 0, 64, 0, 512),
                       COEFT[:, j:j + 1].broadcast_to([16, 64]),
                       REP[:, 0:512], start=True, stop=True)
            return eng.matmul(pbank(3, 0, 64, 0, 512),
                              COEFT[:, j:j + 1].broadcast_to([16, 64]),
                              REP[:, 512:1024], start=True, stop=True)
        kb.stage(tensor=x_bc)

        def x_acc(eng, j=j):
            eng.tensor_tensor(out=MSKT[0:64, 0:512],
                              in0=pbank(2, 0, 64, 0, 512),
                              in1=RFIN[:, 0:512], op=ALU.mult)
            eng.tensor_tensor(out=MSKT[0:64, 512:1024],
                              in0=pbank(3, 0, 64, 0, 512),
                              in1=RFIN[:, 512:1024], op=ALU.mult)
            return eng.tensor_tensor(out=ACC, in0=ACC, in1=MSKT[0:64, :],
                                     op=ALU.add)
        kb.stage(vector=x_acc)

    def x_cast(eng):
        return eng.tensor_copy(out=SINIT.ap()[0:64, :], in_=ACC)
    kb.stage(vector=x_cast)

    if dbg:
        def x_dbg(eng):
            pairs = []
            for jj in range(8):
                pairs.append((dbg_s[jj * 65:(jj + 1) * 65, :],
                              cc_recv[jj * 65:(jj + 1) * 65, :]))
            return dseq(eng, pairs)
        kb.stage(sync=x_dbg)

    def x_cast2(eng):
        eng.dma_batch([(SINIT.ap()[64:128, :], SINIT.ap()[0:64, :])])
        return "self"
    kb.stage(gpsimd=x_cast2)

    for dcz in range(DC):
        def w_l(eng, dcz=dcz):
            return dseq(eng, [(WSTG, woutt_d[dcz * 128:(dcz + 1) * 128, :])])
        def w_c(eng, dcz=dcz):
            return eng.copy(out=WOUTT[:, dcz * D:(dcz + 1) * D], in_=WSTG)
        kb.stage(sync=w_l)
        kb.stage(scalar=w_c)

    marks['exch'] = len(kb.stages)
    XR = r4(0, 1024)
    OFULL = r4(1024, 1024)
    T2 = r4(2048, 1024)
    X2T = r4(3072, 1024)
    LN2T = r4(4096, 1024)
    QSA = r4(5120, 1024)
    QSB = r4(6144, 1024)
    QSC = r4(7168, 1024)
    CBM = r4(1920, 64)
    CBM2 = r4(1984, 64)
    OIR = r5(0, 1024)
    QNR = r5(1024, 1024)
    ZR = r5(2048, 1024)
    SLZ = r5(3072, 1024)
    OGD = r5(4096, 1024)
    OGF = r5(5120, 1024)
    XQB = r5(6144, 1024)

    def mk_c1(c):
        def c_s1(eng, c=c):
            return dseq(eng, [
                (OIR, sp_oi[c * 128:(c + 1) * 128, :]),
                (QNR, sp_qn[c * 128:(c + 1) * 128, :]),
                (ZR, sp_z[c * 128:(c + 1) * 128, :]),
                (XR, x_d[c * 128:(c + 1) * 128, :]),
            ])
        return c_s1

    def c_s2f(eng):
        last = None
        for h in range(H):
            qc, hh = h // 2, h % 2
            last = eng.matmul(
                pbank(h % 2, 0, 128, (h // 2) * 64, 64),
                QNR[hh * 64:(hh + 1) * 64, qc * 128:(qc + 1) * 128],
                SINIT.ap()[hh * 64:(hh + 1) * 64, h * 64:(h + 1) * 64],
                start=True, stop=True)
        return last

    def mk_c3(c):
        def c_s3(eng, c=c):
            last = eng.activation(out=SLZ, in_=ZR, func=AF.Silu)
            for h in range(H):
                last = eng.activation(
                    out=T2[:, h * 64:(h + 1) * 64],
                    in_=pbank(h % 2, 0, 128, (h // 2) * 64, 64),
                    func=AF.Copy,
                    scale=ELGD.ap()[:, c * H + h:c * H + h + 1])
            return last
        return c_s3

    def c_s4f(eng):
        return eng.tensor_tensor(out=OFULL, in0=T2, in1=OIR, op=ALU.add)

    def c_s5f(eng):
        last = None
        for h in range(H):
            last = eng.activation(out=T2[:, 0:64],
                                  in_=OFULL[:, h * 64:(h + 1) * 64],
                                  func=AF.Square,
                                  accum_out=MSQ.ap()[:, h:h + 1])
        return last

    def c_s6f(eng):
        return eng.tensor_scalar(out=RSTD.ap(), in0=MSQ.ap(),
                                 scalar1=1.0 / HD, scalar2=1e-6,
                                 op0=ALU.mult, op1=ALU.add)

    def c_s7f(eng):
        eng.activation(out=MSQ.ap(), in_=RSTD.ap(), func=AF.Ln)
        return eng.activation(out=RSTD.ap(), in_=MSQ.ap(), func=AF.Exp,
                              scale=-0.5)

    def c_s8f(eng):
        last = None
        for h in range(H):
            last = eng.scalar_tensor_tensor(
                out=OGD[:, h * 64:(h + 1) * 64],
                in0=OFULL[:, h * 64:(h + 1) * 64],
                scalar=RSTD.ap()[:, h:h + 1],
                in1=SLZ[:, h * 64:(h + 1) * 64],
                op0=ALU.mult, op1=ALU.mult)
        return last

    def c_s9f(eng):
        last = None
        for dc in range(DC):
            last = eng.transpose(pbank(dc, 0, 128, 0, 64, BF16),
                                 OGD[:, dc * 128:(dc + 1) * 128],
                                 IDB.ap())
        return last

    def c_s10f(eng):
        last = None
        with eng.parallel():
         for dc in range(DC):
            last = eng.tensor_copy(out=OGF[:, dc * 128:(dc + 1) * 128],
                                   in_=pbank(dc, 0, 128, 0, 64, BF16))
        return last

    def c_s11f(eng):
        last = None
        for dc in range(DC):
            for pi in range(2):
                last = eng.matmul(
                    pbank(4 + pi, 0, 128, 0, 512),
                    OGF[:, dc * 128:(dc + 1) * 128],
                    WOUTT[:, dc * D + pi * 512:dc * D + (pi + 1) * 512],
                    start=(dc == 0), stop=(dc == DC - 1))
        return last

    def c_s12f(eng):
        with eng.parallel():
            eng.tensor_tensor(out=X2T[:, 0:512],
                              in0=pbank(4, 0, 128, 0, 512),
                              in1=XR[:, 0:512], op=ALU.add)
            last = eng.tensor_tensor(out=X2T[:, 512:1024],
                                     in0=pbank(5, 0, 128, 0, 512),
                                     in1=XR[:, 512:1024], op=ALU.add)
        return last

    def c_s13f(eng):
        with eng.parallel():
            eng.activation(out=T2[:, 0:512], in_=X2T[:, 0:512],
                           func=AF.Copy, accum_out=STA.ap()[:, 2:3])
            eng.activation(out=T2[:, 512:1024], in_=X2T[:, 512:1024],
                           func=AF.Copy, accum_out=STA.ap()[:, 3:4])
            eng.activation(out=LN2T[:, 0:512], in_=X2T[:, 0:512],
                           func=AF.Square, accum_out=STA.ap()[:, 4:5])
            last = eng.activation(out=LN2T[:, 512:1024],
                                  in_=X2T[:, 512:1024],
                                  func=AF.Square,
                                  accum_out=STA.ap()[:, 5:6])
        return last

    def c_s14f(eng):
        with eng.parallel():
            eng.tensor_tensor(out=STA.ap()[:, 0:1], in0=STA.ap()[:, 2:3],
                              in1=STA.ap()[:, 3:4], op=ALU.add)
            eng.tensor_tensor(out=STA.ap()[:, 1:2], in0=STA.ap()[:, 4:5],
                              in1=STA.ap()[:, 5:6], op=ALU.add)
        eng.tensor_scalar(out=STA.ap()[:, 2:3], in0=STA.ap()[:, 0:1],
                          scalar1=1.0 / D, scalar2=None, op0=ALU.mult)
        eng.tensor_scalar(out=STA.ap()[:, 3:4], in0=STA.ap()[:, 1:2],
                          scalar1=1.0 / D, scalar2=None, op0=ALU.mult)
        eng.tensor_tensor(out=STA.ap()[:, 4:5], in0=STA.ap()[:, 2:3],
                          in1=STA.ap()[:, 2:3], op=ALU.mult)
        return eng.tensor_tensor(out=STA.ap()[:, 5:6],
                                 in0=STA.ap()[:, 3:4],
                                 in1=STA.ap()[:, 4:5], op=ALU.subtract)

    def c_s15f(eng):
        eng.activation(out=STA.ap()[:, 6:7], in_=STA.ap()[:, 5:6],
                       func=AF.Ln, bias=CCOL.ap()[:, 0:1])
        return eng.activation(out=STA.ap()[:, 6:7], in_=STA.ap()[:, 6:7],
                              func=AF.Exp, scale=-0.5)

    def c_s16f(eng):
        return eng.scalar_tensor_tensor(
            out=STA.ap()[:, 7:8], in0=STA.ap()[:, 2:3], scalar=-1.0,
            in1=STA.ap()[:, 6:7], op0=ALU.mult, op1=ALU.mult)

    def c_s17f(eng):
        return eng.tensor_scalar(out=LN2T, in0=X2T,
                                 scalar1=STA.ap()[:, 6:7],
                                 scalar2=STA.ap()[:, 7:8],
                                 op0=ALU.mult, op1=ALU.add)

    def c_s18f(eng):
        return quant_chain(eng, LN2T, D, XQB, QSA, QSB, QSC, CBM, CBM2)

    def mk_c19(d0, d1):
        def c_s19(eng, d0=d0, d1=d1):
            last = None
            for dc in range(d0, d1):
                last = eng.transpose(pbank(2 + (dc - d0) % 6, 0, 128, 0, 64,
                                           BF16),
                                     XQB[:, dc * 128:(dc + 1) * 128],
                                     IDB.ap())
            return last
        return c_s19

    def mk_c20(d0, d1):
        def c_s20(eng, d0=d0, d1=d1):
            last = None
            with eng.parallel():
             for dc in range(d0, d1):
                last = eng.tensor_copy(
                    out=OGF[:, dc * 128:(dc + 1) * 128],
                    in_=pbank(2 + (dc - d0) % 6, 0, 128, 0, 64, BF16))
            return last
        return c_s20

    def mk_c21(c):
        def c_s21(eng, c=c):
            return dseq(eng, [
                (sp_xqf[c * 128:(c + 1) * 128, :], OGF),
                (sp_x2[c * 128:(c + 1) * 128, :], X2T),
            ])
        return c_s21

    def emit_c_tail(cv, c):
        i19a = kb.stage(tensor=mk_c19(0, 6),
                        after_={"vector": cv["s18"]})
        a20 = {"tensor": i19a}
        if cv.get("s21") is not None:
            a20 = {"tensor": i19a, "sync": cv["s21"]}
        i20a = kb.stage(vector=mk_c20(0, 6), after_=a20)
        i19b = kb.stage(tensor=mk_c19(6, 8), after_={"vector": i20a})
        i20b = kb.stage(vector=mk_c20(6, 8), after_={"tensor": i19b})
        i21 = kb.stage(sync=mk_c21(c), after_={"vector": i20b})
        return i19b, i21

    CFSRC = R5.ap().bitcast(F32)[:, 3584:4608]
    CFOUT = r5(9216, 1024)
    cfq_segs = []
    for nt in range(IF // 128):
        def mku(nt=nt):
            return lambda k: wuT_q[:, k * IF + nt * 128:
                                   k * IF + nt * 128 + 128]
        cfq_segs.append((wu_d, nt * 128, 0, D, mku(), 0))
    for dt in range(DC):
        def mkd2(dt=dt):
            return lambda k: wdT_chunk(k)[:, dt * 128:dt * 128 + 128]
        cfq_segs.append((wd_d, dt * 128, 0, 1024, mkd2(), 0))
    cfq_segs = cfq_segs[::-1]

    def cfq_l(seg):
        src_d, r0, c0, F = seg[0], seg[1], seg[2], seg[3]
        def f(eng, src_d=src_d, r0=r0, c0=c0, F=F):
            return dseq(eng, [(CFSRC[:, 0:F],
                               src_d[r0:r0 + 128, c0:c0 + F])])
        return f

    def cfq_q(seg):
        F = seg[3]
        def f(eng, F=F):
            return quant_chain(eng, CFSRC[:, 0:F], F, CFOUT[:, 0:F],
                               CFSRC[:, 0:F], QSB[:, 0:F], QSC[:, 0:F],
                               QSA[:, 0:F // 16], QSA[:, 88:88 + F // 16])
        return f

    def cfq_tr(k0, k1):
        def f(eng, k0=k0, k1=k1):
            last = None
            for k in range(k0, k1):
                last = eng.transpose(
                    pbank(2 + (k - k0) % 6, 0, 128, 0, 64, BF16),
                    CFOUT[:, k * 128:(k + 1) * 128], IDB.ap())
            return last
        return f

    def cfq_ev(seg, k0, k1):
        dst_fn, sub0 = seg[4], seg[5]
        def f(eng, k0=k0, k1=k1, dst_fn=dst_fn, sub0=sub0):
            last = None
            with eng.parallel():
             for k in range(k0, k1):
                last = eng.tensor_copy(
                    out=dst_fn(sub0 + k),
                    in_=pbank(2 + (k - k0) % 6, 0, 128, 0, 64, BF16))
            return last
        return f

    def emit_cfq_trev(seg, i_fq):
        i_tr1 = kb.stage(tensor=cfq_tr(0, 6), after_={"vector": i_fq})
        i_ev1 = kb.stage(vector=cfq_ev(seg, 0, 6), after_={"tensor": i_tr1})
        i_tr2 = kb.stage(tensor=cfq_tr(6, 8), after_={"vector": i_ev1})
        i_ev2 = kb.stage(vector=cfq_ev(seg, 6, 8), after_={"tensor": i_tr2})
        return i_tr2, i_ev2

    i_s1c = kb.stage(sync=mk_c1(0))
    cv = None
    pend_b = None
    for c in range(NCH):
        if cv is None:
            i2 = kb.stage(tensor=c_s2f)
            i3 = kb.stage(scalar=mk_c3(c))
            i21 = i19b_prev = None
        else:
            i2 = kb.stage(tensor=c_s2f,
                          after_={"sync": i_s1c, "vector": cv["s10"],
                                  "scalar": cv["s3"]})
            i3 = kb.stage(scalar=mk_c3(c),
                          after_={"tensor": i2, "vector": cv["s8"]})
            segA = cfq_segs.pop() if cfq_segs else None
            segB = cfq_segs.pop() if cfq_segs else None
            if pend_b is not None:
                i_trB2, i_evB2 = emit_cfq_trev(*pend_b)
                cv["s18"] = i_evB2
            else:
                i_trB2 = None
            i19b_prev, i21 = emit_c_tail(cv, c - 1)
            if segA is not None:
                a_fl = {"vector": pend_b[1]} if pend_b else None
                i_flA = kb.stage(sync=cfq_l(segA), after_=a_fl)
        if cv is None:
            i4 = kb.stage(vector=c_s4f)
            i5 = kb.stage(scalar=c_s5f)
            i6 = kb.stage(vector=c_s6f)
            i7 = kb.stage(scalar=c_s7f)
            i8 = kb.stage(vector=c_s8f)
            i9 = kb.stage(tensor=c_s9f)
            i10 = kb.stage(vector=c_s10f)
            i11 = kb.stage(tensor=c_s11f)
            i12 = kb.stage(vector=c_s12f)
            i13 = kb.stage(scalar=c_s13f)
            i14 = kb.stage(vector=c_s14f)
            i15 = kb.stage(scalar=c_s15f)
            i16 = kb.stage(vector=c_s16f)
            i17 = kb.stage(vector=c_s17f)
            i18 = kb.stage(vector=c_s18f)
        else:
            i4 = kb.stage(vector=c_s4f,
                          after_={"scalar": i3, "sync": i_s1c})
            i5 = kb.stage(scalar=c_s5f, after_={"vector": i4})
            i6 = kb.stage(vector=c_s6f, after_={"scalar": i5})
            i7 = kb.stage(scalar=c_s7f, after_={"vector": i6})
            i8 = kb.stage(vector=c_s8f,
                          after_={"scalar": i7, "tensor": cv["s9"]})
            if segA is not None:
                a_fq = {"sync": i_flA}
                if i_trB2 is not None:
                    a_fq["tensor"] = i_trB2
                i_fqA = kb.stage(vector=cfq_q(segA), after_=a_fq)
            i9 = kb.stage(tensor=c_s9f,
                          after_={"vector": i8, "scalar": i3})
            i10 = kb.stage(vector=c_s10f,
                           after_={"tensor": i9, "sync": i21})
            i11 = kb.stage(tensor=c_s11f, after_={"vector": i10})
            i12 = kb.stage(vector=c_s12f,
                           after_={"tensor": i11, "sync": i21})
            if segA is not None:
                i_trA1 = kb.stage(tensor=cfq_tr(0, 6),
                                  after_={"vector": i12})
            i13 = kb.stage(scalar=c_s13f, after_={"vector": i12})
            i14 = kb.stage(vector=c_s14f, after_={"scalar": i13})
            i15 = kb.stage(scalar=c_s15f, after_={"vector": i14})
            i16 = kb.stage(vector=c_s16f, after_={"scalar": i15})
            i17 = kb.stage(vector=c_s17f, after_={"scalar": i15})
            i18 = kb.stage(vector=c_s18f,
                           after_={"tensor": i19b_prev})
        if c + 1 < NCH:
            i_s1n = kb.stage(
                sync=mk_c1(c + 1),
                after_=(None if cv is None else
                        {"vector": i12, "tensor": i2, "scalar": i3}))
        else:
            i_s1n = None
        if cv is not None and segA is not None:
            i_evA1 = kb.stage(vector=cfq_ev(segA, 0, 6),
                              after_={"tensor": i_trA1})
            i_trA2 = kb.stage(tensor=cfq_tr(6, 8),
                              after_={"vector": i_evA1})
            i_evA2 = kb.stage(vector=cfq_ev(segA, 6, 8),
                              after_={"tensor": i_trA2})
            pend_b = None
            if segB is not None:
                i_flB = kb.stage(sync=cfq_l(segB),
                                 after_={"vector": i_fqA})
                i_fqB = kb.stage(vector=cfq_q(segB),
                                 after_={"sync": i_flB,
                                         "tensor": i_trA2})
                pend_b = (segB, i_fqB)
        cv = {"s3": i3, "s8": i8, "s9": i9, "s10": i10, "s18": i18,
              "s21": i21}
        i_s1c = i_s1n
    if pend_b is not None:
        _, i_evB2 = emit_cfq_trev(*pend_b)
        cv["s18"] = i_evB2
    emit_c_tail(cv, NCH - 1)

    marks['c'] = len(kb.stages)
    emit_fq("tail")

    marks['fq'] = len(kb.stages)
    DX2R = r4(0, 1024)
    DHB = r4(1024, 2816)
    DSA = r4(3840, 1408)
    DSB = r4(5248, 1408)
    DSC = r4(6656, 1408)
    DBM = r4(8064, 88)
    DBM2 = R5.ap().bitcast(F32)[:, 4280:4368]
    XQFR = r5(0, 1024)
    HQ = r5(1024, 2816)
    HQFD = r5(3840, 2816)
    OUTT = R5.ap().bitcast(F32)[:, 3328:4352]
    DBM2 = R5.ap().bitcast(F32)[:, 4352:4440]

    DSL = r4(7712, 352)
    tr_groups = [(0, 4), (4, 8), (8, 11), (11, 15), (15, 19), (19, 22)]

    def mk_mm(np_):
        b = (np_ % 2) * 2
        def d_mm(eng, np_=np_, b=b):
            last = None
            for dc in range(DC):
                last = eng.matmul(
                    pbank(b, 0, 128, 0, 352),
                    XQFR[:, dc * 128:(dc + 1) * 128],
                    wgT_q[:, dc * IF + np_ * 352:
                          dc * IF + (np_ + 1) * 352],
                    start=(dc == 0), stop=(dc == DC - 1))
                last = eng.matmul(
                    pbank(b + 1, 0, 128, 0, 352),
                    XQFR[:, dc * 128:(dc + 1) * 128],
                    wuT_q[:, dc * IF + np_ * 352:
                          dc * IF + (np_ + 1) * 352],
                    start=(dc == 0), stop=(dc == DC - 1))
            return last
        return d_mm

    def mk_si(np_):
        b = (np_ % 2) * 2
        def d_si(eng, np_=np_, b=b):
            return eng.activation(out=DSL, in_=pbank(b, 0, 128, 0, 352),
                                  func=AF.Silu)
        return d_si

    def mk_h(np_):
        b = (np_ % 2) * 2
        def d_h(eng, np_=np_, b=b):
            return eng.tensor_tensor(
                out=DHB[:, np_ * 352:(np_ + 1) * 352],
                in0=DSL, in1=pbank(b + 1, 0, 128, 0, 352), op=ALU.mult)
        return d_h

    def mk_tr(k0, k1):
        def d_tr(eng, k0=k0, k1=k1):
            last = None
            for k in range(k0, k1):
                last = eng.transpose(
                    pbank(4 + (k - k0) % 4, 0, 128, 0, 64, BF16),
                    HQ[:, k * 128:(k + 1) * 128], IDB.ap())
            return last
        return d_tr

    def mk_te(k0, k1):
        def d_te(eng, k0=k0, k1=k1):
            last = None
            with eng.parallel():
             for k in range(k0, k1):
                last = eng.tensor_copy(
                    out=HQFD[:, k * 128:(k + 1) * 128],
                    in_=pbank(4 + (k - k0) % 4, 0, 128, 0, 64, BF16))
            return last
        return d_te

    def d_dn(eng):
        last = None
        for k in range(22):
            for pi in range(2):
                last = eng.matmul(pbank(4 + pi, 0, 128, 0, 512),
                                  HQFD[:, k * 128:(k + 1) * 128],
                                  wdT_chunk(k)[:, pi * 512:(pi + 1) * 512],
                                  start=(k == 0), stop=(k == 21))
        return last

    def d_out(eng):
        with eng.parallel():
            eng.tensor_tensor(out=OUTT[:, 0:512],
                              in0=pbank(4, 0, 128, 0, 512),
                              in1=DX2R[:, 0:512], op=ALU.add)
            last = eng.tensor_tensor(out=OUTT[:, 512:1024],
                                     in0=pbank(5, 0, 128, 0, 512),
                                     in1=DX2R[:, 512:1024], op=ALU.add)
        return last

    def emit_tail_tr(pv, c):
        first = pv.get("out") is None
        last_te = last_tr = None
        for gi, (k0, k1) in enumerate(tr_groups):
            if gi == 0:
                a_tr = pv["q1"] if first else {"vector": pv["out"]}
            else:
                a_tr = last_te if first else {"vector": last_te}
            last_tr = kb.stage(tensor=mk_tr(k0, k1), after_=a_tr)
            last_te = kb.stage(vector=mk_te(k0, k1),
                               after_=(last_tr if first
                                       else {"tensor": last_tr}))
        return last_tr, last_te

    def emit_tail_dn(pv, c, last_te):
        first = pv.get("st") is None
        i_dn = kb.stage(tensor=d_dn,
                        after_=(last_te if first
                                else {"vector": last_te}))
        if first:
            a_out = i_dn
        else:
            a_out = {"tensor": i_dn,
                     "sync": max(pv["s1b"], pv["st"])}
        i_out = kb.stage(vector=d_out, after_=a_out)

        def d_st(eng, c=c):
            return dseq(eng, [(y_d[c * 128:(c + 1) * 128, :], OUTT)])
        i_st = kb.stage(sync=d_st,
                        after_=(i_out if first else {"vector": i_out}))
        return i_out, i_st

    pv = None
    for c in range(NCH):
        def d_s1(eng, c=c):
            return dseq(eng, [(XQFR, sp_xqf[c * 128:(c + 1) * 128, :])])
        i_s1 = kb.stage(sync=d_s1,
                        after_=(None if pv is None
                                else {"tensor": pv["mm7"]}))

        i_mm = [None] * 8
        i_si = [None] * 8
        i_h = [None] * 8
        for np_ in range(8):
            if pv is None:
                amm = i_s1 if np_ == 0 else (
                    i_mm[0] if np_ == 1 else i_h[np_ - 2])
            elif np_ == 0:
                amm = {"sync": i_s1, "scalar": pv["si6"],
                       "vector": pv["h6"]}
            elif np_ == 1:
                amm = {"sync": i_s1, "scalar": pv["si7"],
                       "vector": pv["h7"]}
            else:
                amm = {"sync": i_s1, "scalar": i_si[np_ - 2],
                       "vector": i_h[np_ - 2]}
            i_mm[np_] = kb.stage(tensor=mk_mm(np_), after_=amm)
            if pv is None:
                asi = (max(i_h[np_ - 1], i_mm[np_]) if np_ > 0
                       else i_mm[0])
            elif np_ == 0:
                asi = {"tensor": i_mm[0], "vector": pv["q1"]}
            else:
                asi = {"tensor": i_mm[np_], "vector": i_h[np_ - 1]}
            i_si[np_] = kb.stage(scalar=mk_si(np_), after_=asi)
            ah = (i_si[np_] if pv is None else
                  {"scalar": i_si[np_], "tensor": i_mm[np_]})
            i_h[np_] = kb.stage(vector=mk_h(np_), after_=ah)

        if pv is not None:
            last_tr, last_te = emit_tail_tr(pv, c - 1)
        else:
            last_tr = last_te = None

        def d_q0(eng):
            return quant_chain(eng, DHB[:, 0:1408], 1408, HQ[:, 0:1408],
                               DSA, DSB, DSC, DBM, DBM2)
        i_q0 = kb.stage(vector=d_q0,
                        after_=(i_h[3] if pv is None
                                else {"tensor": last_tr}))

        def d_q1(eng):
            return quant_chain(eng, DHB[:, 1408:2816], 1408,
                               HQ[:, 1408:2816], DSA, DSB, DSC, DBM, DBM2)
        i_q1 = kb.stage(vector=d_q1,
                        after_=(i_h[7] if pv is None else {}))

        if pv is not None:
            i_out, i_st = emit_tail_dn(pv, c - 1, last_te)
        else:
            i_out = i_st = None

        def d_s1b(eng, c=c):
            return dseq(eng, [(DX2R, sp_x2[c * 128:(c + 1) * 128, :])])
        i_s1b = kb.stage(sync=d_s1b,
                         after_=(i_h[7] if pv is None
                                 else {"vector": i_out}))

        pv = {"mm7": i_mm[7], "si6": i_si[6], "si7": i_si[7],
              "h6": i_h[6], "h7": i_h[7], "q1": i_q1,
              "out": i_out, "st": i_st, "s1b": i_s1b}

    last_tr, last_te = emit_tail_tr(pv, NCH - 1)
    emit_tail_dn(pv, NCH - 1, last_te)

    marks['all'] = len(kb.stages)
    if max_stage is not None:
        if isinstance(max_stage, str):
            max_stage = marks[max_stage]
        kb.stages = kb.stages[:max_stage]
    nc._phase_marks = marks
    stgs = {n: st.enter_context(nc.semaphore("st_" + n))
            for n in ("sync", "scalar", "vector", "tensor", "gpsimd")}
    with nc.Block() as block:
        block.sync(kb.body(stgs, esems["sync"], "sync"))
        block.scalar(kb.body(stgs, esems["scalar"], "scalar"))
        block.vector(kb.body(stgs, esems["vector"], "vector"))
        block.tensor(kb.body(stgs, esems["tensor"], "tensor"))
        block.gpsimd(kb.body(stgs, esems["gpsimd"], "gpsimd"))

    st.close()
    return nc



def kernel(L0_attn_x, L0_attn_W_qkv, L0_attn_conv_w, L0_attn_W_z, L0_attn_W_b,
           L0_attn_W_a, L0_attn_dt_bias, L0_attn_A_log, L0_attn_W_out,
           L0_ffn_w_gate, L0_ffn_w_up, L0_ffn_w_down):
    x = np.ascontiguousarray(np.asarray(L0_attn_x, np.float32).reshape(T, D))
    wqkv = np.asarray(L0_attn_W_qkv, np.float32)
    convw = np.ascontiguousarray(
        np.asarray(L0_attn_conv_w, np.float32).reshape(CONV, 4))
    wz = np.asarray(L0_attn_W_z, np.float32)
    wb = np.asarray(L0_attn_W_b, np.float32)
    wa = np.asarray(L0_attn_W_a, np.float32)
    dtb = np.asarray(L0_attn_dt_bias, np.float32)
    alog = np.asarray(L0_attn_A_log, np.float32)
    wout = np.asarray(L0_attn_W_out, np.float32)
    wg = np.ascontiguousarray(np.asarray(L0_ffn_w_gate, np.float32))
    wu = np.ascontiguousarray(np.asarray(L0_ffn_w_up, np.float32))
    wd = np.ascontiguousarray(np.asarray(L0_ffn_w_down, np.float32))

    wqkvt = np.ascontiguousarray(wqkv.T)
    wqkvt_tiled = np.ascontiguousarray(
        wqkvt.reshape(DC, 128, CCH, 128).transpose(2, 0, 1, 3)
    ).reshape(CCH * DC * 128, 128)
    wzt = np.ascontiguousarray(wz.T)
    woutt = np.ascontiguousarray(wout.T)
    wbat = np.zeros((D + 1, 32), np.float32)
    wbat[:D, 0:16] = wb.T
    wbat[:D, 16:32] = wa.T
    wbat[D, 16:32] = dtb

    s_ar = np.arange(128)
    tri = (s_ar[:, None] <= s_ar[None, :]).astype(np.float32)
    ones = np.ones((128, 128), np.float32)
    identf = np.eye(128, dtype=np.float32)
    identb = np.eye(128).astype(BF)
    i16 = np.eye(16, dtype=np.float32)
    maskb = np.where(s_ar[:, None] <= s_ar[None, :], 0xFFFF, 0).astype(np.uint16)
    bones = np.zeros((128, 8 * 16), np.float32)
    for qc in range(8):
        for p in range(128):
            bones[p, qc * 16 + 2 * qc + (1 if p >= 64 else 0)] = 1
    bcastp = np.zeros((16, 8 * 128), np.float32)
    for qc in range(8):
        for p in range(128):
            bcastp[2 * qc + (1 if p >= 64 else 0), qc * 128 + p] = 1
    rep16 = np.zeros((16, D), np.float32)
    for h in range(16):
        rep16[h, h * 64:(h + 1) * 64] = 1
    su8 = (np.arange(8)[:, None] > np.arange(8)[None, :]).astype(np.float32)

    shared = dict(
        wqkvt=wqkvt_tiled, wzt=wzt, wbat=wbat, alog=alog.reshape(1, H),
        convw=convw, woutt=woutt, wg=wg, wu=wu, wd=wd,
        tri=tri, ones=ones, identf=identf, identb=identb, i16=i16,
        maskb=maskb, bones=bones.astype(BF), bcastp=bcastp.astype(BF),
        rep16=rep16, su8=su8,
    )

    in_maps = []
    for r in range(NCORE):
        selv = np.zeros((128, 8), np.float32)
        selv[:, r] = 1.0
        sellt = (np.arange(8) < r).astype(np.float32).reshape(8, 1)
        invsel = np.tile(1.0 - sellt, (1, 16)).astype(np.float32)
        xprev = np.zeros((3, D), np.float32)
        if r > 0:
            xprev = x[r * TLOC - 3:r * TLOC]
        m = dict(shared)
        m.update(x=np.ascontiguousarray(x[r * TLOC:(r + 1) * TLOC]),
                 xprev=np.ascontiguousarray(xprev),
                 selv=selv, sellt=sellt, invsel=invsel)
        in_maps.append(m)

    nc = build_program()
    res = run_bass_kernel_spmd(nc, in_maps, core_ids=list(range(NCORE)))
    out = np.concatenate(
        [np.asarray(res.results[r]["y"]) for r in range(NCORE)], axis=0)
    return out.reshape(1, T, D).astype(np.float32)


if __name__ == "__main__":
    import sys
    if "--build" in sys.argv:
        nc = build_program()
        n = sum(1 for _ in nc.all_instructions())
        print("built ok, instructions:", n)
